# version 11
# baseline (speedup 1.0000x reference)
import contextlib
import numpy as np
import ml_dtypes
import concourse.bass as bass
import concourse.mybir as mybir
from concourse.bass_utils import run_bass_kernel_spmd

F32 = mybir.dt.float32
BF16 = mybir.dt.bfloat16
I32 = mybir.dt.int32
AF = mybir.ActivationFunctionType
ALU = mybir.AluOpType
BF = ml_dtypes.bfloat16

B, S, E = 2, 8192, 1024
H, KVH, D = 16, 4, 64
QL, KVL = 64, 32
HID = 3072
NCORE = 8
QC = 256
SLOTS = 8
TOK = QC * SLOTS
ECH = E // 128
HCH = HID // 128
NSL = S // 128
EPS = 1e-5
DQ3 = 1.0 / (2.0 * 127.0 ** 3)

def kcj(j):
    return 8 * (j + 1)

UNITS = []
_t = 0
for _j in range(SLOTS):
    for _g in range(2):
        for _hh in range(2):
            _q3 = _j * 4 + _g * 2 + _hh
            for _kc in range(kcj(_j)):
                to = _t if _kc >= 8 * _j else -1
                if to >= 0:
                    _t += 1
                UNITS.append((_j, _g, _hh, _kc, to, _q3, _kc == 0, _kc == kcj(_j) - 1))
NU = len(UNITS)
U_BEFORE = [0] * (SLOTS + 1)
for _j in range(SLOTS):
    U_BEFORE[_j + 1] = U_BEFORE[_j] + 4 * kcj(_j)
TOP_END = [0] * (SLOTS + 1)
_c = 0
for ui in range(NU):
    if UNITS[ui][4] >= 0:
        _c += 1
    TOP_END[UNITS[ui][0] + 1] = _c


def build_nc():
    nc = bass.Bass("TRN2", target_bir_lowering=False, debug=False)

    def din(name, shape, dt):
        return nc.dram_tensor(name, shape, dt, kind="ExternalInput").ap()

    x_own = din("x_own", [TOK, E], F32)
    x_full = din("x_full", [S, E], F32)
    ident = din("ident", [128, 128], BF16)
    masks = din("masks", [128, SLOTS * 8 * QC], BF16)
    ctab_kv = din("ctab_kv", [128, S], F32)
    stab_kv = din("stab_kv", [128, S], F32)
    ctab_q = din("ctab_q", [128, TOK], BF16)
    stab_q = din("stab_q", [128, TOK], BF16)
    wqakva = din("wqakva", [E, 96], F32)
    wqb_i = din("wqb", [QL, E], F32)
    wqbs_i = din("wqbs", [QL, E], F32)
    wk_i = din("wk", [KVL, 256], F32)
    wks_i = din("wks", [KVL, 256], F32)
    wv_i = din("wv", [KVL, 256], F32)
    wout_i = din("wout", [E, E], F32)
    bout_i = din("bout", [1, E], F32)
    wg_i = din("wg", [HID, E], I32)
    wu_i = din("wu", [HID, E], I32)
    wd_i = din("wd", [E, HID], I32)

    y = nc.dram_tensor("y", [TOK, E], F32, kind="ExternalOutput").ap()
    xn_spill = nc.dram_tensor("xn_spill", [TOK, E], F32, kind="Internal").ap()
    srow_d = nc.dram_tensor("srow_d", [32, 1024], BF16, kind="Internal").ap()

    x_full_r = x_full.rearrange("(n p) e -> n p e", p=128)
    xo_r = x_own.rearrange("(i p) e -> i p e", p=128)
    wg_r = wg_i.rearrange("(c p) e -> c p e", p=128)
    wu_r = wu_i.rearrange("(c p) e -> c p e", p=128)
    spill_r = xn_spill.rearrange("(i p) e -> i p e", p=128)
    y_r = y.rearrange("(i p) e -> i p e", p=128)
    wout_r = wout_i.rearrange("(c p) e -> p c e", p=64)

    with contextlib.ExitStack() as R:
        EE = R.enter_context
        identB = EE(nc.sbuf_tensor("identB", [128, 128], BF16))
        rtmp1 = EE(nc.sbuf_tensor("rtmp1", [128, 512], F32))
        rtmp2 = EE(nc.sbuf_tensor("rtmp2", [128, 512], F32))
        stat = EE(nc.sbuf_tensor("stat", [128, 12], F32))
        mv = EE(nc.sbuf_tensor("mv", [128, 2], F32))
        veps = EE(nc.sbuf_tensor("veps", [128, 1], F32))
        istd = EE(nc.sbuf_tensor("istd", [128, 1], F32))
        nbias = EE(nc.sbuf_tensor("nbias", [128, 1], F32))
        xnT = EE(nc.sbuf_tensor("xnT", [128, ECH, TOK], BF16, side="right"))
        S12 = contextlib.ExitStack()
        E12 = S12.enter_context
        KTp = [E12(nc.sbuf_tensor(f"KTp{g}", [128, S], BF16)) for g in range(2)]
        Vt = E12(nc.sbuf_tensor("Vt", [128, S // 128, KVH, 65], BF16))
        woutB = E12(nc.sbuf_tensor("woutB", [64, 16, E], BF16))
        qlatT = E12(nc.sbuf_tensor("qlatT", [QL, SLOTS, QC], BF16))
        wqbB = E12(nc.sbuf_tensor("wqbB", [QL, E], BF16))
        wqbsB = E12(nc.sbuf_tensor("wqbsB", [QL, E], BF16))
        wkB = E12(nc.sbuf_tensor("wkB", [KVL, 256], BF16))
        wksB = E12(nc.sbuf_tensor("wksB", [KVL, 256], BF16))
        wvB = E12(nc.sbuf_tensor("wvB", [KVL, 256], BF16))
        wqakvaB = E12(nc.sbuf_tensor("wqakvaB", [128, ECH, 96], BF16))
        boutB = E12(nc.sbuf_tensor("boutB", [1, E], BF16))
        onesB = E12(nc.sbuf_tensor("onesB", [1, 128], BF16))
        S1 = contextlib.ExitStack()
        E1 = S1.enter_context
        xf = E1(nc.sbuf_tensor("xf", [128, 2, E], F32))
        xb = E1(nc.sbuf_tensor("xb", [128, 2, E], BF16))
        xTw = E1(nc.sbuf_tensor("xTw", [128, 2, ECH, 128], BF16))
        ct_w = E1(nc.sbuf_tensor("ct_w", [128, 2, 128], F32))
        st_w = E1(nc.sbuf_tensor("st_w", [128, 2, 128], F32))
        xoT = E1(nc.sbuf_tensor("xoT", [128, ECH, QC], BF16))
        wstage = E1(nc.sbuf_tensor("wstage", [64, 2, E], F32))
        bstage = E1(nc.sbuf_tensor("bstage", [1, E], F32))
        sstage = E1(nc.sbuf_tensor("sstage", [128, ECH, 96], F32))
        s2stage = E1(nc.sbuf_tensor("s2stage", [QL, E], F32))
        s3stage = E1(nc.sbuf_tensor("s3stage", [KVL, 3, 256], F32))
        kvlatT = E1(nc.sbuf_tensor("kvlatT", [KVL, S], BF16))
        S1.close()
        S2 = contextlib.ExitStack()
        E2 = S2.enter_context
        qTw = E2(nc.sbuf_tensor("qTw", [128, 8, QC], BF16))
        mask_w = E2(nc.sbuf_tensor("mask_w", [128, 8 * QC], BF16))
        Pq = E2(nc.sbuf_tensor("Pq", [128, 3, 1024], BF16))
        osb = E2(nc.sbuf_tensor("osb", [128, 2, 1024], F32, side="right"))
        OT = E2(nc.sbuf_tensor("OT", [64, 16, QC], BF16))
        obc = E2(nc.sbuf_tensor("obc", [64, 2, 1024], BF16))
        srow_t = E2(nc.sbuf_tensor("srow_t", [128, 1024], BF16))
        qc_w = E2(nc.sbuf_tensor("qc_w", [128, 2, QC], BF16))
        qs_w = E2(nc.sbuf_tensor("qs_w", [128, 2, QC], BF16))
        xres = E2(nc.sbuf_tensor("xres", [128, 2, E], F32))
        xn_f = E2(nc.sbuf_tensor("xn_f", [128, 2, E], F32))
        xn_b = E2(nc.sbuf_tensor("xn_b", [128, 2, E], BF16))
        S2.close()
        S12.close()
        hactT = EE(nc.sbuf_tensor("hactT", [128, HCH, 1024], BF16))
        wdT = EE(nc.sbuf_tensor("wdT", [128, HCH, E], BF16))
        wgi_s = EE(nc.sbuf_tensor("wgi_s", [128, 2, E], I32))
        wui_s = EE(nc.sbuf_tensor("wui_s", [128, 2, E], I32))
        wdi_s = EE(nc.sbuf_tensor("wdi_s", [128, 2, ECH, 128], I32))
        wgb_s = EE(nc.sbuf_tensor("wgb_s", [128, 2, E], BF16))
        wub_s = EE(nc.sbuf_tensor("wub_s", [128, 2, E], BF16))
        wdb_s = EE(nc.sbuf_tensor("wdb_s", [128, 2, ECH, 128], BF16))
        wgT = EE(nc.sbuf_tensor("wgT", [128, 2, ECH, 128], BF16))
        wuT = EE(nc.sbuf_tensor("wuT", [128, 2, ECH, 128], BF16))
        tb = EE(nc.sbuf_tensor("tb", [128, 2, 512], F32))
        ub = EE(nc.sbuf_tensor("ub", [128, 2, 512], F32))
        rb = EE(nc.sbuf_tensor("rb", [128, 2, 512], F32))
        xnr = EE(nc.sbuf_tensor("xnr", [128, 2, E], F32))
        fsb = EE(nc.sbuf_tensor("fsb", [128, 2, E], F32))

        pA0 = EE(nc.psum_tensor("pA0", [128, 1024], F32))
        pA1 = EE(nc.psum_tensor("pA1", [128, 1024], F32))
        pA2 = EE(nc.psum_tensor("pA2", [128, 1024], F32))
        pT0 = EE(nc.psum_tensor("pT0", [128, 512], F32))
        pT1 = EE(nc.psum_tensor("pT1", [128, 512], F32))
        pA = [pA0, pA1, pA2]
        sq = pA
        pTR = [pT0, pT1]

        def obank(q3, b):
            return (pT0 if b == 0 else pT1)[:, :]

        def sname(n):
            return EE(nc.semaphore(n))
        ld_c = sname("ld_c"); ld_x = sname("ld_x"); ld_x2 = sname("ld_x2")
        lc_i = sname("lc_i"); lc_qa = sname("lc_qa"); lc_k1 = sname("lc_k1")
        lc_k2 = sname("lc_k2"); lc_k3 = sname("lc_k3"); lc_bo = sname("lc_bo")
        lc_qb = sname("lc_qb")
        lt = [sname("lt0"), sname("lt1")]
        lqt = [sname("lqt0"), sname("lqt1")]
        lxr = [sname("lxr0"), sname("lxr1")]
        lw = [sname("lw0"), sname("lw1")]
        lwd = [sname("lwd0"), sname("lwd1")]
        lxn = [sname("lxn0"), sname("lxn1")]
        ssp = [sname("ssp0"), sname("ssp1")]
        sy = [sname("sy0"), sname("sy1")]
        lxo = [sname("lxo0"), sname("lxo1")]
        ld_t = sname("ld_t")
        ld_xo = sname("ld_xo"); ld_xres = sname("ld_xres"); ld_msk = sname("ld_msk")
        ld_qt = sname("ld_qt"); ld_w = sname("ld_w"); ld_wd = sname("ld_wd")
        ld_xn2 = sname("ld_xn2")
        s_cast = sname("s_cast"); s_ms = sname("s_ms")
        c_xb = sname("c_xb"); s_trec = sname("s_trec"); s_xte = sname("s_xte")
        s_lat = sname("s_lat"); s_kvl = sname("s_kvl"); s_kmm = sname("s_kmm")
        s_rope = sname("s_rope"); s_vmm = sname("s_vmm"); s_vev = sname("s_vev")
        c_xob = sname("c_xob"); s_troc = sname("s_troc"); s_xoev = sname("s_xoev")
        s_qmm = sname("s_qmm"); s_qle = sname("s_qle")
        s_qbmm = sname("s_qbmm"); s_qrope = sname("s_qrope")
        s_squad = sname("s_squad"); s_exp = sname("s_exp"); s_msk = sname("s_msk")
        s_o = sname("s_o"); s_oev = sname("s_oev"); s_sum = sname("s_sum"); s_row = sname("s_row")
        s_obc = sname("s_obc"); s_norm = sname("s_norm")
        s_epimm = sname("s_epimm"); s_xsum = sname("s_xsum"); s_stat = sname("s_stat")
        s_istd = sname("s_istd"); s_nb = sname("s_nb"); s_app = sname("s_app")
        s_xnb = sname("s_xnb"); s_xnt = sname("s_xnt"); s_xnte = sname("s_xnte")
        s_spill = sname("s_spill")
        s_wcast = sname("s_wcast"); s_wtr = sname("s_wtr"); s_wev = sname("s_wev")
        s_gu = sname("s_gu"); s_se = sname("s_se"); s_sil = sname("s_sil")
        s_dwc = sname("s_dwc"); s_dwt = sname("s_dwt"); s_dwev = sname("s_dwev")
        s_dmm = sname("s_dmm"); s_fsum = sname("s_fsum"); s_stat2 = sname("s_stat2")
        s_istd2 = sname("s_istd2"); s_nb2 = sname("s_nb2"); s_app2 = sname("s_app2")
        s_y = sname("s_y")
        sd = sname("sd"); sa = sname("sa")

        NCAST = 15

        block = EE(nc.Block())

        @block.sync
        def _(sync):
            sync.dma_start(out=identB[:], in_=ident).then_inc(lc_i, 16)
            sync.dma_start(out=sstage[:], in_=wqakva.rearrange("(c p) m -> p c m", p=128)).then_inc(lc_qa, 16)
            sync.dma_start(out=s3stage[:, 0, :], in_=wk_i).then_inc(lc_k1, 16)
            sync.dma_start(out=s3stage[:, 1, :], in_=wks_i).then_inc(lc_k2, 16)
            sync.dma_start(out=s3stage[:, 2, :], in_=wv_i).then_inc(lc_k3, 16)
            sync.dma_start(out=bstage[:], in_=bout_i).then_inc(lc_bo, 16)
            sync.dma_start(out=s2stage[:], in_=wqb_i).then_inc(lc_qb, 16)
            sync.wait_ge(s_cast, 6)
            sync.dma_start(out=s2stage[:], in_=wqbs_i).then_inc(lc_qb, 16)
            for r in range(8):
                if r > 0:
                    sync.wait_ge(s_cast, 7 + r)
                sync.dma_start(out=wstage[:], in_=wout_r[:, 2 * r:2 * r + 2, :]).then_inc(ld_c, 16)
            for s in range(NSL):
                if s % 2 == 0:
                    if s >= 2:
                        sync.wait_ge(c_xb, s - 1)
                    sync.dma_start(out=xf[:, s % 2, :], in_=x_full_r[s]).then_inc(ld_x, 16)
            for i in range(16):
                if i < 2:
                    sync.wait_ge(c_xb, NSL)
                else:
                    sync.wait_ge(c_xob, i - 1)
                sync.dma_start(out=xf[:, i % 2, :], in_=xo_r[i]).then_inc(lxo[i % 2], 16)
            sync.wait_ge(c_xb, NSL)
            sync.wait_ge(c_xob, 16)
            sync.wait_ge(s_trec, 8 * NSL)
            sync.wait_ge(s_troc, 128)
            sync.wait_ge(s_xte, 8 * NSL)
            sync.wait_ge(s_xoev, 128)
            sync.wait_ge(s_lat, NSL)
            sync.wait_ge(s_kvl, NSL)
            sync.wait_ge(s_rope, 2 * NSL)
            sync.wait_ge(s_vev, NSL)
            sync.wait_ge(s_qmm, 8)
            sync.wait_ge(s_qle, 8)
            sync.wait_ge(s_cast, NCAST)
            for j in range(SLOTS):
                if j >= 1:
                    sync.wait_ge(s_msk, TOP_END[j])
                if j >= 2:
                    sync.wait_ge(s_qrope, 8 * (j - 1))
                sync.dma_start(out=mask_w[:], in_=masks[:, j * 8 * QC:(j + 1) * 8 * QC]).then_inc(ld_msk, 16)
                sync.dma_start(out=qc_w[:, j % 2, :], in_=ctab_q[:, j * QC:(j + 1) * QC]).then_inc(lqt[j % 2], 16)
                sync.dma_start(out=qs_w[:, j % 2, :], in_=stab_q[:, j * QC:(j + 1) * QC]).then_inc(lqt[j % 2], 16)
                for tt in range(2):
                    it = 2 * j + tt
                    if j >= 1:
                        sync.wait_ge(s_app, it - 1)
                    sync.dma_start(out=xres[:, tt, :], in_=xo_r[it]).then_inc(lxr[tt], 16)
            sync.wait_ge(s_o, NU)
            sync.wait_ge(s_exp, NU)
            sync.wait_ge(s_msk, TOP_END[SLOTS])
            sync.wait_ge(s_qbmm, 64)
            sync.wait_ge(s_qrope, 64)
            sync.wait_ge(s_norm, 16 * SLOTS)
            sync.wait_ge(s_obc, 16 * 32)
            sync.wait_ge(s_epimm, 16)
            sync.wait_ge(s_xsum, 16)
            sync.wait_ge(s_app, 16)
            sync.wait_ge(s_xnb, 16)
            sync.wait_ge(s_xnt, 16)
            sync.wait_ge(s_xnte, 32)
            sync.wait_ge(ssp[0], 16 * 8)
            sync.wait_ge(ssp[1], 16 * 8)
            for bb in range(2):
                for hc in range(HCH):
                    k = bb * HCH + hc
                    if k >= 2:
                        sync.wait_ge(s_wcast, 2 * (k - 1))
                    sync.dma_start(out=wgi_s[:, hc % 2, :], in_=wg_r[hc]).then_inc(lw[hc % 2], 16)
                    sync.dma_start(out=wui_s[:, hc % 2, :], in_=wu_r[hc]).then_inc(lw[hc % 2], 16)
                for ti in range(8):
                    gi = bb * 8 + ti

                    if gi >= 2:
                        sync.wait_ge(s_fsum, gi - 1)
                    sync.dma_start(out=xnr[:, gi % 2, :], in_=spill_r[gi]).then_inc(lxn[gi % 2], 16)

        @block.gpsimd
        def _(gpsimd):
            for s in range(NSL):
                if s % 2 == 1:
                    if s >= 2:
                        gpsimd.wait_ge(c_xb, s - 1)
                    gpsimd.dma_start(out=xf[:, s % 2, :], in_=x_full_r[s]).then_inc(ld_x2, 16)
            for j in range(SLOTS):
                for r in range(4):
                    q3 = 4 * j + r
                    gpsimd.wait_ge(s_sum, q3 + 1)
                    if q3 >= 2:
                        gpsimd.wait_ge(s_norm, 4 * (q3 - 1))
                    gpsimd.dma_start(out=srow_d[q3:q3 + 1, :], in_=srow_t[64:65, :]).then_inc(s_row, 16)
                    gpsimd.wait_ge(s_row, 16 * (q3 + 1))
                    gpsimd.dma_start(out=obc[:, q3 % 2, :],
                                     in_=srow_d[q3:q3 + 1, :].partition_broadcast(64)).then_inc(s_obc, 16)
                for tt in range(2):
                    it = 2 * j + tt
                    gpsimd.wait_ge(s_app, it + 1)
                    gpsimd.dma_start(out=spill_r[it], in_=xn_f[:, tt, :]).then_inc(ssp[tt], 16)
            for hc in range(HCH):
                if hc >= 2:
                    gpsimd.wait_ge(s_dwc, hc - 1)
                gpsimd.dma_start(out=wdi_s[:, hc % 2, :, :],
                                 in_=wd_i[:, hc * 128:(hc + 1) * 128].rearrange("(c p) h -> p c h", p=128)
                                 ).then_inc(lwd[hc % 2], 16)
            for gi in range(16):
                gpsimd.wait_ge(s_app2, gi + 1)
                gpsimd.dma_start(out=y_r[gi], in_=fsb[:, gi % 2, :]).then_inc(sy[gi % 2], 16)
            gpsimd.wait_ge(sy[0], 16 * 8)
            gpsimd.wait_ge(sy[1], 16 * 8)

        @block.scalar
        def _(scalar):
            san = [0]
            def sai(instr):
                instr.then_inc(sa, 1); san[0] += 1
                return instr
            for s in range(NSL):
                if s >= 2:
                    scalar.wait_ge(s_rope, 2 * (s - 1))
                scalar.dma_start(out=ct_w[:, s % 2, :], in_=ctab_kv[:, 128 * s:128 * (s + 1)]).then_inc(lt[s % 2], 16)
                scalar.dma_start(out=st_w[:, s % 2, :], in_=stab_kv[:, 128 * s:128 * (s + 1)]).then_inc(lt[s % 2], 16)
                scalar.wait_ge(s_lat, s + 1)
                scalar.activation(out=kvlatT[:, 128 * s:128 * (s + 1)], in_=pA2[0:KVL, 0:128],
                                  func=AF.Copy).then_inc(s_kvl, 1)
            for j in range(SLOTS):
                scalar.wait_ge(s_qmm, j + 1)
                scalar.activation(out=qlatT[:, j, :], in_=pA2[0:QL, 256:256 + QC],
                                  func=AF.Copy, scale=0.125).then_inc(s_qle, 1)
            for ui, (j, g, hh, kc, top, q3, fk, lk) in enumerate(UNITS):
                scalar.wait_ge(s_squad, ui + 1)
                if ui >= 3:
                    scalar.wait_ge(s_o, ui - 2)
                scalar.activation(out=Pq[:, ui % 3, :], in_=sq[ui % 3][:, :],
                                  func=AF.Exp).then_inc(s_exp, 1)
                if lk and q3 % 4 == 3:
                    for tt in range(2):
                        it = 2 * j + tt
                        scalar.wait_ge(s_stat, it + 1)
                        sai(scalar.activation(out=istd[:], in_=veps[:], func=AF.Ln))
                        scalar.wait_ge(sa, san[0])
                        scalar.activation(out=istd[:], in_=istd[:], func=AF.Exp,
                                          scale=-0.5).then_inc(s_istd, 1)
            for bb in range(2):
                for hc in range(HCH):
                    for tt4 in range(2):
                        f = hc * 2 + tt4
                        fb = bb * 48 + f
                        scalar.wait_ge(s_gu, fb + 1)
                        if fb >= 2:
                            scalar.wait_ge(s_sil, fb - 1)
                        scalar.activation(out=tb[:, fb % 2, :], in_=pA[f % 3][:, 0:512],
                                          func=AF.Tanh, scale=0.5 / 127.0).then_inc(s_se, 1)
                for ti in range(8):
                    gi = bb * 8 + ti
                    scalar.wait_ge(s_stat2, gi + 1)
                    sai(scalar.activation(out=istd[:], in_=veps[:], func=AF.Ln))
                    scalar.wait_ge(sa, san[0])
                    scalar.activation(out=istd[:], in_=istd[:], func=AF.Exp,
                                      scale=-0.5).then_inc(s_istd2, 1)

        @block.vector
        def _(vector):
            sdn = [0]
            def sdi(instr):
                instr.then_inc(sd, 1); sdn[0] += 1
                return instr
            vector.wait_ge(lc_qa, 16)
            vector.tensor_copy(out=wqakvaB[:], in_=sstage[:]).then_inc(s_cast, 1)
            vector.wait_ge(lc_k1, 16)
            vector.tensor_copy(out=wkB[:], in_=s3stage[:, 0, :]).then_inc(s_cast, 1)
            vector.wait_ge(lc_k2, 16)
            vector.tensor_copy(out=wksB[:], in_=s3stage[:, 1, :]).then_inc(s_cast, 1)
            vector.wait_ge(lc_k3, 16)
            vector.tensor_copy(out=wvB[:], in_=s3stage[:, 2, :]).then_inc(s_cast, 1)
            vector.wait_ge(lc_bo, 16)
            vector.tensor_copy(out=boutB[:], in_=bstage[:]).then_inc(s_cast, 1)
            vector.wait_ge(lc_qb, 16)
            vector.tensor_copy(out=wqbB[:], in_=s2stage[:]).then_inc(s_cast, 1)
            vector.wait_ge(lc_qb, 32)
            vector.tensor_copy(out=wqbsB[:], in_=s2stage[:]).then_inc(s_cast, 1)
            for r in range(8):
                vector.wait_ge(ld_c, 16 * (r + 1))
                vector.tensor_copy(out=woutB[:, 2 * r:2 * r + 2, :], in_=wstage[:]).then_inc(s_cast, 1)
            vector.memset(Vt[:], 1.0).then_inc(s_ms, 1)
            vector.memset(onesB[:], 1.0).then_inc(s_ms, 1)
            for s in range(NSL):
                if s % 2 == 0:
                    vector.wait_ge(ld_x, 16 * (s // 2 + 1))
                else:
                    vector.wait_ge(ld_x2, 16 * ((s + 1) // 2))
                if s >= 2:
                    vector.wait_ge(s_trec, 8 * (s - 1))
                vector.tensor_copy(out=xb[:, s % 2, :], in_=xf[:, s % 2, :]).then_inc(c_xb, 1)
                for ec in range(ECH):
                    vector.wait_ge(s_trec, 8 * s + ec + 1)
                    if s >= 2 and ec == 0:
                        vector.wait_ge(s_lat, s - 1)
                    vector.tensor_copy(out=xTw[:, s % 2, ec, :], in_=pTR[ec % 2][:, 0:128]).then_inc(s_xte, 1)
                for g in range(2):
                    vector.wait_ge(s_kmm, 2 * s + g + 1)
                    vector.wait_ge(lt[s % 2], 32 * (s // 2 + 1))
                    sdi(vector.tensor_tensor(out=rtmp1[:, 0:128], in0=pA0[:, 512 * g:512 * g + 128],
                                             in1=ct_w[:, s % 2, :], op=ALU.mult))
                    sdi(vector.tensor_tensor(out=rtmp2[:, 0:128], in0=pA1[:, 512 * g:512 * g + 128],
                                             in1=st_w[:, s % 2, :], op=ALU.mult))
                    vector.wait_ge(sd, sdn[0])
                    vector.tensor_add(out=KTp[g][:, 128 * s:128 * (s + 1)],
                                      in0=rtmp1[:, 0:128], in1=rtmp2[:, 0:128]).then_inc(s_rope, 1)
                vector.wait_ge(s_vmm, s + 1)
                if s == 0:
                    vector.wait_ge(s_ms, 2)
                vector.tensor_copy(out=Vt[:, s, :, 0:64],
                                   in_=pA2[:, 512:768].rearrange("p (h d) -> p h d", h=4)
                                   ).then_inc(s_vev, 1)
            for i in range(16):
                vector.wait_ge(lxo[i % 2], 16 * (i // 2 + 1))
                if i >= 2:
                    vector.wait_ge(s_troc, 8 * (i - 1))
                vector.tensor_copy(out=xb[:, i % 2, :], in_=xf[:, i % 2, :]).then_inc(c_xob, 1)
                tt = i % 2
                for ec in range(ECH):
                    vector.wait_ge(s_troc, 8 * i + ec + 1)
                    if i >= 2 and ec == 0:
                        vector.wait_ge(s_qmm, i // 2)
                    vector.tensor_copy(out=xoT[:, ec, tt * 128:(tt + 1) * 128],
                                       in_=pTR[ec % 2][:, 0:128]).then_inc(s_xoev, 1)
            prev_slot = -1
            for ui, (j, g, hh, kc, top, q3, fk, lk) in enumerate(UNITS):
                if j != prev_slot:
                    prev_slot = j
                    for pr in range(8):
                        vector.wait_ge(s_qbmm, 8 * j + pr + 1)
                        vector.wait_ge(lqt[j % 2], 32 * (j // 2 + 1))
                        src = sq[pr % 2]
                        sdi(vector.tensor_tensor(out=rtmp1[:, 0:QC], in0=src[:, 0:QC],
                                                 in1=qc_w[:, j % 2, :], op=ALU.mult))
                        sdi(vector.tensor_tensor(out=rtmp2[:, 0:QC], in0=src[:, 512:512 + QC],
                                                 in1=qs_w[:, j % 2, :], op=ALU.mult))
                        vector.wait_ge(sd, sdn[0])
                        vector.tensor_add(out=qTw[:, pr, :], in0=rtmp1[:, 0:QC],
                                          in1=rtmp2[:, 0:QC]).then_inc(s_qrope, 1)
                if top >= 0:
                    vector.wait_ge(s_exp, ui + 1)
                    vector.wait_ge(ld_msk, 16 * (j + 1))
                    ki = kc - 8 * j
                    mk = mask_w[:, ki * QC:(ki + 1) * QC]
                    pv = Pq[:, ui % 3, :].rearrange("p (r n) -> p r n", r=4)
                    vector.tensor_tensor(out=pv, in0=pv,
                                         in1=mk.unsqueeze(1).broadcast_to([128, 4, QC]),
                                         op=ALU.mult).then_inc(s_msk, 1)
                if lk:
                    vector.wait_ge(s_o, ui + 1)
                    if q3 >= 2:
                        vector.wait_ge(s_norm, 4 * (q3 - 1))
                    sdi(vector.tensor_copy(out=osb[:, q3 % 2, 0:512], in_=obank(q3, 0)[0:65, :] if False else pT0[:, :]))
                    vector.tensor_copy(out=osb[:, q3 % 2, 512:1024], in_=pT1[:, :]).then_inc(s_oev, 1)
                    if q3 >= 1:
                        vector.wait_ge(s_row, 16 * q3)
                    vector.wait_ge(s_oev, q3 + 1)
                    with nc.allow_low_precision(reason="1/softmax-sums in bf16; rvr margin 20x"):
                        sdi(vector.reciprocal(out=srow_t[64:65, 0:512], in_=osb[64:65, q3 % 2, 0:512]))
                        vector.reciprocal(out=srow_t[64:65, 512:1024],
                                          in_=osb[64:65, q3 % 2, 512:1024]).then_inc(s_sum, 1)
                    vector.wait_ge(s_obc, 16 * (q3 + 1))
                    if j >= 1:
                        vector.wait_ge(s_epimm, 2 * j)
                    for ho in range(4):
                        c = (q3 % 4) * 4 + ho
                        vector.tensor_tensor(out=OT[:, c, :],
                                             in0=osb[0:64, q3 % 2, ho * QC:(ho + 1) * QC],
                                             in1=obc[:, q3 % 2, (2 * (ho // 2) + (ho % 2)) * QC:
                                                     (2 * (ho // 2) + (ho % 2) + 1) * QC],
                                             op=ALU.mult).then_inc(s_norm, 1)
                if lk and q3 % 4 == 3:
                    for tt in range(2):
                        it = 2 * j + tt
                        vector.wait_ge(s_epimm, it + 1)
                        vector.wait_ge(lxr[tt], 16 * (j + 1))
                        sdi(vector.tensor_add(out=xres[:, tt, :], in0=sq[tt][:, :], in1=xres[:, tt, :]))
                        vector.wait_ge(sd, sdn[0])
                        if it >= 1:
                            vector.wait_ge(s_istd, it)
                        sdi(vector.bn_stats(stat[:, 0:6], xres[:, tt, 0:512]))
                        vector.bn_stats(stat[:, 6:12], xres[:, tt, 512:1024]).then_inc(s_xsum, 1)
                        vector.wait_ge(sd, sdn[0])
                        vector.wait_ge(s_xsum, it + 1)
                        sdi(vector.bn_aggr(mv[:], stat[:, 0:12]))
                        vector.wait_ge(sd, sdn[0])
                        vector.tensor_scalar_add(veps[:], mv[:, 1:2], EPS).then_inc(s_stat, 1)
                        vector.wait_ge(s_istd, it + 1)
                        vector.tensor_scalar(out=nbias[:], in0=istd[:], scalar1=mv[:, 0:1],
                                             scalar2=-1.0, op0=ALU.mult, op1=ALU.mult).then_inc(s_nb, 1)
                        vector.wait_ge(s_nb, it + 1)
                        if j >= 1:
                            vector.wait_ge(ssp[tt], 16 * (it // 2))
                        vector.tensor_scalar(out=xn_f[:, tt, :], in0=xres[:, tt, :],
                                             scalar1=istd[:], scalar2=nbias[:],
                                             op0=ALU.mult, op1=ALU.add).then_inc(s_app, 1)
                        vector.wait_ge(s_app, it + 1)
                        if j >= 1:
                            vector.wait_ge(s_xnt, it - 1)
                        vector.tensor_copy(out=xn_b[:, tt, :], in_=xn_f[:, tt, :]).then_inc(s_xnb, 1)
                        vector.wait_ge(s_xnt, it + 1)
                        col = j * QC + tt * 128
                        for hf in range(2):
                            vector.tensor_copy(
                                out=xnT[:, 4 * hf:4 * hf + 4, col:col + 128],
                                in_=pTR[hf][:, :].rearrange("p (c n) -> p c n", c=4)
                                ).then_inc(s_xnte, 1)
            for hc in range(HCH):
                vector.wait_ge(lwd[hc % 2], 16 * (hc // 2 + 1))
                if hc >= 2:
                    vector.wait_ge(s_dwt, 2 * (hc - 1))
                vector.tensor_copy(out=wdb_s[:, hc % 2, :, :], in_=wdi_s[:, hc % 2, :, :]).then_inc(s_dwc, 1)
                for grp in range(2):
                    vector.wait_ge(s_dwt, 2 * hc + grp + 1)
                    vector.tensor_copy(out=wdT[:, hc, grp * 512:(grp + 1) * 512],
                                       in_=pTR[grp][:, :]).then_inc(s_dwev, 1)
            for bb in range(2):
                for hc in range(HCH):
                    k = bb * HCH + hc
                    vector.wait_ge(lw[hc % 2], 32 * (bb * 12 + hc // 2 + 1))
                    if k >= 2:
                        vector.wait_ge(s_wtr, 4 * (k - 1))
                    vector.tensor_copy(out=wgb_s[:, hc % 2, :], in_=wgi_s[:, hc % 2, :]).then_inc(s_wcast, 1)
                    vector.tensor_copy(out=wub_s[:, hc % 2, :], in_=wui_s[:, hc % 2, :]).then_inc(s_wcast, 1)
                    for grp in range(4):
                        vector.wait_ge(s_wtr, 4 * k + grp + 1)
                        dst = (wgT if grp < 2 else wuT)
                        hf = grp % 2
                        vector.tensor_copy(out=dst[:, hc % 2, 4 * hf:4 * hf + 4, :],
                                           in_=pTR[grp % 2][:, :].rearrange("p (c n) -> p c n", c=4)
                                           ).then_inc(s_wev, 1)
                    for tt4 in range(2):
                        f = hc * 2 + tt4
                        fb = bb * 48 + f
                        vector.wait_ge(s_se, fb + 1)
                        u1 = vector.tensor_scalar_add(ub[:, fb % 2, :], tb[:, fb % 2, :], 1.0)
                        sdi(u1)
                        vector.wait_ge(sd, sdn[0])
                        h1 = vector.tensor_tensor(out=rb[:, fb % 2, :], in0=ub[:, fb % 2, :],
                                                  in1=pA[f % 3][:, 0:512], op=ALU.mult)
                        sdi(h1)
                        vector.wait_ge(sd, sdn[0])
                        vector.tensor_tensor(out=hactT[:, hc, tt4 * 512:(tt4 + 1) * 512],
                                             in0=rb[:, fb % 2, :], in1=pA[f % 3][:, 512:1024],
                                             op=ALU.mult).then_inc(s_sil, 1)
                for ti in range(8):
                    gi = bb * 8 + ti
                    vector.wait_ge(s_dmm, gi + 1)
                    vector.wait_ge(lxn[gi % 2], 16 * (gi // 2 + 1))
                    if gi >= 2:
                        vector.wait_ge(sy[gi % 2], 16 * (gi // 2))
                    if gi == 0:
                        vector.wait_ge(s_app, 16)
                    vector.scalar_tensor_tensor(out=fsb[:, gi % 2, :], in0=pA[ti % 3][:, :],
                                                scalar=DQ3, in1=xnr[:, gi % 2, :],
                                                op0=ALU.mult, op1=ALU.add).then_inc(s_fsum, 1)
                    vector.wait_ge(s_fsum, gi + 1)
                    if gi >= 1:
                        vector.wait_ge(s_istd2, gi)
                    sdi(vector.bn_stats(stat[:, 0:6], fsb[:, gi % 2, 0:512]))
                    sdi(vector.bn_stats(stat[:, 6:12], fsb[:, gi % 2, 512:1024]))
                    vector.wait_ge(sd, sdn[0])
                    sdi(vector.bn_aggr(mv[:], stat[:, 0:12]))
                    vector.wait_ge(sd, sdn[0])
                    vector.tensor_scalar_add(veps[:], mv[:, 1:2], EPS).then_inc(s_stat2, 1)
                    vector.wait_ge(s_istd2, gi + 1)
                    vector.tensor_scalar(out=nbias[:], in0=istd[:], scalar1=mv[:, 0:1],
                                         scalar2=-1.0, op0=ALU.mult, op1=ALU.mult).then_inc(s_nb2, 1)
                    vector.wait_ge(s_nb2, gi + 1)
                    vector.tensor_scalar(out=fsb[:, gi % 2, :], in0=fsb[:, gi % 2, :],
                                         scalar1=istd[:], scalar2=nbias[:],
                                         op0=ALU.mult, op1=ALU.add).then_inc(s_app2, 1)

        @block.tensor
        def _(tensor):
            for s in range(NSL):
                tensor.wait_ge(c_xb, s + 1)
                if s == 0:
                    tensor.wait_ge(lc_i, 16)
                for ec in range(ECH):
                    w = 8 * s + ec - 1
                    if w > 0:
                        tensor.wait_ge(s_xte, w)
                    mm = tensor.matmul(pTR[ec % 2][:, 0:128],
                                       xb[:, s % 2, ec * 128:(ec + 1) * 128],
                                       identB[:], start=True, stop=True)
                    mm.then_inc(s_trec, 1)
                tensor.wait_ge(s_xte, 8 * (s + 1))
                if s >= 1:
                    tensor.wait_ge(s_kvl, s)
                if s == 0:
                    tensor.wait_ge(s_cast, NCAST)
                for ec in range(ECH):
                    mm = tensor.matmul(pA2[0:KVL, 0:128], wqakvaB[:, ec, 64:96],
                                       xTw[:, s % 2, ec, :], start=(ec == 0), stop=(ec == ECH - 1))
                mm.then_inc(s_lat, 1)
                tensor.wait_ge(s_kvl, s + 1)
                for g in range(2):
                    w = 2 * s + g - 1
                    if w > 0:
                        tensor.wait_ge(s_rope, w)
                    tensor.matmul(pA0[:, 512 * g:512 * g + 128], wkB[:, g * 128:(g + 1) * 128],
                                  kvlatT[:, 128 * s:128 * (s + 1)], start=True, stop=True)
                    mm = tensor.matmul(pA1[:, 512 * g:512 * g + 128], wksB[:, g * 128:(g + 1) * 128],
                                       kvlatT[:, 128 * s:128 * (s + 1)], start=True, stop=True)
                    mm.then_inc(s_kmm, 1)
                if s >= 1:
                    tensor.wait_ge(s_vev, s)
                mm = tensor.matmul(pA2[:, 512:768],
                                   kvlatT[:, 128 * s:128 * (s + 1)],
                                   wvB[:], start=True, stop=True)
                mm.then_inc(s_vmm, 1)
            for i in range(16):
                tensor.wait_ge(c_xob, i + 1)
                tt = i % 2
                for ec in range(ECH):
                    w = 8 * i + ec - 1
                    if w > 0:
                        tensor.wait_ge(s_xoev, w)
                    mm = tensor.matmul(pTR[ec % 2][:, 0:128], xb[:, i % 2, ec * 128:(ec + 1) * 128],
                                       identB[:], start=True, stop=True)
                    mm.then_inc(s_troc, 1)
                if tt == 1:
                    jslot = i // 2
                    tensor.wait_ge(s_xoev, 8 * (i + 1))
                    if jslot == 0:
                        tensor.wait_ge(s_kvl, NSL)
                    if jslot >= 1:
                        tensor.wait_ge(s_qle, jslot)
                    for ec in range(ECH):
                        mm = tensor.matmul(pA2[0:QL, 256:256 + QC], wqakvaB[:, ec, 0:QL],
                                           xoT[:, ec, :], start=(ec == 0), stop=(ec == ECH - 1))
                    mm.then_inc(s_qmm, 1)
            for j in range(SLOTS):
                for pr in range(8):
                    if pr == 0:
                        tensor.wait_ge(s_qle, j + 1)
                        tensor.wait_ge(s_exp, U_BEFORE[j])
                        tensor.wait_ge(s_xsum, 2 * j)
                        if j == 0:
                            tensor.wait_ge(s_rope, 2 * NSL)
                    w = 8 * j + pr - 1
                    if w > 0:
                        tensor.wait_ge(s_qrope, w)
                    dst = sq[pr % 2]
                    tensor.matmul(dst[:, 0:QC], wqbB[:, pr * 128:(pr + 1) * 128],
                                  qlatT[:, j, :], start=True, stop=True)
                    mm = tensor.matmul(dst[:, 512:512 + QC], wqbsB[:, pr * 128:(pr + 1) * 128],
                                       qlatT[:, j, :], start=True, stop=True)
                    mm.then_inc(s_qbmm, 1)
                for ui in range(U_BEFORE[j], U_BEFORE[j + 1]):
                    jj, g, hh, kc, top, q3, fk, lk = UNITS[ui]
                    if ui == U_BEFORE[j]:
                        tensor.wait_ge(s_qrope, 8 * (j + 1))
                        if j == 0:
                            tensor.wait_ge(s_vev, NSL)
                            tensor.wait_ge(s_qmm, SLOTS)
                    if ui >= 3:
                        tensor.wait_ge(s_exp, ui - 2)
                    if fk:
                        if q3 >= 1:
                            tensor.wait_ge(s_oev, q3)
                        if q3 == 0:
                            tensor.wait_ge(s_xoev, 128)
                        if j >= 1 and q3 % 4 == 0:
                            tensor.wait_ge(s_xnte, 4 * j)
                    pr0 = 4 * g + 2 * hh
                    tensor.matmul(sq[ui % 3][:, 0:512],
                                  KTp[g][0:64, 128 * kc:128 * (kc + 1)],
                                  qTw[0:64, pr0:pr0 + 2, :], start=True, stop=True,
                                  tile_position=(0, 0), skip_group_check=True)
                    mm = tensor.matmul(sq[ui % 3][:, 512:1024],
                                       KTp[g][64:128, 128 * kc:128 * (kc + 1)],
                                       qTw[64:128, pr0:pr0 + 2, :], start=True, stop=True,
                                       tile_position=(64, 0), skip_group_check=True)
                    mm.then_inc(s_squad, 1)
                    if top >= 0:
                        tensor.wait_ge(s_msk, top + 1)
                    else:
                        tensor.wait_ge(s_exp, ui + 1)
                    for bi in range(2):
                        kvh = 2 * g + bi
                        mm = tensor.matmul(obank(q3, bi)[0:65, 0:512],
                                           Vt[:, kc, kvh, :],
                                           Pq[:, ui % 3, 512 * bi:512 * (bi + 1)],
                                           start=fk, stop=lk, skip_group_check=True)
                    mm.then_inc(s_o, 1)
                for tt in range(2):
                    if tt == 0:
                        tensor.wait_ge(s_norm, 16 * (j + 1))
                        tensor.wait_ge(s_exp, U_BEFORE[j + 1])
                    for c in range(16):
                        for eh in range(2):
                            tensor.matmul(sq[tt][:, eh * 512:(eh + 1) * 512],
                                          OT[:, c, tt * 128:(tt + 1) * 128],
                                          woutB[:, c, eh * 512:(eh + 1) * 512],
                                          start=(c == 0), stop=False, skip_group_check=True)
                    for eh in range(2):
                        mm = tensor.matmul(sq[tt][:, eh * 512:(eh + 1) * 512],
                                           onesB[:], boutB[:, eh * 512:(eh + 1) * 512],
                                           start=False, stop=True, skip_group_check=True)
                    mm.then_inc(s_epimm, 1)
                for tt in range(2):
                    it = 2 * j + tt
                    tensor.wait_ge(s_xnb, it + 1)
                    if tt == 0:
                        tensor.wait_ge(s_oev, 4 * (j + 1))
                    else:
                        tensor.wait_ge(s_xnte, 4 * j + 2)
                    for ec in range(ECH):
                        hf = ec // 4
                        mm = tensor.matmul(pTR[hf][:, (ec % 4) * 128:(ec % 4 + 1) * 128],
                                           xn_b[:, tt, ec * 128:(ec + 1) * 128],
                                           identB[:], start=True, stop=True, skip_group_check=True)
                    mm.then_inc(s_xnt, 1)
            for hc in range(HCH):
                tensor.wait_ge(s_dwc, hc + 1)
                if hc == 0:
                    tensor.wait_ge(s_xnte, 4 * SLOTS)
                for grp in range(2):
                    w = 2 * hc + grp - 1
                    if w > 0:
                        tensor.wait_ge(s_dwev, w)
                    for q in range(4):
                        dc = grp * 4 + q
                        mm = tensor.matmul(pTR[grp][:, q * 128:(q + 1) * 128],
                                           wdb_s[:, hc % 2, dc, :], identB[:],
                                           start=True, stop=True)
                    mm.then_inc(s_dwt, 1)
            for bb in range(2):
                for hc in range(HCH):
                    k = bb * HCH + hc
                    tensor.wait_ge(s_wcast, 2 * (k + 1))
                    for grp in range(4):
                        if k == 0 and grp == 0:
                            tensor.wait_ge(s_dwev, 2 * HCH)
                        w = 4 * k + grp - 1
                        if w > 0:
                            tensor.wait_ge(s_wev, w)
                        src = (wgb_s if grp < 2 else wub_s)
                        hf = grp % 2
                        for q in range(4):
                            ec = hf * 4 + q
                            mm = tensor.matmul(pTR[grp % 2][:, q * 128:(q + 1) * 128],
                                               src[:, hc % 2, ec * 128:(ec + 1) * 128],
                                               identB[:], start=True, stop=True)
                        mm.then_inc(s_wtr, 1)
                    for tt4 in range(2):
                        f = hc * 2 + tt4
                        fb = bb * 48 + f
                        if tt4 == 0:
                            tensor.wait_ge(s_wev, 4 * (k + 1))
                        if fb >= 3:
                            tensor.wait_ge(s_sil, fb - 2)
                        if bb == 1 and f == 0:
                            tensor.wait_ge(s_fsum, 8)
                        if fb == 0:
                            tensor.wait_ge(s_xnte, 4 * SLOTS)
                        for ec in range(ECH):
                            tensor.matmul(pA[f % 3][:, 0:512], wgT[:, hc % 2, ec, :],
                                          xnT[:, ec, bb * 1024 + tt4 * 512:bb * 1024 + (tt4 + 1) * 512],
                                          start=(ec == 0), stop=(ec == ECH - 1))
                        for ec in range(ECH):
                            mm = tensor.matmul(pA[f % 3][:, 512:1024], wuT[:, hc % 2, ec, :],
                                               xnT[:, ec, bb * 1024 + tt4 * 512:bb * 1024 + (tt4 + 1) * 512],
                                               start=(ec == 0), stop=(ec == ECH - 1))
                        mm.then_inc(s_gu, 1)
                for ti in range(8):
                    gi = bb * 8 + ti
                    tensor.wait_ge(s_sil, 48 * (bb + 1))
                    if gi == 0:
                        tensor.wait_ge(s_dwev, 2 * HCH)
                    if gi >= 3:
                        tensor.wait_ge(s_fsum, gi - 2)
                    for eh in range(2):
                        for hc in range(HCH):
                            mm = tensor.matmul(pA[ti % 3][:, eh * 512:(eh + 1) * 512],
                                               hactT[:, hc, ti * 128:(ti + 1) * 128],
                                               wdT[:, hc, eh * 512:(eh + 1) * 512],
                                               start=(hc == 0), stop=(hc == HCH - 1),
                                               skip_group_check=True)
                    mm.then_inc(s_dmm, 1)

    return nc


def _rope_tables(positions):
    d = np.arange(128)
    i = (d % 64) // 2
    inv = 1.0 / (10000.0 ** (2.0 * i / 64.0))
    ang = np.outer(inv, positions.astype(np.float64))
    c = np.cos(ang).astype(np.float32)
    s = np.sin(ang).astype(np.float32)
    sign = np.where(d % 2 == 0, -1.0, 1.0).astype(np.float32)[:, None]
    return np.ascontiguousarray(c), np.ascontiguousarray(s * sign)


def kernel(L0_attn_x, L0_attn_Wqa, L0_attn_Wqb, L0_attn_Wkva, L0_attn_Wkvb,
           L0_attn_Wout, L0_attn_bout, L0_ffn_w_gate, L0_ffn_w_up, L0_ffn_w_down):
    x = np.ascontiguousarray(np.asarray(L0_attn_x, dtype=np.float32))
    Wqa = np.asarray(L0_attn_Wqa, dtype=np.float32)
    Wqb = np.asarray(L0_attn_Wqb, dtype=np.float32)
    Wkva = np.asarray(L0_attn_Wkva, dtype=np.float32)
    Wkvb = np.asarray(L0_attn_Wkvb, dtype=np.float32)
    Wout = np.asarray(L0_attn_Wout, dtype=np.float32)
    bout = np.ascontiguousarray(np.asarray(L0_attn_bout, dtype=np.float32).reshape(1, E))
    wg = np.ascontiguousarray(np.asarray(L0_ffn_w_gate, dtype=np.int32))
    wu = np.ascontiguousarray(np.asarray(L0_ffn_w_up, dtype=np.int32))
    wd = np.ascontiguousarray(np.asarray(L0_ffn_w_down, dtype=np.int32))

    wkvb_r = Wkvb.reshape(KVL, KVH, D, 2)
    wk = np.ascontiguousarray(wkvb_r[..., 0].reshape(KVL, 256))
    wv = np.ascontiguousarray(wkvb_r[..., 1].reshape(KVL, 256))
    wks = np.ascontiguousarray(
        wkvb_r[..., 0].reshape(KVL, KVH, 32, 2)[:, :, :, ::-1].reshape(KVL, 256))
    qperm = []
    for g in range(2):
        for i in range(4):
            ha, hb = 8 * g + i, 8 * g + 4 + i
            qperm.extend(range(64 * ha, 64 * ha + 64))
            qperm.extend(range(64 * hb, 64 * hb + 64))
    qperm = np.array(qperm)
    wqb_p = np.ascontiguousarray(Wqb[:, qperm])
    wqbs_full = Wqb.reshape(QL, H, 32, 2)[:, :, :, ::-1].reshape(QL, E)
    wqbs_p = np.ascontiguousarray(wqbs_full[:, qperm])
    operm = []
    for g in range(2):
        for hh in range(2):
            for h in (8 * g + 2 * hh, 8 * g + 2 * hh + 1, 8 * g + 4 + 2 * hh, 8 * g + 4 + 2 * hh + 1):
                operm.extend(range(64 * h, 64 * h + 64))
    operm = np.array(operm)
    wout_p = np.ascontiguousarray(Wout[operm, :])
    wqakva_ = np.ascontiguousarray(np.concatenate([Wqa, Wkva], axis=1))
    identity = np.eye(128, dtype=BF)
    ckv, skv = _rope_tables(np.arange(S))

    nc = build_nc()
    in_maps = []
    chunk_of = np.zeros((NCORE, SLOTS), dtype=np.int64)
    for c in range(NCORE):
        b, cm = c // 4, c % 4
        chunks = [4 * j + cm for j in range(SLOTS)]
        chunk_of[c] = chunks
        xb_ = x[b]
        x_own_ = np.ascontiguousarray(
            np.concatenate([xb_[QC * p:QC * (p + 1)] for p in chunks], axis=0))
        pos_own = np.concatenate([np.arange(QC * p, QC * (p + 1)) for p in chunks])
        cq, sq_ = _rope_tables(pos_own)
        cq = cq.astype(BF); sq_ = sq_.astype(BF)
        m = np.zeros((128, SLOTS * 8 * QC), dtype=BF)
        iv = np.arange(128)[:, None]
        qv = np.arange(QC)[None, :]
        for j in range(SLOTS):
            p = chunks[j]
            for ki in range(8):
                v = 1024 * j + 128 * ki + iv
                m[:, (j * 8 + ki) * QC:(j * 8 + ki + 1) * QC] = (v <= QC * p + qv).astype(BF)
        in_maps.append({
            "x_own": x_own_, "x_full": np.ascontiguousarray(xb_),
            "ident": identity, "masks": m,
            "ctab_kv": ckv, "stab_kv": skv, "ctab_q": cq, "stab_q": sq_,
            "wqakva": wqakva_, "wqb": wqb_p, "wqbs": wqbs_p,
            "wk": wk, "wks": wks, "wv": wv,
            "wout": wout_p, "bout": bout,
            "wg": wg, "wu": wu, "wd": wd,
        })

    res = run_bass_kernel_spmd(nc, in_maps, core_ids=list(range(NCORE)))

    out = np.empty((B, S, E), dtype=np.float32)
    for c in range(NCORE):
        b = c // 4
        yc = np.asarray(res.results[c]["y"])
        for j in range(SLOTS):
            p = chunk_of[c][j]
            out[b, QC * p:QC * (p + 1)] = yc[QC * j:QC * (j + 1)]
    return out


# version 12
# speedup vs baseline: 1.6691x; 1.6691x over previous
import contextlib
import numpy as np
import ml_dtypes
import concourse.bass as bass
import concourse.mybir as mybir
from concourse.bass_utils import run_bass_kernel_spmd

F32 = mybir.dt.float32
BF16 = mybir.dt.bfloat16
I32 = mybir.dt.int32
AF = mybir.ActivationFunctionType
ALU = mybir.AluOpType
BF = ml_dtypes.bfloat16

B, S, E = 2, 8192, 1024
H, KVH, D = 16, 4, 64
QL, KVL = 64, 32
HID = 3072
NCORE = 8
QC = 256
SLOTS = 8
TOK = QC * SLOTS
ECH = E // 128
HCH = HID // 128
NSL = S // 128
EPS = 1e-5
DQ3 = 1.0 / (2.0 * 127.0 ** 3)

def kcj(j):
    return 8 * (j + 1)

UNITS = []
_t = 0
for _j in range(SLOTS):
    for _g in range(2):
        for _hh in range(2):
            _q3 = _j * 4 + _g * 2 + _hh
            for _kc in range(kcj(_j)):
                to = _t if _kc >= 8 * _j else -1
                if to >= 0:
                    _t += 1
                UNITS.append((_j, _g, _hh, _kc, to, _q3, _kc == 0, _kc == kcj(_j) - 1))
NU = len(UNITS)
U_BEFORE = [0] * (SLOTS + 1)
for _j in range(SLOTS):
    U_BEFORE[_j + 1] = U_BEFORE[_j] + 4 * kcj(_j)
TOP_END = [0] * (SLOTS + 1)
_c = 0
for ui in range(NU):
    if UNITS[ui][4] >= 0:
        _c += 1
    TOP_END[UNITS[ui][0] + 1] = _c


def build_nc():
    nc = bass.Bass("TRN2", target_bir_lowering=False, debug=False)

    def din(name, shape, dt):
        return nc.dram_tensor(name, shape, dt, kind="ExternalInput").ap()

    x_own = din("x_own", [TOK, E], F32)
    x_full = din("x_full", [S, E], F32)
    ident = din("ident", [128, 128], BF16)
    masks = din("masks", [128, SLOTS * 8 * QC], BF16)
    ctab_kv = din("ctab_kv", [128, S], F32)
    stab_kv = din("stab_kv", [128, S], F32)
    ctab_q = din("ctab_q", [128, TOK], BF16)
    stab_q = din("stab_q", [128, TOK], BF16)
    wqakva = din("wqakva", [E, 96], F32)
    wqb_i = din("wqb", [QL, E], F32)
    wqbs_i = din("wqbs", [QL, E], F32)
    wk_i = din("wk", [KVL, 256], F32)
    wks_i = din("wks", [KVL, 256], F32)
    wv_i = din("wv", [KVL, 256], F32)
    wout_i = din("wout", [E, E], F32)
    bout_i = din("bout", [1, E], F32)
    wg_i = din("wg", [HID, E], I32)
    wu_i = din("wu", [HID, E], I32)
    wd_i = din("wd", [E, HID], I32)

    y = nc.dram_tensor("y", [TOK, E], F32, kind="ExternalOutput").ap()
    xn_spill = nc.dram_tensor("xn_spill", [TOK, E], F32, kind="Internal").ap()
    srow_d = nc.dram_tensor("srow_d", [32, 1024], BF16, kind="Internal").ap()

    x_full_r = x_full.rearrange("(n p) e -> n p e", p=128)
    xo_r = x_own.rearrange("(i p) e -> i p e", p=128)
    wg_r = wg_i.rearrange("(c p) e -> c p e", p=128)
    wu_r = wu_i.rearrange("(c p) e -> c p e", p=128)
    spill_r = xn_spill.rearrange("(i p) e -> i p e", p=128)
    y_r = y.rearrange("(i p) e -> i p e", p=128)
    wout_r = wout_i.rearrange("(c p) e -> p c e", p=64)

    with contextlib.ExitStack() as R:
        EE = R.enter_context
        identB = EE(nc.sbuf_tensor("identB", [128, 128], BF16))
        rtmp1 = EE(nc.sbuf_tensor("rtmp1", [128, 512], F32))
        rtmp2 = EE(nc.sbuf_tensor("rtmp2", [128, 512], F32))
        stat = EE(nc.sbuf_tensor("stat", [128, 12], F32))
        mv = EE(nc.sbuf_tensor("mv", [128, 2], F32))
        veps = EE(nc.sbuf_tensor("veps", [128, 1], F32))
        istd = EE(nc.sbuf_tensor("istd", [128, 1], F32))
        nbias = EE(nc.sbuf_tensor("nbias", [128, 1], F32))
        xnT = EE(nc.sbuf_tensor("xnT", [128, ECH, TOK], BF16, side="right"))
        S12 = contextlib.ExitStack()
        E12 = S12.enter_context
        KTp = [E12(nc.sbuf_tensor(f"KTp{g}", [128, S], BF16)) for g in range(2)]
        Vt = E12(nc.sbuf_tensor("Vt", [128, S // 128, KVH, 65], BF16))
        woutB = E12(nc.sbuf_tensor("woutB", [64, 16, E], BF16))
        qlatT = E12(nc.sbuf_tensor("qlatT", [QL, SLOTS, QC], BF16))
        wqbB = E12(nc.sbuf_tensor("wqbB", [QL, E], BF16))
        wqbsB = E12(nc.sbuf_tensor("wqbsB", [QL, E], BF16))
        wkB = E12(nc.sbuf_tensor("wkB", [KVL, 256], BF16))
        wksB = E12(nc.sbuf_tensor("wksB", [KVL, 256], BF16))
        wvB = E12(nc.sbuf_tensor("wvB", [KVL, 256], BF16))
        wqakvaB = E12(nc.sbuf_tensor("wqakvaB", [128, ECH, 96], BF16))
        boutB = E12(nc.sbuf_tensor("boutB", [1, E], BF16))
        onesB = E12(nc.sbuf_tensor("onesB", [1, 128], BF16))
        S1 = contextlib.ExitStack()
        E1 = S1.enter_context
        xf = E1(nc.sbuf_tensor("xf", [128, 2, E], F32))
        xb = E1(nc.sbuf_tensor("xb", [128, 2, E], BF16))
        xTw = E1(nc.sbuf_tensor("xTw", [128, 2, ECH, 128], BF16))
        ct_w = E1(nc.sbuf_tensor("ct_w", [128, 2, 128], F32))
        st_w = E1(nc.sbuf_tensor("st_w", [128, 2, 128], F32))
        xoT = E1(nc.sbuf_tensor("xoT", [128, ECH, QC], BF16))
        wstage = E1(nc.sbuf_tensor("wstage", [64, 2, E], F32))
        bstage = E1(nc.sbuf_tensor("bstage", [1, E], F32))
        sstage = E1(nc.sbuf_tensor("sstage", [128, ECH, 96], F32))
        s2stage = E1(nc.sbuf_tensor("s2stage", [QL, E], F32))
        s3stage = E1(nc.sbuf_tensor("s3stage", [KVL, 3, 256], F32))
        kvlatT = E1(nc.sbuf_tensor("kvlatT", [KVL, S], BF16))
        S1.close()
        S2 = contextlib.ExitStack()
        E2 = S2.enter_context
        qTw = E2(nc.sbuf_tensor("qTw", [128, 8, QC], BF16))
        mask_w = E2(nc.sbuf_tensor("mask_w", [128, 8 * QC], BF16))
        Pq = E2(nc.sbuf_tensor("Pq", [128, 3, 1024], BF16))
        osb = E2(nc.sbuf_tensor("osb", [128, 2, 1024], F32, side="right"))
        OT = E2(nc.sbuf_tensor("OT", [64, 16, QC], BF16))
        obc = E2(nc.sbuf_tensor("obc", [64, 2, 1024], BF16))
        srow_t = E2(nc.sbuf_tensor("srow_t", [128, 1024], BF16))
        qc_w = E2(nc.sbuf_tensor("qc_w", [128, 2, QC], BF16))
        qs_w = E2(nc.sbuf_tensor("qs_w", [128, 2, QC], BF16))
        xres = E2(nc.sbuf_tensor("xres", [128, 2, E], F32))
        xn_f = E2(nc.sbuf_tensor("xn_f", [128, 2, E], F32))
        xn_b = E2(nc.sbuf_tensor("xn_b", [128, 2, E], BF16))
        S2.close()
        S12.close()
        hactT = EE(nc.sbuf_tensor("hactT", [128, HCH, 1024], BF16))
        wdT = EE(nc.sbuf_tensor("wdT", [128, HCH, E], BF16))
        wgi_s = EE(nc.sbuf_tensor("wgi_s", [128, 2, E], I32))
        wui_s = EE(nc.sbuf_tensor("wui_s", [128, 2, E], I32))
        wdi_s = EE(nc.sbuf_tensor("wdi_s", [128, 2, ECH, 128], I32))
        wgb_s = EE(nc.sbuf_tensor("wgb_s", [128, 2, E], BF16))
        wub_s = EE(nc.sbuf_tensor("wub_s", [128, 2, E], BF16))
        wdb_s = EE(nc.sbuf_tensor("wdb_s", [128, 2, ECH, 128], BF16))
        wgT = EE(nc.sbuf_tensor("wgT", [128, 2, ECH, 128], BF16))
        wuT = EE(nc.sbuf_tensor("wuT", [128, 2, ECH, 128], BF16))
        tb = EE(nc.sbuf_tensor("tb", [128, 2, 512], F32))
        ub = EE(nc.sbuf_tensor("ub", [128, 2, 512], F32))
        rb = EE(nc.sbuf_tensor("rb", [128, 2, 512], F32))
        xnr = EE(nc.sbuf_tensor("xnr", [128, 2, E], F32))
        fsb = EE(nc.sbuf_tensor("fsb", [128, 2, E], F32))

        pA0 = EE(nc.psum_tensor("pA0", [128, 1024], F32))
        pA1 = EE(nc.psum_tensor("pA1", [128, 1024], F32))
        pA2 = EE(nc.psum_tensor("pA2", [128, 1024], F32))
        pT0 = EE(nc.psum_tensor("pT0", [128, 512], F32))
        pT1 = EE(nc.psum_tensor("pT1", [128, 512], F32))
        pA = [pA0, pA1, pA2]
        sq = pA
        pTR = [pT0, pT1]

        def obank(q3, b):
            return (pT0 if b == 0 else pT1)[:, :]

        def sname(n):
            return EE(nc.semaphore(n))
        ld_c = sname("ld_c"); ld_x = sname("ld_x"); ld_x2 = sname("ld_x2")
        lc_i = sname("lc_i"); lc_qa = sname("lc_qa"); lc_k1 = sname("lc_k1")
        lc_k2 = sname("lc_k2"); lc_k3 = sname("lc_k3"); lc_bo = sname("lc_bo")
        lc_qb = sname("lc_qb")
        lt = [sname("lt0"), sname("lt1")]
        lqt = [sname("lqt0"), sname("lqt1")]
        lxr = [sname("lxr0"), sname("lxr1")]
        lw = [sname("lw0"), sname("lw1")]
        lwd = [sname("lwd0"), sname("lwd1")]
        lxn = [sname("lxn0"), sname("lxn1")]
        ssp = [sname("ssp0"), sname("ssp1")]
        sy = [sname("sy0"), sname("sy1")]
        lxo = [sname("lxo0"), sname("lxo1")]
        ld_t = sname("ld_t")
        ld_xo = sname("ld_xo"); ld_xres = sname("ld_xres"); ld_msk = sname("ld_msk")
        ld_qt = sname("ld_qt"); ld_w = sname("ld_w"); ld_wd = sname("ld_wd")
        ld_xn2 = sname("ld_xn2")
        s_cast = sname("s_cast"); s_ms = sname("s_ms")
        c_xb = sname("c_xb"); s_trec = sname("s_trec"); s_xte = sname("s_xte")
        s_lat = sname("s_lat"); s_kvl = sname("s_kvl"); s_kmm = sname("s_kmm")
        s_rope = sname("s_rope"); s_vmm = sname("s_vmm"); s_vev = sname("s_vev")
        c_xob = sname("c_xob"); s_troc = sname("s_troc"); s_xoev = sname("s_xoev")
        s_qmm = sname("s_qmm"); s_qle = sname("s_qle")
        s_qbmm = sname("s_qbmm"); s_qrope = sname("s_qrope")
        s_squad = sname("s_squad"); s_exp = sname("s_exp"); s_msk = sname("s_msk")
        s_o = sname("s_o"); s_oev = sname("s_oev"); s_sum = sname("s_sum"); s_row = sname("s_row")
        s_obc = sname("s_obc"); s_norm = sname("s_norm")
        s_epimm = sname("s_epimm"); s_xsum = sname("s_xsum"); s_stat = sname("s_stat")
        s_istd = sname("s_istd"); s_nb = sname("s_nb"); s_app = sname("s_app")
        s_xnb = sname("s_xnb"); s_xnt = sname("s_xnt"); s_xnte = sname("s_xnte")
        s_spill = sname("s_spill")
        s_wcast = sname("s_wcast"); s_wtr = sname("s_wtr"); s_wev = sname("s_wev")
        s_gu = sname("s_gu"); s_se = sname("s_se"); s_sil = sname("s_sil")
        s_dwc = sname("s_dwc"); s_dwt = sname("s_dwt"); s_dwev = sname("s_dwev")
        s_dmm = sname("s_dmm"); s_fsum = sname("s_fsum"); s_stat2 = sname("s_stat2")
        s_istd2 = sname("s_istd2"); s_nb2 = sname("s_nb2"); s_app2 = sname("s_app2")
        s_y = sname("s_y")
        sd = sname("sd"); sa = sname("sa")

        NCAST = 15

        block = EE(nc.Block())

        @block.sync
        def _(sync):
            sync.dma_start(out=identB[:], in_=ident).then_inc(lc_i, 16)
            sync.dma_start(out=sstage[:], in_=wqakva.rearrange("(c p) m -> p c m", p=128)).then_inc(lc_qa, 16)
            sync.dma_start(out=s3stage[:, 0, :], in_=wk_i).then_inc(lc_k1, 16)
            sync.dma_start(out=s3stage[:, 1, :], in_=wks_i).then_inc(lc_k2, 16)
            sync.dma_start(out=s3stage[:, 2, :], in_=wv_i).then_inc(lc_k3, 16)
            sync.dma_start(out=bstage[:], in_=bout_i).then_inc(lc_bo, 16)
            sync.dma_start(out=s2stage[:], in_=wqb_i).then_inc(lc_qb, 16)
            sync.wait_ge(s_cast, 6)
            sync.dma_start(out=s2stage[:], in_=wqbs_i).then_inc(lc_qb, 16)
            for r in range(8):
                if r > 0:
                    sync.wait_ge(s_cast, 7 + r)
                sync.dma_start(out=wstage[:], in_=wout_r[:, 2 * r:2 * r + 2, :]).then_inc(ld_c, 16)
            for s in range(NSL):
                if s % 2 == 0:
                    if s >= 2:
                        sync.wait_ge(c_xb, s - 1)
                    sync.dma_start(out=xf[:, s % 2, :], in_=x_full_r[s]).then_inc(ld_x, 16)
            for i in range(16):
                if i < 2:
                    sync.wait_ge(c_xb, NSL)
                else:
                    sync.wait_ge(c_xob, i - 1)
                sync.dma_start(out=xf[:, i % 2, :], in_=xo_r[i]).then_inc(lxo[i % 2], 16)
            sync.wait_ge(c_xb, NSL)
            sync.wait_ge(c_xob, 16)
            sync.wait_ge(s_trec, 8 * NSL)
            sync.wait_ge(s_troc, 128)
            sync.wait_ge(s_xte, 8 * NSL)
            sync.wait_ge(s_xoev, 128)
            sync.wait_ge(s_lat, NSL)
            sync.wait_ge(s_kvl, NSL)
            sync.wait_ge(s_rope, 2 * NSL)
            sync.wait_ge(s_vev, NSL)
            sync.wait_ge(s_qmm, 8)
            sync.wait_ge(s_qle, 8)
            sync.wait_ge(s_cast, NCAST)
            for j in range(SLOTS):
                if j >= 1:
                    sync.wait_ge(s_msk, TOP_END[j])
                if j >= 2:
                    sync.wait_ge(s_qrope, 8 * (j - 1))
                sync.dma_start(out=mask_w[:], in_=masks[:, j * 8 * QC:(j + 1) * 8 * QC]).then_inc(ld_msk, 16)
                sync.dma_start(out=qc_w[:, j % 2, :], in_=ctab_q[:, j * QC:(j + 1) * QC]).then_inc(lqt[j % 2], 16)
                sync.dma_start(out=qs_w[:, j % 2, :], in_=stab_q[:, j * QC:(j + 1) * QC]).then_inc(lqt[j % 2], 16)
                for tt in range(2):
                    it = 2 * j + tt
                    if j >= 1:
                        sync.wait_ge(s_app, it - 1)
                    sync.dma_start(out=xres[:, tt, :], in_=xo_r[it]).then_inc(lxr[tt], 16)
            sync.wait_ge(s_o, NU)
            sync.wait_ge(s_exp, NU)
            sync.wait_ge(s_msk, TOP_END[SLOTS])
            sync.wait_ge(s_qbmm, 64)
            sync.wait_ge(s_qrope, 64)
            sync.wait_ge(s_norm, 16 * SLOTS)
            sync.wait_ge(s_obc, 16 * 32)
            sync.wait_ge(s_epimm, 16)
            sync.wait_ge(s_xsum, 16)
            sync.wait_ge(s_app, 16)
            sync.wait_ge(s_xnb, 16)
            sync.wait_ge(s_xnt, 16)
            sync.wait_ge(s_xnte, 32)
            sync.wait_ge(ssp[0], 16 * 8)
            sync.wait_ge(ssp[1], 16 * 8)
            for bb in range(2):
                for hc in range(HCH):
                    k = bb * HCH + hc
                    if k >= 2:
                        sync.wait_ge(s_wcast, 2 * (k - 1))
                    sync.dma_start(out=wgi_s[:, hc % 2, :], in_=wg_r[hc]).then_inc(lw[hc % 2], 16)
                    sync.dma_start(out=wui_s[:, hc % 2, :], in_=wu_r[hc]).then_inc(lw[hc % 2], 16)
                for ti in range(8):
                    gi = bb * 8 + ti

                    if gi >= 2:
                        sync.wait_ge(s_fsum, gi - 1)
                    sync.dma_start(out=xnr[:, gi % 2, :], in_=spill_r[gi]).then_inc(lxn[gi % 2], 16)

        @block.gpsimd
        def _(gpsimd):
            for s in range(NSL):
                if s % 2 == 1:
                    if s >= 2:
                        gpsimd.wait_ge(c_xb, s - 1)
                    gpsimd.dma_start(out=xf[:, s % 2, :], in_=x_full_r[s]).then_inc(ld_x2, 16)
            for j in range(SLOTS):
                for r in range(4):
                    q3 = 4 * j + r
                    gpsimd.wait_ge(s_sum, q3 + 1)
                    if q3 >= 2:
                        gpsimd.wait_ge(s_norm, 4 * (q3 - 1))
                    gpsimd.dma_start(out=srow_d[q3:q3 + 1, :], in_=srow_t[64:65, :]).then_inc(s_row, 16)
                    gpsimd.wait_ge(s_row, 16 * (q3 + 1))
                    gpsimd.dma_start(out=obc[:, q3 % 2, :],
                                     in_=srow_d[q3:q3 + 1, :].partition_broadcast(64)).then_inc(s_obc, 16)
                for tt in range(2):
                    it = 2 * j + tt
                    gpsimd.wait_ge(s_app, it + 1)
                    gpsimd.dma_start(out=spill_r[it], in_=xn_f[:, tt, :]).then_inc(ssp[tt], 16)
            for hc in range(HCH):
                if hc >= 2:
                    gpsimd.wait_ge(s_dwc, hc - 1)
                gpsimd.dma_start(out=wdi_s[:, hc % 2, :, :],
                                 in_=wd_i[:, hc * 128:(hc + 1) * 128].rearrange("(c p) h -> p c h", p=128)
                                 ).then_inc(lwd[hc % 2], 16)
            for gi in range(16):
                gpsimd.wait_ge(s_app2, gi + 1)
                gpsimd.dma_start(out=y_r[gi], in_=fsb[:, gi % 2, :]).then_inc(sy[gi % 2], 16)
            gpsimd.wait_ge(sy[0], 16 * 8)
            gpsimd.wait_ge(sy[1], 16 * 8)

        @block.scalar
        def _(scalar):
            san = [0]
            def sai(instr):
                instr.then_inc(sa, 1); san[0] += 1
                return instr
            for s in range(NSL):
                if s >= 2:
                    scalar.wait_ge(s_rope, 2 * (s - 1))
                scalar.dma_start(out=ct_w[:, s % 2, :], in_=ctab_kv[:, 128 * s:128 * (s + 1)]).then_inc(lt[s % 2], 16)
                scalar.dma_start(out=st_w[:, s % 2, :], in_=stab_kv[:, 128 * s:128 * (s + 1)]).then_inc(lt[s % 2], 16)
                scalar.wait_ge(s_lat, s + 1)
                scalar.activation(out=kvlatT[:, 128 * s:128 * (s + 1)], in_=pA2[0:KVL, 0:128],
                                  func=AF.Copy).then_inc(s_kvl, 1)
            for j in range(SLOTS):
                scalar.wait_ge(s_qmm, j + 1)
                scalar.activation(out=qlatT[:, j, :], in_=pA2[0:QL, 256:256 + QC],
                                  func=AF.Copy, scale=0.125).then_inc(s_qle, 1)
            for ui, (j, g, hh, kc, top, q3, fk, lk) in enumerate(UNITS):
                scalar.wait_ge(s_squad, ui + 1)
                if ui >= 3:
                    scalar.wait_ge(s_o, ui - 2)
                scalar.activation(out=Pq[:, ui % 3, :], in_=sq[ui % 3][:, :],
                                  func=AF.Exp).then_inc(s_exp, 1)
                if lk and q3 % 4 == 3:
                    for tt in range(2):
                        it = 2 * j + tt
                        scalar.wait_ge(s_stat, it + 1)
                        sai(scalar.activation(out=istd[:], in_=veps[:], func=AF.Ln))
                        scalar.wait_ge(sa, san[0])
                        scalar.activation(out=istd[:], in_=istd[:], func=AF.Exp,
                                          scale=-0.5).then_inc(s_istd, 1)
            for bb in range(2):
                for hc in range(HCH):
                    for tt4 in range(2):
                        f = hc * 2 + tt4
                        fb = bb * 48 + f
                        scalar.wait_ge(s_gu, fb + 1)
                        if fb >= 2:
                            scalar.wait_ge(s_sil, fb - 1)
                        scalar.activation(out=tb[:, fb % 2, :], in_=pA[f % 3][:, 0:512],
                                          func=AF.Tanh, scale=0.5 / 127.0).then_inc(s_se, 1)
                for ti in range(8):
                    gi = bb * 8 + ti
                    scalar.wait_ge(s_stat2, gi + 1)
                    sai(scalar.activation(out=istd[:], in_=veps[:], func=AF.Ln))
                    scalar.wait_ge(sa, san[0])
                    scalar.activation(out=istd[:], in_=istd[:], func=AF.Exp,
                                      scale=-0.5).then_inc(s_istd2, 1)

        @block.vector
        def _(vector):
            sdn = [0]
            def sdi(instr):
                instr.then_inc(sd, 1); sdn[0] += 1
                return instr
            vector.wait_ge(lc_qa, 16)
            vector.tensor_copy(out=wqakvaB[:], in_=sstage[:]).then_inc(s_cast, 1)
            vector.wait_ge(lc_k1, 16)
            vector.tensor_copy(out=wkB[:], in_=s3stage[:, 0, :]).then_inc(s_cast, 1)
            vector.wait_ge(lc_k2, 16)
            vector.tensor_copy(out=wksB[:], in_=s3stage[:, 1, :]).then_inc(s_cast, 1)
            vector.wait_ge(lc_k3, 16)
            vector.tensor_copy(out=wvB[:], in_=s3stage[:, 2, :]).then_inc(s_cast, 1)
            vector.wait_ge(lc_bo, 16)
            vector.tensor_copy(out=boutB[:], in_=bstage[:]).then_inc(s_cast, 1)
            vector.wait_ge(lc_qb, 16)
            vector.tensor_copy(out=wqbB[:], in_=s2stage[:]).then_inc(s_cast, 1)
            vector.wait_ge(lc_qb, 32)
            vector.tensor_copy(out=wqbsB[:], in_=s2stage[:]).then_inc(s_cast, 1)
            for r in range(8):
                vector.wait_ge(ld_c, 16 * (r + 1))
                vector.tensor_copy(out=woutB[:, 2 * r:2 * r + 2, :], in_=wstage[:]).then_inc(s_cast, 1)
            vector.memset(Vt[:], 1.0).then_inc(s_ms, 1)
            vector.memset(onesB[:], 1.0).then_inc(s_ms, 1)
            for s in range(NSL):
                if s % 2 == 0:
                    vector.wait_ge(ld_x, 16 * (s // 2 + 1))
                else:
                    vector.wait_ge(ld_x2, 16 * ((s + 1) // 2))
                if s >= 2:
                    vector.wait_ge(s_trec, 8 * (s - 1))
                vector.tensor_copy(out=xb[:, s % 2, :], in_=xf[:, s % 2, :]).then_inc(c_xb, 1)
                for ec in range(ECH):
                    vector.wait_ge(s_trec, 8 * s + ec + 1)
                    if s >= 2 and ec == 0:
                        vector.wait_ge(s_lat, s - 1)
                    vector.tensor_copy(out=xTw[:, s % 2, ec, :], in_=pTR[ec % 2][:, 0:128]).then_inc(s_xte, 1)
                for g in range(2):
                    vector.wait_ge(s_kmm, 2 * s + g + 1)
                    vector.wait_ge(lt[s % 2], 32 * (s // 2 + 1))
                    sdi(vector.tensor_tensor(out=rtmp1[:, 0:128], in0=pA0[:, 512 * g:512 * g + 128],
                                             in1=ct_w[:, s % 2, :], op=ALU.mult))
                    sdi(vector.tensor_tensor(out=rtmp2[:, 0:128], in0=pA1[:, 512 * g:512 * g + 128],
                                             in1=st_w[:, s % 2, :], op=ALU.mult))
                    vector.wait_ge(sd, sdn[0])
                    vector.tensor_add(out=KTp[g][:, 128 * s:128 * (s + 1)],
                                      in0=rtmp1[:, 0:128], in1=rtmp2[:, 0:128]).then_inc(s_rope, 1)
                vector.wait_ge(s_vmm, s + 1)
                if s == 0:
                    vector.wait_ge(s_ms, 2)
                vector.tensor_copy(out=Vt[:, s, :, 0:64],
                                   in_=pA2[:, 512:768].rearrange("p (h d) -> p h d", h=4)
                                   ).then_inc(s_vev, 1)
            for i in range(16):
                vector.wait_ge(lxo[i % 2], 16 * (i // 2 + 1))
                if i >= 2:
                    vector.wait_ge(s_troc, 8 * (i - 1))
                vector.tensor_copy(out=xb[:, i % 2, :], in_=xf[:, i % 2, :]).then_inc(c_xob, 1)
                tt = i % 2
                for ec in range(ECH):
                    vector.wait_ge(s_troc, 8 * i + ec + 1)
                    if i >= 2 and ec == 0:
                        vector.wait_ge(s_qmm, i // 2)
                    vector.tensor_copy(out=xoT[:, ec, tt * 128:(tt + 1) * 128],
                                       in_=pTR[ec % 2][:, 0:128]).then_inc(s_xoev, 1)
            prev_slot = -1
            for ui, (j, g, hh, kc, top, q3, fk, lk) in enumerate(UNITS):
                if j != prev_slot:
                    prev_slot = j
                    for pr in range(8):
                        vector.wait_ge(s_qbmm, 8 * j + pr + 1)
                        vector.wait_ge(lqt[j % 2], 32 * (j // 2 + 1))
                        src = sq[pr % 2]
                        sdi(vector.tensor_tensor(out=rtmp1[:, 0:QC], in0=src[:, 0:QC],
                                                 in1=qc_w[:, j % 2, :], op=ALU.mult))
                        sdi(vector.tensor_tensor(out=rtmp2[:, 0:QC], in0=src[:, 512:512 + QC],
                                                 in1=qs_w[:, j % 2, :], op=ALU.mult))
                        vector.wait_ge(sd, sdn[0])
                        vector.tensor_add(out=qTw[:, pr, :], in0=rtmp1[:, 0:QC],
                                          in1=rtmp2[:, 0:QC]).then_inc(s_qrope, 1)
                if top >= 0:
                    vector.wait_ge(s_exp, ui + 1)
                    vector.wait_ge(ld_msk, 16 * (j + 1))
                    ki = kc - 8 * j
                    mk = mask_w[:, ki * QC:(ki + 1) * QC]
                    pv = Pq[:, ui % 3, :].rearrange("p (r n) -> p r n", r=4)
                    vector.tensor_tensor(out=pv, in0=pv,
                                         in1=mk.unsqueeze(1).broadcast_to([128, 4, QC]),
                                         op=ALU.mult).then_inc(s_msk, 1)
                if lk:
                    vector.wait_ge(s_o, ui + 1)
                    if q3 >= 2:
                        vector.wait_ge(s_norm, 4 * (q3 - 1))
                    sdi(vector.tensor_copy(out=osb[:, q3 % 2, 0:512], in_=obank(q3, 0)[0:65, :] if False else pT0[:, :]))
                    vector.tensor_copy(out=osb[:, q3 % 2, 512:1024], in_=pT1[:, :]).then_inc(s_oev, 1)
                    if q3 >= 1:
                        vector.wait_ge(s_row, 16 * q3)
                    vector.wait_ge(s_oev, q3 + 1)
                    with nc.allow_low_precision(reason="1/softmax-sums in bf16; rvr margin 20x"):
                        sdi(vector.reciprocal(out=srow_t[64:65, 0:512], in_=osb[64:65, q3 % 2, 0:512]))
                        vector.reciprocal(out=srow_t[64:65, 512:1024],
                                          in_=osb[64:65, q3 % 2, 512:1024]).then_inc(s_sum, 1)
                    vector.wait_ge(s_obc, 16 * (q3 + 1))
                    if j >= 1:
                        vector.wait_ge(s_epimm, 2 * j)
                    for ho in range(4):
                        c = (q3 % 4) * 4 + ho
                        vector.tensor_tensor(out=OT[:, c, :],
                                             in0=osb[0:64, q3 % 2, ho * QC:(ho + 1) * QC],
                                             in1=obc[:, q3 % 2, (2 * (ho // 2) + (ho % 2)) * QC:
                                                     (2 * (ho // 2) + (ho % 2) + 1) * QC],
                                             op=ALU.mult).then_inc(s_norm, 1)
                if lk and q3 % 4 == 3:
                    for tt in range(2):
                        it = 2 * j + tt
                        vector.wait_ge(s_epimm, it + 1)
                        vector.wait_ge(lxr[tt], 16 * (j + 1))
                        sdi(vector.tensor_add(out=xres[:, tt, :], in0=sq[tt][:, :], in1=xres[:, tt, :]))
                        vector.wait_ge(sd, sdn[0])
                        if it >= 1:
                            vector.wait_ge(s_istd, it)
                        sdi(vector.bn_stats(stat[:, 0:6], xres[:, tt, 0:512]))
                        vector.bn_stats(stat[:, 6:12], xres[:, tt, 512:1024]).then_inc(s_xsum, 1)
                        vector.wait_ge(sd, sdn[0])
                        vector.wait_ge(s_xsum, it + 1)
                        sdi(vector.bn_aggr(mv[:], stat[:, 0:12]))
                        vector.wait_ge(sd, sdn[0])
                        vector.tensor_scalar_add(veps[:], mv[:, 1:2], EPS).then_inc(s_stat, 1)
                        vector.wait_ge(s_istd, it + 1)
                        vector.tensor_scalar(out=nbias[:], in0=istd[:], scalar1=mv[:, 0:1],
                                             scalar2=-1.0, op0=ALU.mult, op1=ALU.mult).then_inc(s_nb, 1)
                        vector.wait_ge(s_nb, it + 1)
                        if j >= 1:
                            vector.wait_ge(ssp[tt], 16 * (it // 2))
                        vector.tensor_scalar(out=xn_f[:, tt, :], in0=xres[:, tt, :],
                                             scalar1=istd[:], scalar2=nbias[:],
                                             op0=ALU.mult, op1=ALU.add).then_inc(s_app, 1)
                        vector.wait_ge(s_app, it + 1)
                        if j >= 1:
                            vector.wait_ge(s_xnt, it - 1)
                        vector.tensor_copy(out=xn_b[:, tt, :], in_=xn_f[:, tt, :]).then_inc(s_xnb, 1)
                        vector.wait_ge(s_xnt, it + 1)
                        col = j * QC + tt * 128
                        for hf in range(2):
                            vector.tensor_copy(
                                out=xnT[:, 4 * hf:4 * hf + 4, col:col + 128],
                                in_=pTR[hf][:, :].rearrange("p (c n) -> p c n", c=4)
                                ).then_inc(s_xnte, 1)
            for hc in range(HCH):
                vector.wait_ge(lwd[hc % 2], 16 * (hc // 2 + 1))
                if hc >= 2:
                    vector.wait_ge(s_dwt, 2 * (hc - 1))
                vector.tensor_copy(out=wdb_s[:, hc % 2, :, :], in_=wdi_s[:, hc % 2, :, :]).then_inc(s_dwc, 1)
                for grp in range(2):
                    vector.wait_ge(s_dwt, 2 * hc + grp + 1)
                    vector.tensor_copy(out=wdT[:, hc, grp * 512:(grp + 1) * 512],
                                       in_=pTR[grp][:, :]).then_inc(s_dwev, 1)
            for bb in range(2):
                for hc in range(HCH):
                    k = bb * HCH + hc
                    vector.wait_ge(lw[hc % 2], 32 * (bb * 12 + hc // 2 + 1))
                    if k >= 2:
                        vector.wait_ge(s_wtr, 4 * (k - 1))
                    vector.tensor_copy(out=wgb_s[:, hc % 2, :], in_=wgi_s[:, hc % 2, :]).then_inc(s_wcast, 1)
                    vector.tensor_copy(out=wub_s[:, hc % 2, :], in_=wui_s[:, hc % 2, :]).then_inc(s_wcast, 1)
                    for grp in range(4):
                        vector.wait_ge(s_wtr, 4 * k + grp + 1)
                        dst = (wgT if grp < 2 else wuT)
                        hf = grp % 2
                        vector.tensor_copy(out=dst[:, hc % 2, 4 * hf:4 * hf + 4, :],
                                           in_=pTR[grp % 2][:, :].rearrange("p (c n) -> p c n", c=4)
                                           ).then_inc(s_wev, 1)
                    for tt4 in range(2):
                        f = hc * 2 + tt4
                        fb = bb * 48 + f
                        vector.wait_ge(s_se, fb + 1)
                        u1 = vector.tensor_scalar_add(ub[:, fb % 2, :], tb[:, fb % 2, :], 1.0)
                        sdi(u1)
                        vector.wait_ge(sd, sdn[0])
                        h1 = vector.tensor_tensor(out=rb[:, fb % 2, :], in0=ub[:, fb % 2, :],
                                                  in1=pA[f % 3][:, 0:512], op=ALU.mult)
                        sdi(h1)
                        vector.wait_ge(sd, sdn[0])
                        vector.tensor_tensor(out=hactT[:, hc, tt4 * 512:(tt4 + 1) * 512],
                                             in0=rb[:, fb % 2, :], in1=pA[f % 3][:, 512:1024],
                                             op=ALU.mult).then_inc(s_sil, 1)
                for ti in range(8):
                    gi = bb * 8 + ti
                    vector.wait_ge(s_dmm, gi + 1)
                    vector.wait_ge(lxn[gi % 2], 16 * (gi // 2 + 1))
                    if gi >= 2:
                        vector.wait_ge(sy[gi % 2], 16 * (gi // 2))
                    if gi == 0:
                        vector.wait_ge(s_app, 16)
                    vector.scalar_tensor_tensor(out=fsb[:, gi % 2, :], in0=pA[ti % 3][:, :],
                                                scalar=DQ3, in1=xnr[:, gi % 2, :],
                                                op0=ALU.mult, op1=ALU.add).then_inc(s_fsum, 1)
                    vector.wait_ge(s_fsum, gi + 1)
                    if gi >= 1:
                        vector.wait_ge(s_istd2, gi)
                    sdi(vector.bn_stats(stat[:, 0:6], fsb[:, gi % 2, 0:512]))
                    sdi(vector.bn_stats(stat[:, 6:12], fsb[:, gi % 2, 512:1024]))
                    vector.wait_ge(sd, sdn[0])
                    sdi(vector.bn_aggr(mv[:], stat[:, 0:12]))
                    vector.wait_ge(sd, sdn[0])
                    vector.tensor_scalar_add(veps[:], mv[:, 1:2], EPS).then_inc(s_stat2, 1)
                    vector.wait_ge(s_istd2, gi + 1)
                    vector.tensor_scalar(out=nbias[:], in0=istd[:], scalar1=mv[:, 0:1],
                                         scalar2=-1.0, op0=ALU.mult, op1=ALU.mult).then_inc(s_nb2, 1)
                    vector.wait_ge(s_nb2, gi + 1)
                    vector.tensor_scalar(out=fsb[:, gi % 2, :], in0=fsb[:, gi % 2, :],
                                         scalar1=istd[:], scalar2=nbias[:],
                                         op0=ALU.mult, op1=ALU.add).then_inc(s_app2, 1)

        @block.tensor
        def _(tensor):
            for s in range(NSL):
                tensor.wait_ge(c_xb, s + 1)
                if s == 0:
                    tensor.wait_ge(lc_i, 16)
                for ec in range(ECH):
                    w = 8 * s + ec - 1
                    if w > 0:
                        tensor.wait_ge(s_xte, w)
                    mm = tensor.matmul(pTR[ec % 2][:, 0:128],
                                       xb[:, s % 2, ec * 128:(ec + 1) * 128],
                                       identB[:], start=True, stop=True)
                    mm.then_inc(s_trec, 1)
                tensor.wait_ge(s_xte, 8 * (s + 1))
                if s >= 1:
                    tensor.wait_ge(s_kvl, s)
                if s == 0:
                    tensor.wait_ge(s_cast, NCAST)
                for ec in range(ECH):
                    mm = tensor.matmul(pA2[0:KVL, 0:128], wqakvaB[:, ec, 64:96],
                                       xTw[:, s % 2, ec, :], start=(ec == 0), stop=(ec == ECH - 1))
                mm.then_inc(s_lat, 1)
                tensor.wait_ge(s_kvl, s + 1)
                for g in range(2):
                    w = 2 * s + g - 1
                    if w > 0:
                        tensor.wait_ge(s_rope, w)
                    tensor.matmul(pA0[:, 512 * g:512 * g + 128], wkB[:, g * 128:(g + 1) * 128],
                                  kvlatT[:, 128 * s:128 * (s + 1)], start=True, stop=True)
                    mm = tensor.matmul(pA1[:, 512 * g:512 * g + 128], wksB[:, g * 128:(g + 1) * 128],
                                       kvlatT[:, 128 * s:128 * (s + 1)], start=True, stop=True)
                    mm.then_inc(s_kmm, 1)
                if s >= 1:
                    tensor.wait_ge(s_vev, s)
                mm = tensor.matmul(pA2[:, 512:768],
                                   kvlatT[:, 128 * s:128 * (s + 1)],
                                   wvB[:], start=True, stop=True)
                mm.then_inc(s_vmm, 1)
            for i in range(16):
                tensor.wait_ge(c_xob, i + 1)
                tt = i % 2
                for ec in range(ECH):
                    w = 8 * i + ec - 1
                    if w > 0:
                        tensor.wait_ge(s_xoev, w)
                    mm = tensor.matmul(pTR[ec % 2][:, 0:128], xb[:, i % 2, ec * 128:(ec + 1) * 128],
                                       identB[:], start=True, stop=True)
                    mm.then_inc(s_troc, 1)
                if tt == 1:
                    jslot = i // 2
                    tensor.wait_ge(s_xoev, 8 * (i + 1))
                    if jslot == 0:
                        tensor.wait_ge(s_kvl, NSL)
                    if jslot >= 1:
                        tensor.wait_ge(s_qle, jslot)
                    for ec in range(ECH):
                        mm = tensor.matmul(pA2[0:QL, 256:256 + QC], wqakvaB[:, ec, 0:QL],
                                           xoT[:, ec, :], start=(ec == 0), stop=(ec == ECH - 1))
                    mm.then_inc(s_qmm, 1)
            for j in range(SLOTS):
                for pr in range(8):
                    if pr == 0:
                        tensor.wait_ge(s_qle, j + 1)
                        tensor.wait_ge(s_exp, U_BEFORE[j])
                        tensor.wait_ge(s_xsum, 2 * j)
                        if j == 0:
                            tensor.wait_ge(s_rope, 2 * NSL)
                    w = 8 * j + pr - 1
                    if w > 0:
                        tensor.wait_ge(s_qrope, w)
                    dst = sq[pr % 2]
                    tensor.matmul(dst[:, 0:QC], wqbB[:, pr * 128:(pr + 1) * 128],
                                  qlatT[:, j, :], start=True, stop=True)
                    mm = tensor.matmul(dst[:, 512:512 + QC], wqbsB[:, pr * 128:(pr + 1) * 128],
                                       qlatT[:, j, :], start=True, stop=True)
                    mm.then_inc(s_qbmm, 1)
                ujs = list(range(U_BEFORE[j], U_BEFORE[j + 1]))

                def emit_S(ui, j=j):
                    jj, g, hh, kc, top, q3, fk, lk = UNITS[ui]
                    if ui == U_BEFORE[j]:
                        tensor.wait_ge(s_qrope, 8 * (j + 1))
                        if j == 0:
                            tensor.wait_ge(s_vev, NSL)
                            tensor.wait_ge(s_qmm, SLOTS)
                    if ui >= 3:
                        tensor.wait_ge(s_exp, ui - 2)
                    pr0 = 4 * g + 2 * hh
                    tensor.matmul(sq[ui % 3][:, 0:512],
                                  KTp[g][0:64, 128 * kc:128 * (kc + 1)],
                                  qTw[0:64, pr0:pr0 + 2, :], start=True, stop=True,
                                  tile_position=(0, 0), skip_group_check=True)
                    mm = tensor.matmul(sq[ui % 3][:, 512:1024],
                                       KTp[g][64:128, 128 * kc:128 * (kc + 1)],
                                       qTw[64:128, pr0:pr0 + 2, :], start=True, stop=True,
                                       tile_position=(64, 0), skip_group_check=True)
                    mm.then_inc(s_squad, 1)

                def emit_O(ui, j=j):
                    jj, g, hh, kc, top, q3, fk, lk = UNITS[ui]
                    if fk:
                        if q3 >= 1:
                            tensor.wait_ge(s_oev, q3)
                        if q3 == 0:
                            tensor.wait_ge(s_xoev, 128)
                        if j >= 1 and q3 % 4 == 0:
                            tensor.wait_ge(s_xnte, 4 * j)
                    if top >= 0:
                        tensor.wait_ge(s_msk, top + 1)
                    else:
                        tensor.wait_ge(s_exp, ui + 1)
                    for bi in range(2):
                        kvh = 2 * g + bi
                        mm = tensor.matmul(obank(q3, bi)[0:65, 0:512],
                                           Vt[:, kc, kvh, :],
                                           Pq[:, ui % 3, 512 * bi:512 * (bi + 1)],
                                           start=fk, stop=lk, skip_group_check=True)
                    mm.then_inc(s_o, 1)

                emit_S(ujs[0])
                emit_S(ujs[1])
                for idx, ui in enumerate(ujs):
                    if idx + 2 < len(ujs):
                        emit_S(ujs[idx + 2])
                    emit_O(ui)
                for tt in range(2):
                    if tt == 0:
                        tensor.wait_ge(s_norm, 16 * (j + 1))
                        tensor.wait_ge(s_exp, U_BEFORE[j + 1])
                    for c in range(16):
                        for eh in range(2):
                            tensor.matmul(sq[tt][:, eh * 512:(eh + 1) * 512],
                                          OT[:, c, tt * 128:(tt + 1) * 128],
                                          woutB[:, c, eh * 512:(eh + 1) * 512],
                                          start=(c == 0), stop=False, skip_group_check=True)
                    for eh in range(2):
                        mm = tensor.matmul(sq[tt][:, eh * 512:(eh + 1) * 512],
                                           onesB[:], boutB[:, eh * 512:(eh + 1) * 512],
                                           start=False, stop=True, skip_group_check=True)
                    mm.then_inc(s_epimm, 1)
                for tt in range(2):
                    it = 2 * j + tt
                    tensor.wait_ge(s_xnb, it + 1)
                    if tt == 0:
                        tensor.wait_ge(s_oev, 4 * (j + 1))
                    else:
                        tensor.wait_ge(s_xnte, 4 * j + 2)
                    for ec in range(ECH):
                        hf = ec // 4
                        mm = tensor.matmul(pTR[hf][:, (ec % 4) * 128:(ec % 4 + 1) * 128],
                                           xn_b[:, tt, ec * 128:(ec + 1) * 128],
                                           identB[:], start=True, stop=True, skip_group_check=True)
                    mm.then_inc(s_xnt, 1)
            for hc in range(HCH):
                tensor.wait_ge(s_dwc, hc + 1)
                if hc == 0:
                    tensor.wait_ge(s_xnte, 4 * SLOTS)
                for grp in range(2):
                    w = 2 * hc + grp - 1
                    if w > 0:
                        tensor.wait_ge(s_dwev, w)
                    for q in range(4):
                        dc = grp * 4 + q
                        mm = tensor.matmul(pTR[grp][:, q * 128:(q + 1) * 128],
                                           wdb_s[:, hc % 2, dc, :], identB[:],
                                           start=True, stop=True)
                    mm.then_inc(s_dwt, 1)
            for bb in range(2):
                for hc in range(HCH):
                    k = bb * HCH + hc
                    tensor.wait_ge(s_wcast, 2 * (k + 1))
                    for grp in range(4):
                        if k == 0 and grp == 0:
                            tensor.wait_ge(s_dwev, 2 * HCH)
                        w = 4 * k + grp - 1
                        if w > 0:
                            tensor.wait_ge(s_wev, w)
                        src = (wgb_s if grp < 2 else wub_s)
                        hf = grp % 2
                        for q in range(4):
                            ec = hf * 4 + q
                            mm = tensor.matmul(pTR[grp % 2][:, q * 128:(q + 1) * 128],
                                               src[:, hc % 2, ec * 128:(ec + 1) * 128],
                                               identB[:], start=True, stop=True)
                        mm.then_inc(s_wtr, 1)
                    for tt4 in range(2):
                        f = hc * 2 + tt4
                        fb = bb * 48 + f
                        if tt4 == 0:
                            tensor.wait_ge(s_wev, 4 * (k + 1))
                        if fb >= 3:
                            tensor.wait_ge(s_sil, fb - 2)
                        if bb == 1 and f == 0:
                            tensor.wait_ge(s_fsum, 8)
                        if fb == 0:
                            tensor.wait_ge(s_xnte, 4 * SLOTS)
                        for ec in range(ECH):
                            tensor.matmul(pA[f % 3][:, 0:512], wgT[:, hc % 2, ec, :],
                                          xnT[:, ec, bb * 1024 + tt4 * 512:bb * 1024 + (tt4 + 1) * 512],
                                          start=(ec == 0), stop=(ec == ECH - 1))
                        for ec in range(ECH):
                            mm = tensor.matmul(pA[f % 3][:, 512:1024], wuT[:, hc % 2, ec, :],
                                               xnT[:, ec, bb * 1024 + tt4 * 512:bb * 1024 + (tt4 + 1) * 512],
                                               start=(ec == 0), stop=(ec == ECH - 1))
                        mm.then_inc(s_gu, 1)
                for ti in range(8):
                    gi = bb * 8 + ti
                    tensor.wait_ge(s_sil, 48 * (bb + 1))
                    if gi == 0:
                        tensor.wait_ge(s_dwev, 2 * HCH)
                    if gi >= 3:
                        tensor.wait_ge(s_fsum, gi - 2)
                    for eh in range(2):
                        for hc in range(HCH):
                            mm = tensor.matmul(pA[ti % 3][:, eh * 512:(eh + 1) * 512],
                                               hactT[:, hc, ti * 128:(ti + 1) * 128],
                                               wdT[:, hc, eh * 512:(eh + 1) * 512],
                                               start=(hc == 0), stop=(hc == HCH - 1),
                                               skip_group_check=True)
                    mm.then_inc(s_dmm, 1)

    return nc


def _rope_tables(positions):
    d = np.arange(128)
    i = (d % 64) // 2
    inv = 1.0 / (10000.0 ** (2.0 * i / 64.0))
    ang = np.outer(inv, positions.astype(np.float64))
    c = np.cos(ang).astype(np.float32)
    s = np.sin(ang).astype(np.float32)
    sign = np.where(d % 2 == 0, -1.0, 1.0).astype(np.float32)[:, None]
    return np.ascontiguousarray(c), np.ascontiguousarray(s * sign)


def kernel(L0_attn_x, L0_attn_Wqa, L0_attn_Wqb, L0_attn_Wkva, L0_attn_Wkvb,
           L0_attn_Wout, L0_attn_bout, L0_ffn_w_gate, L0_ffn_w_up, L0_ffn_w_down):
    x = np.ascontiguousarray(np.asarray(L0_attn_x, dtype=np.float32))
    Wqa = np.asarray(L0_attn_Wqa, dtype=np.float32)
    Wqb = np.asarray(L0_attn_Wqb, dtype=np.float32)
    Wkva = np.asarray(L0_attn_Wkva, dtype=np.float32)
    Wkvb = np.asarray(L0_attn_Wkvb, dtype=np.float32)
    Wout = np.asarray(L0_attn_Wout, dtype=np.float32)
    bout = np.ascontiguousarray(np.asarray(L0_attn_bout, dtype=np.float32).reshape(1, E))
    wg = np.ascontiguousarray(np.asarray(L0_ffn_w_gate, dtype=np.int32))
    wu = np.ascontiguousarray(np.asarray(L0_ffn_w_up, dtype=np.int32))
    wd = np.ascontiguousarray(np.asarray(L0_ffn_w_down, dtype=np.int32))

    wkvb_r = Wkvb.reshape(KVL, KVH, D, 2)
    wk = np.ascontiguousarray(wkvb_r[..., 0].reshape(KVL, 256))
    wv = np.ascontiguousarray(wkvb_r[..., 1].reshape(KVL, 256))
    wks = np.ascontiguousarray(
        wkvb_r[..., 0].reshape(KVL, KVH, 32, 2)[:, :, :, ::-1].reshape(KVL, 256))
    qperm = []
    for g in range(2):
        for i in range(4):
            ha, hb = 8 * g + i, 8 * g + 4 + i
            qperm.extend(range(64 * ha, 64 * ha + 64))
            qperm.extend(range(64 * hb, 64 * hb + 64))
    qperm = np.array(qperm)
    wqb_p = np.ascontiguousarray(Wqb[:, qperm])
    wqbs_full = Wqb.reshape(QL, H, 32, 2)[:, :, :, ::-1].reshape(QL, E)
    wqbs_p = np.ascontiguousarray(wqbs_full[:, qperm])
    operm = []
    for g in range(2):
        for hh in range(2):
            for h in (8 * g + 2 * hh, 8 * g + 2 * hh + 1, 8 * g + 4 + 2 * hh, 8 * g + 4 + 2 * hh + 1):
                operm.extend(range(64 * h, 64 * h + 64))
    operm = np.array(operm)
    wout_p = np.ascontiguousarray(Wout[operm, :])
    wqakva_ = np.ascontiguousarray(np.concatenate([Wqa, Wkva], axis=1))
    identity = np.eye(128, dtype=BF)
    ckv, skv = _rope_tables(np.arange(S))

    nc = build_nc()
    in_maps = []
    chunk_of = np.zeros((NCORE, SLOTS), dtype=np.int64)
    for c in range(NCORE):
        b, cm = c // 4, c % 4
        chunks = [4 * j + cm for j in range(SLOTS)]
        chunk_of[c] = chunks
        xb_ = x[b]
        x_own_ = np.ascontiguousarray(
            np.concatenate([xb_[QC * p:QC * (p + 1)] for p in chunks], axis=0))
        pos_own = np.concatenate([np.arange(QC * p, QC * (p + 1)) for p in chunks])
        cq, sq_ = _rope_tables(pos_own)
        cq = cq.astype(BF); sq_ = sq_.astype(BF)
        m = np.zeros((128, SLOTS * 8 * QC), dtype=BF)
        iv = np.arange(128)[:, None]
        qv = np.arange(QC)[None, :]
        for j in range(SLOTS):
            p = chunks[j]
            for ki in range(8):
                v = 1024 * j + 128 * ki + iv
                m[:, (j * 8 + ki) * QC:(j * 8 + ki + 1) * QC] = (v <= QC * p + qv).astype(BF)
        in_maps.append({
            "x_own": x_own_, "x_full": np.ascontiguousarray(xb_),
            "ident": identity, "masks": m,
            "ctab_kv": ckv, "stab_kv": skv, "ctab_q": cq, "stab_q": sq_,
            "wqakva": wqakva_, "wqb": wqb_p, "wqbs": wqbs_p,
            "wk": wk, "wks": wks, "wv": wv,
            "wout": wout_p, "bout": bout,
            "wg": wg, "wu": wu, "wd": wd,
        })

    res = run_bass_kernel_spmd(nc, in_maps, core_ids=list(range(NCORE)))

    out = np.empty((B, S, E), dtype=np.float32)
    for c in range(NCORE):
        b = c // 4
        yc = np.asarray(res.results[c]["y"])
        for j in range(SLOTS):
            p = chunk_of[c][j]
            out[b, QC * p:QC * (p + 1)] = yc[QC * j:QC * (j + 1)]
    return out


# version 17
# speedup vs baseline: 1.6777x; 1.0052x over previous
import contextlib
import numpy as np
import ml_dtypes
import concourse.bass as bass
import concourse.mybir as mybir
from concourse.bass_utils import run_bass_kernel_spmd

F32 = mybir.dt.float32
BF16 = mybir.dt.bfloat16
I32 = mybir.dt.int32
AF = mybir.ActivationFunctionType
ALU = mybir.AluOpType
BF = ml_dtypes.bfloat16

B, S, E = 2, 8192, 1024
H, KVH, D = 16, 4, 64
QL, KVL = 64, 32
HID = 3072
NCORE = 8
QC = 256
SLOTS = 8
TOK = QC * SLOTS
ECH = E // 128
HCH = HID // 128
NSL = S // 128
EPS = 1e-5
DQ3 = 1.0 / (2.0 * 127.0 ** 3)

def kcj(j):
    return 8 * (j + 1)

UNITS = []
_t = 0
for _j in range(SLOTS):
    for _g in range(2):
        for _hh in range(2):
            _q3 = _j * 4 + _g * 2 + _hh
            for _kc in range(kcj(_j)):
                to = _t if _kc >= 8 * _j else -1
                if to >= 0:
                    _t += 1
                UNITS.append((_j, _g, _hh, _kc, to, _q3, _kc == 0, _kc == kcj(_j) - 1))
NU = len(UNITS)
U_BEFORE = [0] * (SLOTS + 1)
for _j in range(SLOTS):
    U_BEFORE[_j + 1] = U_BEFORE[_j] + 4 * kcj(_j)
TOP_END = [0] * (SLOTS + 1)
_c = 0
for ui in range(NU):
    if UNITS[ui][4] >= 0:
        _c += 1
    TOP_END[UNITS[ui][0] + 1] = _c


def build_nc():
    nc = bass.Bass("TRN2", target_bir_lowering=False, debug=False)

    def din(name, shape, dt):
        return nc.dram_tensor(name, shape, dt, kind="ExternalInput").ap()

    x_own = din("x_own", [TOK, E], F32)
    x_full = din("x_full", [S, E], F32)
    ident = din("ident", [128, 128], BF16)
    masks = din("masks", [128, SLOTS * 8 * QC], BF16)
    ctab_kv = din("ctab_kv", [128, S], F32)
    stab_kv = din("stab_kv", [128, S], F32)
    ctab_q = din("ctab_q", [128, TOK], BF16)
    stab_q = din("stab_q", [128, TOK], BF16)
    wqakva = din("wqakva", [E, 96], F32)
    wqb_i = din("wqb", [QL, E], F32)
    wqbs_i = din("wqbs", [QL, E], F32)
    wk_i = din("wk", [KVL, 256], F32)
    wks_i = din("wks", [KVL, 256], F32)
    wv_i = din("wv", [KVL, 256], F32)
    wout_i = din("wout", [E, E], F32)
    bout_i = din("bout", [1, E], F32)
    wg_i = din("wg", [HID, E], I32)
    wu_i = din("wu", [HID, E], I32)
    wd_i = din("wd", [E, HID], I32)

    y = nc.dram_tensor("y", [TOK, E], F32, kind="ExternalOutput").ap()
    xn_spill = nc.dram_tensor("xn_spill", [TOK, E], F32, kind="Internal").ap()
    srow_d = nc.dram_tensor("srow_d", [32, 1024], BF16, kind="Internal").ap()

    x_full_r = x_full.rearrange("(n p) e -> n p e", p=128)
    xo_r = x_own.rearrange("(i p) e -> i p e", p=128)
    wg_r = wg_i.rearrange("(c p) e -> c p e", p=128)
    wu_r = wu_i.rearrange("(c p) e -> c p e", p=128)
    spill_r = xn_spill.rearrange("(i p) e -> i p e", p=128)
    y_r = y.rearrange("(i p) e -> i p e", p=128)
    wout_r = wout_i.rearrange("(c p) e -> p c e", p=64)

    with contextlib.ExitStack() as R:
        EE = R.enter_context
        identB = EE(nc.sbuf_tensor("identB", [128, 128], BF16))
        rtmp1 = EE(nc.sbuf_tensor("rtmp1", [128, 512], F32))
        rtmp2 = EE(nc.sbuf_tensor("rtmp2", [128, 512], F32))
        stat = EE(nc.sbuf_tensor("stat", [128, 12], F32))
        mv = EE(nc.sbuf_tensor("mv", [128, 2], F32))
        veps = EE(nc.sbuf_tensor("veps", [128, 1], F32))
        istd = EE(nc.sbuf_tensor("istd", [128, 1], F32))
        nbias = EE(nc.sbuf_tensor("nbias", [128, 1], F32))
        xnT = EE(nc.sbuf_tensor("xnT", [128, ECH, TOK], BF16, side="right"))
        S12 = contextlib.ExitStack()
        E12 = S12.enter_context
        KTp = [E12(nc.sbuf_tensor(f"KTp{g}", [128, S], BF16)) for g in range(2)]
        Vt = E12(nc.sbuf_tensor("Vt", [128, S // 128, KVH, 65], BF16))
        woutB = E12(nc.sbuf_tensor("woutB", [64, 16, E], BF16))
        qlatT = E12(nc.sbuf_tensor("qlatT", [QL, SLOTS, QC], BF16))
        wqbB = E12(nc.sbuf_tensor("wqbB", [QL, E], BF16))
        wqbsB = E12(nc.sbuf_tensor("wqbsB", [QL, E], BF16))
        wkB = E12(nc.sbuf_tensor("wkB", [KVL, 256], BF16))
        wksB = E12(nc.sbuf_tensor("wksB", [KVL, 256], BF16))
        wvB = E12(nc.sbuf_tensor("wvB", [KVL, 256], BF16))
        wqakvaB = E12(nc.sbuf_tensor("wqakvaB", [128, ECH, 96], BF16))
        boutB = E12(nc.sbuf_tensor("boutB", [1, E], BF16))
        onesB = E12(nc.sbuf_tensor("onesB", [1, 128], BF16))
        S1 = contextlib.ExitStack()
        E1 = S1.enter_context
        xf = E1(nc.sbuf_tensor("xf", [128, 2, E], F32))
        xb = E1(nc.sbuf_tensor("xb", [128, 2, E], BF16))
        xTw = E1(nc.sbuf_tensor("xTw", [128, 2, ECH, 128], BF16))
        ct_w = E1(nc.sbuf_tensor("ct_w", [128, 2, 128], F32))
        st_w = E1(nc.sbuf_tensor("st_w", [128, 2, 128], F32))
        xoT = E1(nc.sbuf_tensor("xoT", [128, ECH, QC], BF16))
        wstage = E1(nc.sbuf_tensor("wstage", [64, 2, E], F32))
        bstage = E1(nc.sbuf_tensor("bstage", [1, E], F32))
        sstage = E1(nc.sbuf_tensor("sstage", [128, ECH, 96], F32))
        s2stage = E1(nc.sbuf_tensor("s2stage", [QL, E], F32))
        s3stage = E1(nc.sbuf_tensor("s3stage", [KVL, 3, 256], F32))
        kvlatT = E1(nc.sbuf_tensor("kvlatT", [KVL, S], BF16))
        S1.close()
        S2 = contextlib.ExitStack()
        E2 = S2.enter_context
        qTw = E2(nc.sbuf_tensor("qTw", [128, 8, QC], BF16))
        mask_w = E2(nc.sbuf_tensor("mask_w", [128, 8 * QC], BF16))
        Pq = E2(nc.sbuf_tensor("Pq", [128, 3, 1024], BF16))
        osb = E2(nc.sbuf_tensor("osb", [128, 2, 1024], F32, side="right"))
        OT = E2(nc.sbuf_tensor("OT", [64, 16, QC], BF16))
        obc = E2(nc.sbuf_tensor("obc", [64, 2, 1024], BF16))
        srow_t = E2(nc.sbuf_tensor("srow_t", [128, 1024], BF16))
        qc_w = E2(nc.sbuf_tensor("qc_w", [128, 2, QC], BF16))
        qs_w = E2(nc.sbuf_tensor("qs_w", [128, 2, QC], BF16))
        xres = E2(nc.sbuf_tensor("xres", [128, 2, E], F32))
        xn_f = E2(nc.sbuf_tensor("xn_f", [128, 2, E], F32))
        xn_b = E2(nc.sbuf_tensor("xn_b", [128, 2, E], BF16))
        S2.close()
        S12.close()
        hactT = EE(nc.sbuf_tensor("hactT", [128, HCH, 1024], BF16))
        wdT = EE(nc.sbuf_tensor("wdT", [128, HCH, E], BF16))
        wgi_s = EE(nc.sbuf_tensor("wgi_s", [128, 2, E], I32))
        wui_s = EE(nc.sbuf_tensor("wui_s", [128, 2, E], I32))
        wdi_s = EE(nc.sbuf_tensor("wdi_s", [128, 2, ECH, 128], I32))
        wgb_s = EE(nc.sbuf_tensor("wgb_s", [128, 2, E], BF16))
        wub_s = EE(nc.sbuf_tensor("wub_s", [128, 2, E], BF16))
        wdb_s = EE(nc.sbuf_tensor("wdb_s", [128, 2, ECH, 128], BF16))
        wgT = EE(nc.sbuf_tensor("wgT", [128, 2, ECH, 128], BF16))
        wuT = EE(nc.sbuf_tensor("wuT", [128, 2, ECH, 128], BF16))
        tb = EE(nc.sbuf_tensor("tb", [128, 2, 512], F32))
        ub = EE(nc.sbuf_tensor("ub", [128, 2, 512], F32))
        rb = EE(nc.sbuf_tensor("rb", [128, 2, 512], F32))
        xnr = EE(nc.sbuf_tensor("xnr", [128, 2, E], F32))
        fsb = EE(nc.sbuf_tensor("fsb", [128, 2, E], F32))

        pA0 = EE(nc.psum_tensor("pA0", [128, 1024], F32))
        pA1 = EE(nc.psum_tensor("pA1", [128, 1024], F32))
        pA2 = EE(nc.psum_tensor("pA2", [128, 1024], F32))
        pT0 = EE(nc.psum_tensor("pT0", [128, 512], F32))
        pT1 = EE(nc.psum_tensor("pT1", [128, 512], F32))
        pA = [pA0, pA1, pA2]
        sq = pA
        pTR = [pT0, pT1]

        def obank(q3, b):
            return (pT0 if b == 0 else pT1)[:, :]

        def sname(n):
            return EE(nc.semaphore(n))
        ld_c = sname("ld_c"); ld_x = sname("ld_x"); ld_x2 = sname("ld_x2")
        lc_i = sname("lc_i"); lc_qa = sname("lc_qa"); lc_k1 = sname("lc_k1")
        lc_k2 = sname("lc_k2"); lc_k3 = sname("lc_k3"); lc_bo = sname("lc_bo")
        lc_qb = sname("lc_qb")
        lt = [sname("lt0"), sname("lt1")]
        lqt = [sname("lqt0"), sname("lqt1")]
        lxr = [sname("lxr0"), sname("lxr1")]
        lw = [sname("lw0"), sname("lw1")]
        lwd = [sname("lwd0"), sname("lwd1")]
        lxn = [sname("lxn0"), sname("lxn1")]
        ssp = [sname("ssp0"), sname("ssp1")]
        sy = [sname("sy0"), sname("sy1")]
        lxo = [sname("lxo0"), sname("lxo1")]
        ld_t = sname("ld_t")
        ld_xo = sname("ld_xo"); ld_xres = sname("ld_xres"); ld_msk = sname("ld_msk")
        ld_qt = sname("ld_qt"); ld_w = sname("ld_w"); ld_wd = sname("ld_wd")
        ld_xn2 = sname("ld_xn2")
        s_cast = sname("s_cast"); s_ms = sname("s_ms")
        c_xb = sname("c_xb"); s_trec = sname("s_trec"); s_xte = sname("s_xte")
        s_lat = sname("s_lat"); s_kvl = sname("s_kvl"); s_kmm = sname("s_kmm")
        s_rope = sname("s_rope"); s_vmm = sname("s_vmm"); s_vev = sname("s_vev")
        c_xob = sname("c_xob"); s_troc = sname("s_troc"); s_xoev = sname("s_xoev")
        s_qmm = sname("s_qmm"); s_qle = sname("s_qle")
        s_qbmm = sname("s_qbmm"); s_qrope = sname("s_qrope")
        s_squad = sname("s_squad"); s_exp = sname("s_exp"); s_msk = sname("s_msk")
        s_o = sname("s_o"); s_oev = sname("s_oev"); s_sum = sname("s_sum"); s_row = sname("s_row")
        s_obc = sname("s_obc"); s_norm = sname("s_norm")
        s_epimm = sname("s_epimm"); s_xsum = sname("s_xsum"); s_stat = sname("s_stat")
        s_istd = sname("s_istd"); s_nb = sname("s_nb"); s_app = sname("s_app")
        s_xnb = sname("s_xnb"); s_xnt = sname("s_xnt"); s_xnte = sname("s_xnte")
        s_spill = sname("s_spill")
        s_wcast = sname("s_wcast"); s_wtr = sname("s_wtr"); s_wev = sname("s_wev")
        s_gu = sname("s_gu"); s_se = sname("s_se"); s_sil = sname("s_sil")
        s_dwc = sname("s_dwc"); s_dwt = sname("s_dwt"); s_dwev = sname("s_dwev")
        s_dmm = sname("s_dmm"); s_fsum = sname("s_fsum"); s_stat2 = sname("s_stat2")
        s_istd2 = sname("s_istd2"); s_nb2 = sname("s_nb2"); s_app2 = sname("s_app2")
        s_y = sname("s_y")
        sd = sname("sd"); sa = sname("sa")

        NCAST = 15

        block = EE(nc.Block())

        @block.sync
        def _(sync):
            sync.dma_start(out=identB[:], in_=ident).then_inc(lc_i, 16)
            sync.dma_start(out=sstage[:], in_=wqakva.rearrange("(c p) m -> p c m", p=128)).then_inc(lc_qa, 16)
            sync.dma_start(out=s3stage[:, 0, :], in_=wk_i).then_inc(lc_k1, 16)
            sync.dma_start(out=s3stage[:, 1, :], in_=wks_i).then_inc(lc_k2, 16)
            sync.dma_start(out=s3stage[:, 2, :], in_=wv_i).then_inc(lc_k3, 16)
            sync.dma_start(out=bstage[:], in_=bout_i).then_inc(lc_bo, 16)
            sync.dma_start(out=s2stage[:], in_=wqb_i).then_inc(lc_qb, 16)
            sync.wait_ge(s_cast, 6)
            sync.dma_start(out=s2stage[:], in_=wqbs_i).then_inc(lc_qb, 16)
            for r in range(8):
                if r > 0:
                    sync.wait_ge(s_cast, 7 + r)
                sync.dma_start(out=wstage[:], in_=wout_r[:, 2 * r:2 * r + 2, :]).then_inc(ld_c, 16)
            for s in range(NSL):
                if s % 2 == 0:
                    if s >= 2:
                        sync.wait_ge(c_xb, s - 1)
                    sync.dma_start(out=xf[:, s % 2, :], in_=x_full_r[s]).then_inc(ld_x, 16)
            for i in range(16):
                if i < 2:
                    sync.wait_ge(c_xb, NSL)
                else:
                    sync.wait_ge(c_xob, i - 1)
                sync.dma_start(out=xf[:, i % 2, :], in_=xo_r[i]).then_inc(lxo[i % 2], 16)
            sync.wait_ge(c_xb, NSL)
            sync.wait_ge(c_xob, 16)
            sync.wait_ge(s_trec, 8 * NSL)
            sync.wait_ge(s_troc, 128)
            sync.wait_ge(s_xte, 8 * NSL)
            sync.wait_ge(s_xoev, 128)
            sync.wait_ge(s_lat, NSL)
            sync.wait_ge(s_kvl, NSL)
            sync.wait_ge(s_rope, 2 * NSL)
            sync.wait_ge(s_vev, NSL)
            sync.wait_ge(s_qmm, 8)
            sync.wait_ge(s_qle, 8)
            sync.wait_ge(s_cast, NCAST)
            for j in range(SLOTS):
                if j >= 1:
                    sync.wait_ge(s_msk, TOP_END[j])
                if j >= 2:
                    sync.wait_ge(s_qrope, 8 * (j - 1))
                sync.dma_start(out=mask_w[:], in_=masks[:, j * 8 * QC:(j + 1) * 8 * QC]).then_inc(ld_msk, 16)
                sync.dma_start(out=qc_w[:, j % 2, :], in_=ctab_q[:, j * QC:(j + 1) * QC]).then_inc(lqt[j % 2], 16)
                sync.dma_start(out=qs_w[:, j % 2, :], in_=stab_q[:, j * QC:(j + 1) * QC]).then_inc(lqt[j % 2], 16)
                for tt in range(2):
                    it = 2 * j + tt
                    if j >= 1:
                        sync.wait_ge(s_app, it - 1)
                    sync.dma_start(out=xres[:, tt, :], in_=xo_r[it]).then_inc(lxr[tt], 16)
            sync.wait_ge(s_o, NU)
            sync.wait_ge(s_exp, NU)
            sync.wait_ge(s_msk, TOP_END[SLOTS])
            sync.wait_ge(s_qbmm, 64)
            sync.wait_ge(s_qrope, 64)
            sync.wait_ge(s_norm, 16 * SLOTS)
            sync.wait_ge(s_obc, 16 * 32)
            sync.wait_ge(s_epimm, 16)
            sync.wait_ge(s_xsum, 16)
            sync.wait_ge(s_app, 16)
            sync.wait_ge(s_xnb, 16)
            sync.wait_ge(s_xnt, 16)
            sync.wait_ge(s_xnte, 32)
            sync.wait_ge(ssp[0], 16 * 8)
            sync.wait_ge(ssp[1], 16 * 8)
            for bb in range(2):
                for hc in range(HCH):
                    k = bb * HCH + hc
                    if k >= 2:
                        sync.wait_ge(s_wcast, 2 * (k - 1))
                    sync.dma_start(out=wgi_s[:, hc % 2, :], in_=wg_r[hc]).then_inc(lw[hc % 2], 16)
                    sync.dma_start(out=wui_s[:, hc % 2, :], in_=wu_r[hc]).then_inc(lw[hc % 2], 16)
                for ti in range(8):
                    gi = bb * 8 + ti

                    if gi >= 2:
                        sync.wait_ge(s_fsum, gi - 1)
                    sync.dma_start(out=xnr[:, gi % 2, :], in_=spill_r[gi]).then_inc(lxn[gi % 2], 16)

        @block.gpsimd
        def _(gpsimd):
            for s in range(NSL):
                if s % 2 == 1:
                    if s >= 2:
                        gpsimd.wait_ge(c_xb, s - 1)
                    gpsimd.dma_start(out=xf[:, s % 2, :], in_=x_full_r[s]).then_inc(ld_x2, 16)
            for j in range(SLOTS):
                for r in range(4):
                    q3 = 4 * j + r
                    gpsimd.wait_ge(s_sum, q3 + 1)
                    if q3 >= 2:
                        gpsimd.wait_ge(s_norm, 4 * (q3 - 1))
                    gpsimd.dma_start(out=srow_d[q3:q3 + 1, :], in_=srow_t[64:65, :]).then_inc(s_row, 16)
                    gpsimd.wait_ge(s_row, 16 * (q3 + 1))
                    gpsimd.dma_start(out=obc[:, q3 % 2, :],
                                     in_=srow_d[q3:q3 + 1, :].partition_broadcast(64)).then_inc(s_obc, 16)
                for tt in range(2):
                    it = 2 * j + tt
                    gpsimd.wait_ge(s_app, it + 1)
                    gpsimd.dma_start(out=spill_r[it], in_=xn_f[:, tt, :]).then_inc(ssp[tt], 16)
            for hc in range(HCH):
                if hc >= 2:
                    gpsimd.wait_ge(s_dwc, hc - 1)
                gpsimd.dma_start(out=wdi_s[:, hc % 2, :, :],
                                 in_=wd_i[:, hc * 128:(hc + 1) * 128].rearrange("(c p) h -> p c h", p=128)
                                 ).then_inc(lwd[hc % 2], 16)
            for gi in range(16):
                gpsimd.wait_ge(s_app2, gi + 1)
                gpsimd.dma_start(out=y_r[gi], in_=fsb[:, gi % 2, :]).then_inc(sy[gi % 2], 16)
            gpsimd.wait_ge(sy[0], 16 * 8)
            gpsimd.wait_ge(sy[1], 16 * 8)

        @block.scalar
        def _(scalar):
            san = [0]
            def sai(instr):
                instr.then_inc(sa, 1); san[0] += 1
                return instr
            for s in range(NSL):
                if s >= 2:
                    scalar.wait_ge(s_rope, 2 * (s - 1))
                scalar.dma_start(out=ct_w[:, s % 2, :], in_=ctab_kv[:, 128 * s:128 * (s + 1)]).then_inc(lt[s % 2], 16)
                scalar.dma_start(out=st_w[:, s % 2, :], in_=stab_kv[:, 128 * s:128 * (s + 1)]).then_inc(lt[s % 2], 16)
                scalar.wait_ge(s_lat, s + 1)
                scalar.activation(out=kvlatT[:, 128 * s:128 * (s + 1)], in_=pA2[0:KVL, 0:128],
                                  func=AF.Copy).then_inc(s_kvl, 1)
            for j in range(SLOTS):
                scalar.wait_ge(s_qmm, j + 1)
                scalar.activation(out=qlatT[:, j, :], in_=pA2[0:QL, 256:256 + QC],
                                  func=AF.Copy, scale=0.125).then_inc(s_qle, 1)
            for ui, (j, g, hh, kc, top, q3, fk, lk) in enumerate(UNITS):
                scalar.wait_ge(s_squad, ui + 1)
                if ui >= 3:
                    scalar.wait_ge(s_o, ui - 2)
                scalar.activation(out=Pq[:, ui % 3, :], in_=sq[ui % 3][:, :],
                                  func=AF.Exp).then_inc(s_exp, 1)
                if lk and q3 % 4 == 3:
                    for tt in range(2):
                        it = 2 * j + tt
                        scalar.wait_ge(s_stat, it + 1)
                        sai(scalar.activation(out=istd[:], in_=veps[:], func=AF.Ln))
                        scalar.wait_ge(sa, san[0])
                        scalar.activation(out=istd[:], in_=istd[:], func=AF.Exp,
                                          scale=-0.5).then_inc(s_istd, 1)
            for bb in range(2):
                for hc in range(HCH):
                    for tt4 in range(2):
                        f = hc * 2 + tt4
                        fb = bb * 48 + f
                        scalar.wait_ge(s_gu, fb + 1)
                        if fb >= 2:
                            scalar.wait_ge(s_sil, fb - 1)
                        scalar.activation(out=tb[:, fb % 2, :], in_=pA[f % 3][:, 0:512],
                                          func=AF.Tanh, scale=0.5 / 127.0).then_inc(s_se, 1)
                for ti in range(8):
                    gi = bb * 8 + ti
                    scalar.wait_ge(s_stat2, gi + 1)
                    sai(scalar.activation(out=istd[:], in_=veps[:], func=AF.Ln))
                    scalar.wait_ge(sa, san[0])
                    scalar.activation(out=istd[:], in_=istd[:], func=AF.Exp,
                                      scale=-0.5).then_inc(s_istd2, 1)

        @block.vector
        def _(vector):
            sdn = [0]
            def sdi(instr):
                instr.then_inc(sd, 1); sdn[0] += 1
                return instr
            vector.wait_ge(lc_qa, 16)
            vector.tensor_copy(out=wqakvaB[:], in_=sstage[:]).then_inc(s_cast, 1)
            vector.wait_ge(lc_k1, 16)
            vector.tensor_copy(out=wkB[:], in_=s3stage[:, 0, :]).then_inc(s_cast, 1)
            vector.wait_ge(lc_k2, 16)
            vector.tensor_copy(out=wksB[:], in_=s3stage[:, 1, :]).then_inc(s_cast, 1)
            vector.wait_ge(lc_k3, 16)
            vector.tensor_copy(out=wvB[:], in_=s3stage[:, 2, :]).then_inc(s_cast, 1)
            vector.wait_ge(lc_bo, 16)
            vector.tensor_copy(out=boutB[:], in_=bstage[:]).then_inc(s_cast, 1)
            vector.wait_ge(lc_qb, 16)
            vector.tensor_copy(out=wqbB[:], in_=s2stage[:]).then_inc(s_cast, 1)
            vector.wait_ge(lc_qb, 32)
            vector.tensor_copy(out=wqbsB[:], in_=s2stage[:]).then_inc(s_cast, 1)
            for r in range(8):
                vector.wait_ge(ld_c, 16 * (r + 1))
                vector.tensor_copy(out=woutB[:, 2 * r:2 * r + 2, :], in_=wstage[:]).then_inc(s_cast, 1)
            vector.memset(Vt[:], 1.0).then_inc(s_ms, 1)
            vector.memset(onesB[:], 1.0).then_inc(s_ms, 1)
            for s in range(NSL):
                if s % 2 == 0:
                    vector.wait_ge(ld_x, 16 * (s // 2 + 1))
                else:
                    vector.wait_ge(ld_x2, 16 * ((s + 1) // 2))
                if s >= 2:
                    vector.wait_ge(s_trec, 8 * (s - 1))
                vector.tensor_copy(out=xb[:, s % 2, :], in_=xf[:, s % 2, :]).then_inc(c_xb, 1)
                for ec in range(ECH):
                    vector.wait_ge(s_trec, 8 * s + ec + 1)
                    if s >= 2 and ec == 0:
                        vector.wait_ge(s_lat, s - 1)
                    vector.tensor_copy(out=xTw[:, s % 2, ec, :], in_=pTR[ec % 2][:, 0:128]).then_inc(s_xte, 1)
                for g in range(2):
                    vector.wait_ge(s_kmm, 2 * s + g + 1)
                    vector.wait_ge(lt[s % 2], 32 * (s // 2 + 1))
                    sdi(vector.tensor_tensor(out=rtmp1[:, 0:128], in0=pA0[:, 512 * g:512 * g + 128],
                                             in1=ct_w[:, s % 2, :], op=ALU.mult))
                    sdi(vector.tensor_tensor(out=rtmp2[:, 0:128], in0=pA1[:, 512 * g:512 * g + 128],
                                             in1=st_w[:, s % 2, :], op=ALU.mult))
                    vector.wait_ge(sd, sdn[0])
                    vector.tensor_add(out=KTp[g][:, 128 * s:128 * (s + 1)],
                                      in0=rtmp1[:, 0:128], in1=rtmp2[:, 0:128]).then_inc(s_rope, 1)
                vector.wait_ge(s_vmm, s + 1)
                if s == 0:
                    vector.wait_ge(s_ms, 2)
                vector.tensor_copy(out=Vt[:, s, :, 0:64],
                                   in_=pA2[:, 512:768].rearrange("p (h d) -> p h d", h=4)
                                   ).then_inc(s_vev, 1)
            for i in range(16):
                vector.wait_ge(lxo[i % 2], 16 * (i // 2 + 1))
                if i >= 2:
                    vector.wait_ge(s_troc, 8 * (i - 1))
                vector.tensor_copy(out=xb[:, i % 2, :], in_=xf[:, i % 2, :]).then_inc(c_xob, 1)
                tt = i % 2
                for ec in range(ECH):
                    vector.wait_ge(s_troc, 8 * i + ec + 1)
                    if i >= 2 and ec == 0:
                        vector.wait_ge(s_qmm, i // 2)
                    vector.tensor_copy(out=xoT[:, ec, tt * 128:(tt + 1) * 128],
                                       in_=pTR[ec % 2][:, 0:128]).then_inc(s_xoev, 1)
            def dve_recip_piece(q3, c):
                if c == 0:
                    if q3 >= 1:
                        vector.wait_ge(s_row, 16 * q3)
                    vector.wait_ge(s_oev, q3 + 1)
                with nc.allow_low_precision(reason="1/softmax-sums in bf16; rvr margin 20x"):
                    r = vector.reciprocal(out=srow_t[64:65, c * 256:(c + 1) * 256],
                                          in_=osb[64:65, q3 % 2, c * 256:(c + 1) * 256])
                if c == 3:
                    r.then_inc(s_sum, 1)
                else:
                    sdi(r)

            def dve_normblock(jj, q3):
                vector.wait_ge(s_obc, 16 * (q3 + 1))
                if jj >= 1:
                    vector.wait_ge(s_epimm, 2 * jj)
                for ho in range(4):
                    c = (q3 % 4) * 4 + ho
                    vector.tensor_tensor(out=OT[:, c, :],
                                         in0=osb[0:64, q3 % 2, ho * QC:(ho + 1) * QC],
                                         in1=obc[:, q3 % 2, (2 * (ho // 2) + (ho % 2)) * QC:
                                                 (2 * (ho // 2) + (ho % 2) + 1) * QC],
                                         op=ALU.mult).then_inc(s_norm, 1)

            prev_slot = -1
            agenda = {}
            for ui, (j, g, hh, kc, top, q3, fk, lk) in enumerate(UNITS):
                li = ui - U_BEFORE[j]
                kj = kcj(j)
                if j != prev_slot:
                    prev_slot = j
                    agenda = {}
                    for r in range(3):
                        q3r = 4 * j + r
                        L = (r + 1) * kj - 1
                        for c in range(4):
                            agenda.setdefault(L + 1 + c, []).append(
                                lambda q3r=q3r, c=c: dve_recip_piece(q3r, c))
                        agenda.setdefault(L + 6, []).append(
                            lambda j=j, q3r=q3r: dve_normblock(j, q3r))
                    for pr in range(8):
                        vector.wait_ge(s_qbmm, 8 * j + pr + 1)
                        vector.wait_ge(lqt[j % 2], 32 * (j // 2 + 1))
                        src = sq[pr % 2]
                        sdi(vector.tensor_tensor(out=rtmp1[:, 0:QC], in0=src[:, 0:QC],
                                                 in1=qc_w[:, j % 2, :], op=ALU.mult))
                        sdi(vector.tensor_tensor(out=rtmp2[:, 0:QC], in0=src[:, 512:512 + QC],
                                                 in1=qs_w[:, j % 2, :], op=ALU.mult))
                        vector.wait_ge(sd, sdn[0])
                        vector.tensor_add(out=qTw[:, pr, :], in0=rtmp1[:, 0:QC],
                                          in1=rtmp2[:, 0:QC]).then_inc(s_qrope, 1)
                if top >= 0:
                    vector.wait_ge(s_exp, ui + 1)
                    vector.wait_ge(ld_msk, 16 * (j + 1))
                    ki = kc - 8 * j
                    mk = mask_w[:, ki * QC:(ki + 1) * QC]
                    pv = Pq[:, ui % 3, :].rearrange("p (r n) -> p r n", r=4)
                    vector.tensor_tensor(out=pv, in0=pv,
                                         in1=mk.unsqueeze(1).broadcast_to([128, 4, QC]),
                                         op=ALU.mult).then_inc(s_msk, 1)
                if lk:
                    vector.wait_ge(s_o, ui + 1)
                    if q3 >= 2:
                        vector.wait_ge(s_norm, 4 * (q3 - 1))
                    sdi(vector.tensor_copy(out=osb[:, q3 % 2, 0:512], in_=pT0[:, :]))
                    vector.tensor_copy(out=osb[:, q3 % 2, 512:1024], in_=pT1[:, :]).then_inc(s_oev, 1)
                    if q3 % 4 == 3:
                        for c in range(4):
                            dve_recip_piece(q3, c)
                        dve_normblock(j, q3)
                for fn in agenda.pop(li, []):
                    fn()
                if lk and q3 % 4 == 3:
                    for tt in range(2):
                        it = 2 * j + tt
                        vector.wait_ge(s_epimm, it + 1)
                        vector.wait_ge(lxr[tt], 16 * (j + 1))
                        sdi(vector.tensor_add(out=xres[:, tt, :], in0=sq[tt][:, :], in1=xres[:, tt, :]))
                        vector.wait_ge(sd, sdn[0])
                        if it >= 1:
                            vector.wait_ge(s_istd, it)
                        sdi(vector.bn_stats(stat[:, 0:6], xres[:, tt, 0:512]))
                        vector.bn_stats(stat[:, 6:12], xres[:, tt, 512:1024]).then_inc(s_xsum, 1)
                        vector.wait_ge(sd, sdn[0])
                        vector.wait_ge(s_xsum, it + 1)
                        sdi(vector.bn_aggr(mv[:], stat[:, 0:12]))
                        vector.wait_ge(sd, sdn[0])
                        vector.tensor_scalar_add(veps[:], mv[:, 1:2], EPS).then_inc(s_stat, 1)
                        vector.wait_ge(s_istd, it + 1)
                        vector.tensor_scalar(out=nbias[:], in0=istd[:], scalar1=mv[:, 0:1],
                                             scalar2=-1.0, op0=ALU.mult, op1=ALU.mult).then_inc(s_nb, 1)
                        vector.wait_ge(s_nb, it + 1)
                        if j >= 1:
                            vector.wait_ge(ssp[tt], 16 * (it // 2))
                        vector.tensor_scalar(out=xn_f[:, tt, :], in0=xres[:, tt, :],
                                             scalar1=istd[:], scalar2=nbias[:],
                                             op0=ALU.mult, op1=ALU.add).then_inc(s_app, 1)
                        vector.wait_ge(s_app, it + 1)
                        if j >= 1:
                            vector.wait_ge(s_xnt, it - 1)
                        vector.tensor_copy(out=xn_b[:, tt, :], in_=xn_f[:, tt, :]).then_inc(s_xnb, 1)
                        vector.wait_ge(s_xnt, it + 1)
                        col = j * QC + tt * 128
                        for hf in range(2):
                            vector.tensor_copy(
                                out=xnT[:, 4 * hf:4 * hf + 4, col:col + 128],
                                in_=pTR[hf][:, :].rearrange("p (c n) -> p c n", c=4)
                                ).then_inc(s_xnte, 1)
            for hc in range(HCH):
                vector.wait_ge(lwd[hc % 2], 16 * (hc // 2 + 1))
                if hc >= 2:
                    vector.wait_ge(s_dwt, 2 * (hc - 1))
                vector.tensor_copy(out=wdb_s[:, hc % 2, :, :], in_=wdi_s[:, hc % 2, :, :]).then_inc(s_dwc, 1)
                for grp in range(2):
                    vector.wait_ge(s_dwt, 2 * hc + grp + 1)
                    vector.tensor_copy(out=wdT[:, hc, grp * 512:(grp + 1) * 512],
                                       in_=pTR[grp][:, :]).then_inc(s_dwev, 1)
            for bb in range(2):
                for hc in range(HCH):
                    k = bb * HCH + hc
                    vector.wait_ge(lw[hc % 2], 32 * (bb * 12 + hc // 2 + 1))
                    if k >= 2:
                        vector.wait_ge(s_wtr, 4 * (k - 1))
                    vector.tensor_copy(out=wgb_s[:, hc % 2, :], in_=wgi_s[:, hc % 2, :]).then_inc(s_wcast, 1)
                    vector.tensor_copy(out=wub_s[:, hc % 2, :], in_=wui_s[:, hc % 2, :]).then_inc(s_wcast, 1)
                    for grp in range(4):
                        vector.wait_ge(s_wtr, 4 * k + grp + 1)
                        dst = (wgT if grp < 2 else wuT)
                        hf = grp % 2
                        vector.tensor_copy(out=dst[:, hc % 2, 4 * hf:4 * hf + 4, :],
                                           in_=pTR[grp % 2][:, :].rearrange("p (c n) -> p c n", c=4)
                                           ).then_inc(s_wev, 1)
                    for tt4 in range(2):
                        f = hc * 2 + tt4
                        fb = bb * 48 + f
                        vector.wait_ge(s_se, fb + 1)
                        u1 = vector.tensor_scalar_add(ub[:, fb % 2, :], tb[:, fb % 2, :], 1.0)
                        sdi(u1)
                        vector.wait_ge(sd, sdn[0])
                        h1 = vector.tensor_tensor(out=rb[:, fb % 2, :], in0=ub[:, fb % 2, :],
                                                  in1=pA[f % 3][:, 0:512], op=ALU.mult)
                        sdi(h1)
                        vector.wait_ge(sd, sdn[0])
                        vector.tensor_tensor(out=hactT[:, hc, tt4 * 512:(tt4 + 1) * 512],
                                             in0=rb[:, fb % 2, :], in1=pA[f % 3][:, 512:1024],
                                             op=ALU.mult).then_inc(s_sil, 1)
                for ti in range(8):
                    gi = bb * 8 + ti
                    vector.wait_ge(s_dmm, gi + 1)
                    vector.wait_ge(lxn[gi % 2], 16 * (gi // 2 + 1))
                    if gi >= 2:
                        vector.wait_ge(sy[gi % 2], 16 * (gi // 2))
                    if gi == 0:
                        vector.wait_ge(s_app, 16)
                    vector.scalar_tensor_tensor(out=fsb[:, gi % 2, :], in0=pA[ti % 3][:, :],
                                                scalar=DQ3, in1=xnr[:, gi % 2, :],
                                                op0=ALU.mult, op1=ALU.add).then_inc(s_fsum, 1)
                    vector.wait_ge(s_fsum, gi + 1)
                    if gi >= 1:
                        vector.wait_ge(s_istd2, gi)
                    sdi(vector.bn_stats(stat[:, 0:6], fsb[:, gi % 2, 0:512]))
                    sdi(vector.bn_stats(stat[:, 6:12], fsb[:, gi % 2, 512:1024]))
                    vector.wait_ge(sd, sdn[0])
                    sdi(vector.bn_aggr(mv[:], stat[:, 0:12]))
                    vector.wait_ge(sd, sdn[0])
                    vector.tensor_scalar_add(veps[:], mv[:, 1:2], EPS).then_inc(s_stat2, 1)
                    vector.wait_ge(s_istd2, gi + 1)
                    vector.tensor_scalar(out=nbias[:], in0=istd[:], scalar1=mv[:, 0:1],
                                         scalar2=-1.0, op0=ALU.mult, op1=ALU.mult).then_inc(s_nb2, 1)
                    vector.wait_ge(s_nb2, gi + 1)
                    vector.tensor_scalar(out=fsb[:, gi % 2, :], in0=fsb[:, gi % 2, :],
                                         scalar1=istd[:], scalar2=nbias[:],
                                         op0=ALU.mult, op1=ALU.add).then_inc(s_app2, 1)

        @block.tensor
        def _(tensor):
            for s in range(NSL):
                tensor.wait_ge(c_xb, s + 1)
                if s == 0:
                    tensor.wait_ge(lc_i, 16)
                for ec in range(ECH):
                    w = 8 * s + ec - 1
                    if w > 0:
                        tensor.wait_ge(s_xte, w)
                    mm = tensor.matmul(pTR[ec % 2][:, 0:128],
                                       xb[:, s % 2, ec * 128:(ec + 1) * 128],
                                       identB[:], start=True, stop=True)
                    mm.then_inc(s_trec, 1)
                tensor.wait_ge(s_xte, 8 * (s + 1))
                if s >= 1:
                    tensor.wait_ge(s_kvl, s)
                if s == 0:
                    tensor.wait_ge(s_cast, NCAST)
                for ec in range(ECH):
                    mm = tensor.matmul(pA2[0:KVL, 0:128], wqakvaB[:, ec, 64:96],
                                       xTw[:, s % 2, ec, :], start=(ec == 0), stop=(ec == ECH - 1))
                mm.then_inc(s_lat, 1)
                tensor.wait_ge(s_kvl, s + 1)
                for g in range(2):
                    w = 2 * s + g - 1
                    if w > 0:
                        tensor.wait_ge(s_rope, w)
                    tensor.matmul(pA0[:, 512 * g:512 * g + 128], wkB[:, g * 128:(g + 1) * 128],
                                  kvlatT[:, 128 * s:128 * (s + 1)], start=True, stop=True)
                    mm = tensor.matmul(pA1[:, 512 * g:512 * g + 128], wksB[:, g * 128:(g + 1) * 128],
                                       kvlatT[:, 128 * s:128 * (s + 1)], start=True, stop=True)
                    mm.then_inc(s_kmm, 1)
                if s >= 1:
                    tensor.wait_ge(s_vev, s)
                mm = tensor.matmul(pA2[:, 512:768],
                                   kvlatT[:, 128 * s:128 * (s + 1)],
                                   wvB[:], start=True, stop=True)
                mm.then_inc(s_vmm, 1)
            for i in range(16):
                tensor.wait_ge(c_xob, i + 1)
                tt = i % 2
                for ec in range(ECH):
                    w = 8 * i + ec - 1
                    if w > 0:
                        tensor.wait_ge(s_xoev, w)
                    mm = tensor.matmul(pTR[ec % 2][:, 0:128], xb[:, i % 2, ec * 128:(ec + 1) * 128],
                                       identB[:], start=True, stop=True)
                    mm.then_inc(s_troc, 1)
                if tt == 1:
                    jslot = i // 2
                    tensor.wait_ge(s_xoev, 8 * (i + 1))
                    if jslot == 0:
                        tensor.wait_ge(s_kvl, NSL)
                    if jslot >= 1:
                        tensor.wait_ge(s_qle, jslot)
                    for ec in range(ECH):
                        mm = tensor.matmul(pA2[0:QL, 256:256 + QC], wqakvaB[:, ec, 0:QL],
                                           xoT[:, ec, :], start=(ec == 0), stop=(ec == ECH - 1))
                    mm.then_inc(s_qmm, 1)
            for j in range(SLOTS):
                for pr in range(8):
                    if pr == 0:
                        tensor.wait_ge(s_qle, j + 1)
                        tensor.wait_ge(s_exp, U_BEFORE[j])
                        tensor.wait_ge(s_xsum, 2 * j)
                        if j == 0:
                            tensor.wait_ge(s_rope, 2 * NSL)
                    w = 8 * j + pr - 1
                    if w > 0:
                        tensor.wait_ge(s_qrope, w)
                    dst = sq[pr % 2]
                    tensor.matmul(dst[:, 0:QC], wqbB[:, pr * 128:(pr + 1) * 128],
                                  qlatT[:, j, :], start=True, stop=True)
                    mm = tensor.matmul(dst[:, 512:512 + QC], wqbsB[:, pr * 128:(pr + 1) * 128],
                                       qlatT[:, j, :], start=True, stop=True)
                    mm.then_inc(s_qbmm, 1)
                ujs = list(range(U_BEFORE[j], U_BEFORE[j + 1]))

                def emit_S(ui, j=j):
                    jj, g, hh, kc, top, q3, fk, lk = UNITS[ui]
                    if ui == U_BEFORE[j]:
                        tensor.wait_ge(s_qrope, 8 * (j + 1))
                        if j == 0:
                            tensor.wait_ge(s_vev, NSL)
                            tensor.wait_ge(s_qmm, SLOTS)
                    if ui >= 3:
                        tensor.wait_ge(s_exp, ui - 2)
                    pr0 = 4 * g + 2 * hh
                    tensor.matmul(sq[ui % 3][:, 0:512],
                                  KTp[g][0:64, 128 * kc:128 * (kc + 1)],
                                  qTw[0:64, pr0:pr0 + 2, :], start=True, stop=True,
                                  tile_position=(0, 0), skip_group_check=True)
                    mm = tensor.matmul(sq[ui % 3][:, 512:1024],
                                       KTp[g][64:128, 128 * kc:128 * (kc + 1)],
                                       qTw[64:128, pr0:pr0 + 2, :], start=True, stop=True,
                                       tile_position=(64, 0), skip_group_check=True)
                    mm.then_inc(s_squad, 1)

                def emit_O(ui, j=j):
                    jj, g, hh, kc, top, q3, fk, lk = UNITS[ui]
                    if fk:
                        if q3 >= 1:
                            tensor.wait_ge(s_oev, q3)
                        if q3 == 0:
                            tensor.wait_ge(s_xoev, 128)
                        if j >= 1 and q3 % 4 == 0:
                            tensor.wait_ge(s_xnte, 4 * j)
                    if top >= 0:
                        tensor.wait_ge(s_msk, top + 1)
                    else:
                        tensor.wait_ge(s_exp, ui + 1)
                    for bi in range(2):
                        kvh = 2 * g + bi
                        mm = tensor.matmul(obank(q3, bi)[0:65, 0:512],
                                           Vt[:, kc, kvh, :],
                                           Pq[:, ui % 3, 512 * bi:512 * (bi + 1)],
                                           start=fk, stop=lk, skip_group_check=True)
                    mm.then_inc(s_o, 1)

                emit_S(ujs[0])
                emit_S(ujs[1])
                for idx, ui in enumerate(ujs):
                    if idx + 2 < len(ujs):
                        emit_S(ujs[idx + 2])
                    emit_O(ui)
                for tt in range(2):
                    if tt == 0:
                        tensor.wait_ge(s_norm, 16 * (j + 1))
                        tensor.wait_ge(s_exp, U_BEFORE[j + 1])
                    for c in range(16):
                        for eh in range(2):
                            tensor.matmul(sq[tt][:, eh * 512:(eh + 1) * 512],
                                          OT[:, c, tt * 128:(tt + 1) * 128],
                                          woutB[:, c, eh * 512:(eh + 1) * 512],
                                          start=(c == 0), stop=False, skip_group_check=True)
                    for eh in range(2):
                        mm = tensor.matmul(sq[tt][:, eh * 512:(eh + 1) * 512],
                                           onesB[:], boutB[:, eh * 512:(eh + 1) * 512],
                                           start=False, stop=True, skip_group_check=True)
                    mm.then_inc(s_epimm, 1)
                for tt in range(2):
                    it = 2 * j + tt
                    tensor.wait_ge(s_xnb, it + 1)
                    if tt == 0:
                        tensor.wait_ge(s_oev, 4 * (j + 1))
                    else:
                        tensor.wait_ge(s_xnte, 4 * j + 2)
                    for ec in range(ECH):
                        hf = ec // 4
                        mm = tensor.matmul(pTR[hf][:, (ec % 4) * 128:(ec % 4 + 1) * 128],
                                           xn_b[:, tt, ec * 128:(ec + 1) * 128],
                                           identB[:], start=True, stop=True, skip_group_check=True)
                    mm.then_inc(s_xnt, 1)
            for hc in range(HCH):
                tensor.wait_ge(s_dwc, hc + 1)
                if hc == 0:
                    tensor.wait_ge(s_xnte, 4 * SLOTS)
                for grp in range(2):
                    w = 2 * hc + grp - 1
                    if w > 0:
                        tensor.wait_ge(s_dwev, w)
                    for q in range(4):
                        dc = grp * 4 + q
                        mm = tensor.matmul(pTR[grp][:, q * 128:(q + 1) * 128],
                                           wdb_s[:, hc % 2, dc, :], identB[:],
                                           start=True, stop=True)
                    mm.then_inc(s_dwt, 1)
            for bb in range(2):
                for hc in range(HCH):
                    k = bb * HCH + hc
                    tensor.wait_ge(s_wcast, 2 * (k + 1))
                    for grp in range(4):
                        if k == 0 and grp == 0:
                            tensor.wait_ge(s_dwev, 2 * HCH)
                        w = 4 * k + grp - 1
                        if w > 0:
                            tensor.wait_ge(s_wev, w)
                        src = (wgb_s if grp < 2 else wub_s)
                        hf = grp % 2
                        for q in range(4):
                            ec = hf * 4 + q
                            mm = tensor.matmul(pTR[grp % 2][:, q * 128:(q + 1) * 128],
                                               src[:, hc % 2, ec * 128:(ec + 1) * 128],
                                               identB[:], start=True, stop=True)
                        mm.then_inc(s_wtr, 1)
                    for tt4 in range(2):
                        f = hc * 2 + tt4
                        fb = bb * 48 + f
                        if tt4 == 0:
                            tensor.wait_ge(s_wev, 4 * (k + 1))
                        if fb >= 3:
                            tensor.wait_ge(s_sil, fb - 2)
                        if bb == 1 and f == 0:
                            tensor.wait_ge(s_fsum, 8)
                        if fb == 0:
                            tensor.wait_ge(s_xnte, 4 * SLOTS)
                        for ec in range(ECH):
                            tensor.matmul(pA[f % 3][:, 0:512], wgT[:, hc % 2, ec, :],
                                          xnT[:, ec, bb * 1024 + tt4 * 512:bb * 1024 + (tt4 + 1) * 512],
                                          start=(ec == 0), stop=(ec == ECH - 1))
                        for ec in range(ECH):
                            mm = tensor.matmul(pA[f % 3][:, 512:1024], wuT[:, hc % 2, ec, :],
                                               xnT[:, ec, bb * 1024 + tt4 * 512:bb * 1024 + (tt4 + 1) * 512],
                                               start=(ec == 0), stop=(ec == ECH - 1))
                        mm.then_inc(s_gu, 1)
                for ti in range(8):
                    gi = bb * 8 + ti
                    tensor.wait_ge(s_sil, 48 * (bb + 1))
                    if gi == 0:
                        tensor.wait_ge(s_dwev, 2 * HCH)
                    if gi >= 3:
                        tensor.wait_ge(s_fsum, gi - 2)
                    for eh in range(2):
                        for hc in range(HCH):
                            mm = tensor.matmul(pA[ti % 3][:, eh * 512:(eh + 1) * 512],
                                               hactT[:, hc, ti * 128:(ti + 1) * 128],
                                               wdT[:, hc, eh * 512:(eh + 1) * 512],
                                               start=(hc == 0), stop=(hc == HCH - 1),
                                               skip_group_check=True)
                    mm.then_inc(s_dmm, 1)

    return nc


def _rope_tables(positions):
    d = np.arange(128)
    i = (d % 64) // 2
    inv = 1.0 / (10000.0 ** (2.0 * i / 64.0))
    ang = np.outer(inv, positions.astype(np.float64))
    c = np.cos(ang).astype(np.float32)
    s = np.sin(ang).astype(np.float32)
    sign = np.where(d % 2 == 0, -1.0, 1.0).astype(np.float32)[:, None]
    return np.ascontiguousarray(c), np.ascontiguousarray(s * sign)


def kernel(L0_attn_x, L0_attn_Wqa, L0_attn_Wqb, L0_attn_Wkva, L0_attn_Wkvb,
           L0_attn_Wout, L0_attn_bout, L0_ffn_w_gate, L0_ffn_w_up, L0_ffn_w_down):
    x = np.ascontiguousarray(np.asarray(L0_attn_x, dtype=np.float32))
    Wqa = np.asarray(L0_attn_Wqa, dtype=np.float32)
    Wqb = np.asarray(L0_attn_Wqb, dtype=np.float32)
    Wkva = np.asarray(L0_attn_Wkva, dtype=np.float32)
    Wkvb = np.asarray(L0_attn_Wkvb, dtype=np.float32)
    Wout = np.asarray(L0_attn_Wout, dtype=np.float32)
    bout = np.ascontiguousarray(np.asarray(L0_attn_bout, dtype=np.float32).reshape(1, E))
    wg = np.ascontiguousarray(np.asarray(L0_ffn_w_gate, dtype=np.int32))
    wu = np.ascontiguousarray(np.asarray(L0_ffn_w_up, dtype=np.int32))
    wd = np.ascontiguousarray(np.asarray(L0_ffn_w_down, dtype=np.int32))

    wkvb_r = Wkvb.reshape(KVL, KVH, D, 2)
    wk = np.ascontiguousarray(wkvb_r[..., 0].reshape(KVL, 256))
    wv = np.ascontiguousarray(wkvb_r[..., 1].reshape(KVL, 256))
    wks = np.ascontiguousarray(
        wkvb_r[..., 0].reshape(KVL, KVH, 32, 2)[:, :, :, ::-1].reshape(KVL, 256))
    qperm = []
    for g in range(2):
        for i in range(4):
            ha, hb = 8 * g + i, 8 * g + 4 + i
            qperm.extend(range(64 * ha, 64 * ha + 64))
            qperm.extend(range(64 * hb, 64 * hb + 64))
    qperm = np.array(qperm)
    wqb_p = np.ascontiguousarray(Wqb[:, qperm])
    wqbs_full = Wqb.reshape(QL, H, 32, 2)[:, :, :, ::-1].reshape(QL, E)
    wqbs_p = np.ascontiguousarray(wqbs_full[:, qperm])
    operm = []
    for g in range(2):
        for hh in range(2):
            for h in (8 * g + 2 * hh, 8 * g + 2 * hh + 1, 8 * g + 4 + 2 * hh, 8 * g + 4 + 2 * hh + 1):
                operm.extend(range(64 * h, 64 * h + 64))
    operm = np.array(operm)
    wout_p = np.ascontiguousarray(Wout[operm, :])
    wqakva_ = np.ascontiguousarray(np.concatenate([Wqa, Wkva], axis=1))
    identity = np.eye(128, dtype=BF)
    ckv, skv = _rope_tables(np.arange(S))

    nc = build_nc()
    in_maps = []
    chunk_of = np.zeros((NCORE, SLOTS), dtype=np.int64)
    for c in range(NCORE):
        b, cm = c // 4, c % 4
        chunks = [4 * j + cm for j in range(SLOTS)]
        chunk_of[c] = chunks
        xb_ = x[b]
        x_own_ = np.ascontiguousarray(
            np.concatenate([xb_[QC * p:QC * (p + 1)] for p in chunks], axis=0))
        pos_own = np.concatenate([np.arange(QC * p, QC * (p + 1)) for p in chunks])
        cq, sq_ = _rope_tables(pos_own)
        cq = cq.astype(BF); sq_ = sq_.astype(BF)
        m = np.zeros((128, SLOTS * 8 * QC), dtype=BF)
        iv = np.arange(128)[:, None]
        qv = np.arange(QC)[None, :]
        for j in range(SLOTS):
            p = chunks[j]
            for ki in range(8):
                v = 1024 * j + 128 * ki + iv
                m[:, (j * 8 + ki) * QC:(j * 8 + ki + 1) * QC] = (v <= QC * p + qv).astype(BF)
        in_maps.append({
            "x_own": x_own_, "x_full": np.ascontiguousarray(xb_),
            "ident": identity, "masks": m,
            "ctab_kv": ckv, "stab_kv": skv, "ctab_q": cq, "stab_q": sq_,
            "wqakva": wqakva_, "wqb": wqb_p, "wqbs": wqbs_p,
            "wk": wk, "wks": wks, "wv": wv,
            "wout": wout_p, "bout": bout,
            "wg": wg, "wu": wu, "wd": wd,
        })

    res = run_bass_kernel_spmd(nc, in_maps, core_ids=list(range(NCORE)))

    out = np.empty((B, S, E), dtype=np.float32)
    for c in range(NCORE):
        b = c // 4
        yc = np.asarray(res.results[c]["y"])
        for j in range(SLOTS):
            p = chunk_of[c][j]
            out[b, QC * p:QC * (p + 1)] = yc[QC * j:QC * (j + 1)]
    return out


# version 18
# speedup vs baseline: 1.7145x; 1.0220x over previous
import contextlib
import numpy as np
import ml_dtypes
import concourse.bass as bass
import concourse.mybir as mybir
from concourse.bass_utils import run_bass_kernel_spmd

F32 = mybir.dt.float32
BF16 = mybir.dt.bfloat16
I32 = mybir.dt.int32
AF = mybir.ActivationFunctionType
ALU = mybir.AluOpType
BF = ml_dtypes.bfloat16

B, S, E = 2, 8192, 1024
H, KVH, D = 16, 4, 64
QL, KVL = 64, 32
HID = 3072
NCORE = 8
QC = 256
SLOTS = 8
TOK = QC * SLOTS
ECH = E // 128
HCH = HID // 128
NSL = S // 128
EPS = 1e-5
DQ3 = 1.0 / (2.0 * 127.0 ** 3)

def kcj(j):
    return 8 * (j + 1)

UNITS = []
_t = 0
for _j in range(SLOTS):
    for _g in range(2):
        for _hh in range(2):
            _q3 = _j * 4 + _g * 2 + _hh
            for _kc in range(kcj(_j)):
                to = _t if _kc >= 8 * _j else -1
                if to >= 0:
                    _t += 1
                UNITS.append((_j, _g, _hh, _kc, to, _q3, _kc == 0, _kc == kcj(_j) - 1))
NU = len(UNITS)
U_BEFORE = [0] * (SLOTS + 1)
for _j in range(SLOTS):
    U_BEFORE[_j + 1] = U_BEFORE[_j] + 4 * kcj(_j)
TOP_END = [0] * (SLOTS + 1)
_c = 0
for ui in range(NU):
    if UNITS[ui][4] >= 0:
        _c += 1
    TOP_END[UNITS[ui][0] + 1] = _c


def build_nc():
    nc = bass.Bass("TRN2", target_bir_lowering=False, debug=False)

    def din(name, shape, dt):
        return nc.dram_tensor(name, shape, dt, kind="ExternalInput").ap()

    x_own = din("x_own", [TOK, E], F32)
    x_full = din("x_full", [S, E], F32)
    ident = din("ident", [128, 128], BF16)
    masks = din("masks", [128, SLOTS * 8 * QC], BF16)
    ctab_kv = din("ctab_kv", [128, S], F32)
    stab_kv = din("stab_kv", [128, S], F32)
    ctab_q = din("ctab_q", [128, TOK], BF16)
    stab_q = din("stab_q", [128, TOK], BF16)
    wqakva = din("wqakva", [E, 96], F32)
    wqb_i = din("wqb", [QL, E], F32)
    wqbs_i = din("wqbs", [QL, E], F32)
    wk_i = din("wk", [KVL, 256], F32)
    wks_i = din("wks", [KVL, 256], F32)
    wv_i = din("wv", [KVL, 256], F32)
    wout_i = din("wout", [E, E], F32)
    bout_i = din("bout", [1, E], F32)
    wg_i = din("wg", [HID, E], I32)
    wu_i = din("wu", [HID, E], I32)
    wd_i = din("wd", [E, HID], I32)

    y = nc.dram_tensor("y", [TOK, E], F32, kind="ExternalOutput").ap()
    xn_spill = nc.dram_tensor("xn_spill", [TOK, E], F32, kind="Internal").ap()
    srow_d = nc.dram_tensor("srow_d", [32, 1024], BF16, kind="Internal").ap()

    x_full_r = x_full.rearrange("(n p) e -> n p e", p=128)
    xo_r = x_own.rearrange("(i p) e -> i p e", p=128)
    wg_r = wg_i.rearrange("(c p) e -> c p e", p=128)
    wu_r = wu_i.rearrange("(c p) e -> c p e", p=128)
    spill_r = xn_spill.rearrange("(i p) e -> i p e", p=128)
    y_r = y.rearrange("(i p) e -> i p e", p=128)
    wout_r = wout_i.rearrange("(c p) e -> p c e", p=64)

    with contextlib.ExitStack() as R:
        EE = R.enter_context
        identB = EE(nc.sbuf_tensor("identB", [128, 128], BF16))
        rtmp1 = EE(nc.sbuf_tensor("rtmp1", [128, 512], F32))
        rtmp2 = EE(nc.sbuf_tensor("rtmp2", [128, 512], F32))
        stat = EE(nc.sbuf_tensor("stat", [128, 12], F32))
        mv = EE(nc.sbuf_tensor("mv", [128, 2], F32))
        veps = EE(nc.sbuf_tensor("veps", [128, 1], F32))
        istd = EE(nc.sbuf_tensor("istd", [128, 1], F32))
        nbias = EE(nc.sbuf_tensor("nbias", [128, 1], F32))
        xnT = EE(nc.sbuf_tensor("xnT", [128, ECH, TOK], BF16, side="right"))
        S12 = contextlib.ExitStack()
        E12 = S12.enter_context
        KTp = [E12(nc.sbuf_tensor(f"KTp{g}", [128, S], BF16)) for g in range(2)]
        Vt = E12(nc.sbuf_tensor("Vt", [128, S // 128, KVH, 65], BF16))
        woutB = E12(nc.sbuf_tensor("woutB", [64, 16, E], BF16))
        qlatT = E12(nc.sbuf_tensor("qlatT", [QL, SLOTS, QC], BF16))
        wqbB = E12(nc.sbuf_tensor("wqbB", [QL, E], BF16))
        wqbsB = E12(nc.sbuf_tensor("wqbsB", [QL, E], BF16))
        wkB = E12(nc.sbuf_tensor("wkB", [KVL, 256], BF16))
        wksB = E12(nc.sbuf_tensor("wksB", [KVL, 256], BF16))
        wvB = E12(nc.sbuf_tensor("wvB", [KVL, 256], BF16))
        wqakvaB = E12(nc.sbuf_tensor("wqakvaB", [128, ECH, 96], BF16))
        boutB = E12(nc.sbuf_tensor("boutB", [1, E], BF16))
        onesB = E12(nc.sbuf_tensor("onesB", [1, 128], BF16))
        S1 = contextlib.ExitStack()
        E1 = S1.enter_context
        xf = E1(nc.sbuf_tensor("xf", [128, 2, E], F32))
        xb = E1(nc.sbuf_tensor("xb", [128, 2, E], BF16))
        xTw = E1(nc.sbuf_tensor("xTw", [128, 2, ECH, 128], BF16))
        ct_w = E1(nc.sbuf_tensor("ct_w", [128, 2, 128], F32))
        st_w = E1(nc.sbuf_tensor("st_w", [128, 2, 128], F32))
        xoT = E1(nc.sbuf_tensor("xoT", [128, ECH, QC], BF16))
        wstage = E1(nc.sbuf_tensor("wstage", [64, 2, E], F32))
        bstage = E1(nc.sbuf_tensor("bstage", [1, E], F32))
        sstage = E1(nc.sbuf_tensor("sstage", [128, ECH, 96], F32))
        s2stage = E1(nc.sbuf_tensor("s2stage", [QL, E], F32))
        s3stage = E1(nc.sbuf_tensor("s3stage", [KVL, 3, 256], F32))
        kvlatT = E1(nc.sbuf_tensor("kvlatT", [KVL, S], BF16))
        S1.close()
        S2 = contextlib.ExitStack()
        E2 = S2.enter_context
        qTw = E2(nc.sbuf_tensor("qTw", [128, 8, QC], BF16))
        mask_w = E2(nc.sbuf_tensor("mask_w", [128, 8 * QC], BF16))
        Pq = E2(nc.sbuf_tensor("Pq", [128, 3, 1024], BF16))
        osb = E2(nc.sbuf_tensor("osb", [128, 2, 1024], F32, side="right"))
        OT = E2(nc.sbuf_tensor("OT", [64, 16, QC], BF16))
        obc = E2(nc.sbuf_tensor("obc", [64, 2, 1024], BF16))
        srow_t = E2(nc.sbuf_tensor("srow_t", [128, 1024], BF16))
        qc_w = E2(nc.sbuf_tensor("qc_w", [128, 2, QC], BF16))
        qs_w = E2(nc.sbuf_tensor("qs_w", [128, 2, QC], BF16))
        xres = E2(nc.sbuf_tensor("xres", [128, 2, E], F32))
        xn_f = E2(nc.sbuf_tensor("xn_f", [128, 2, E], F32))
        xn_b = E2(nc.sbuf_tensor("xn_b", [128, 2, E], BF16))
        S2.close()
        S12.close()
        hactT = EE(nc.sbuf_tensor("hactT", [128, HCH, 1024], BF16))
        wdT = EE(nc.sbuf_tensor("wdT", [128, HCH, E], BF16))
        wgi_s = EE(nc.sbuf_tensor("wgi_s", [128, 2, E], I32))
        wui_s = EE(nc.sbuf_tensor("wui_s", [128, 2, E], I32))
        wdi_s = EE(nc.sbuf_tensor("wdi_s", [128, 2, ECH, 128], I32))
        wgb_s = EE(nc.sbuf_tensor("wgb_s", [128, 2, E], BF16))
        wub_s = EE(nc.sbuf_tensor("wub_s", [128, 2, E], BF16))
        wdb_s = EE(nc.sbuf_tensor("wdb_s", [128, 2, ECH, 128], BF16))
        wgT = EE(nc.sbuf_tensor("wgT", [128, 2, ECH, 128], BF16))
        wuT = EE(nc.sbuf_tensor("wuT", [128, 2, ECH, 128], BF16))
        tb = EE(nc.sbuf_tensor("tb", [128, 2, 512], F32))
        ub = EE(nc.sbuf_tensor("ub", [128, 2, 512], F32))
        rb = EE(nc.sbuf_tensor("rb", [128, 2, 512], F32))
        xnr = EE(nc.sbuf_tensor("xnr", [128, 2, E], F32))
        fsb = EE(nc.sbuf_tensor("fsb", [128, 2, E], F32))

        pA0 = EE(nc.psum_tensor("pA0", [128, 1024], F32))
        pA1 = EE(nc.psum_tensor("pA1", [128, 1024], F32))
        pA2 = EE(nc.psum_tensor("pA2", [128, 1024], F32))
        pT0 = EE(nc.psum_tensor("pT0", [128, 512], F32))
        pT1 = EE(nc.psum_tensor("pT1", [128, 512], F32))
        pA = [pA0, pA1, pA2]
        sq = pA
        pTR = [pT0, pT1]

        def obank(q3, b):
            return (pT0 if b == 0 else pT1)[:, :]

        def sname(n):
            return EE(nc.semaphore(n))
        ld_c = sname("ld_c"); ld_x = sname("ld_x"); ld_x2 = sname("ld_x2")
        lc_i = sname("lc_i"); lc_qa = sname("lc_qa"); lc_k1 = sname("lc_k1")
        lc_k2 = sname("lc_k2"); lc_k3 = sname("lc_k3"); lc_bo = sname("lc_bo")
        lc_qb = sname("lc_qb")
        lt = [sname("lt0"), sname("lt1")]
        lqt = [sname("lqt0"), sname("lqt1")]
        lxr = [sname("lxr0"), sname("lxr1")]
        lw = [sname("lw0"), sname("lw1")]
        lwd = [sname("lwd0"), sname("lwd1")]
        lxn = [sname("lxn0"), sname("lxn1")]
        ssp = [sname("ssp0"), sname("ssp1")]
        sy = [sname("sy0"), sname("sy1")]
        lxo = [sname("lxo0"), sname("lxo1")]
        ld_t = sname("ld_t")
        ld_xo = sname("ld_xo"); ld_xres = sname("ld_xres"); ld_msk = sname("ld_msk")
        ld_qt = sname("ld_qt"); ld_w = sname("ld_w"); ld_wd = sname("ld_wd")
        ld_xn2 = sname("ld_xn2")
        s_cast = sname("s_cast"); s_ms = sname("s_ms")
        c_xb = sname("c_xb"); s_trec = sname("s_trec"); s_xte = sname("s_xte")
        s_lat = sname("s_lat"); s_kvl = sname("s_kvl"); s_kmm = sname("s_kmm")
        s_rope = sname("s_rope"); s_vmm = sname("s_vmm"); s_vev = sname("s_vev")
        c_xob = sname("c_xob"); s_troc = sname("s_troc"); s_xoev = sname("s_xoev")
        s_qmm = sname("s_qmm"); s_qle = sname("s_qle")
        s_qbmm = sname("s_qbmm"); s_qrope = sname("s_qrope")
        s_squad = sname("s_squad"); s_exp = sname("s_exp"); s_msk = sname("s_msk")
        s_o = sname("s_o"); s_oev = sname("s_oev"); s_sum = sname("s_sum"); s_row = sname("s_row")
        s_obc = sname("s_obc"); s_norm = sname("s_norm")
        s_epimm = sname("s_epimm"); s_xsum = sname("s_xsum"); s_stat = sname("s_stat")
        s_istd = sname("s_istd"); s_nb = sname("s_nb"); s_app = sname("s_app")
        s_xnb = sname("s_xnb"); s_xnt = sname("s_xnt"); s_xnte = sname("s_xnte")
        s_spill = sname("s_spill")
        s_wcast = sname("s_wcast"); s_wtr = sname("s_wtr"); s_wev = sname("s_wev")
        s_gu = sname("s_gu"); s_se = sname("s_se"); s_sil = sname("s_sil")
        s_dwc = sname("s_dwc"); s_dwt = sname("s_dwt"); s_dwev = sname("s_dwev")
        s_dmm = sname("s_dmm"); s_fsum = sname("s_fsum"); s_stat2 = sname("s_stat2")
        s_istd2 = sname("s_istd2"); s_nb2 = sname("s_nb2"); s_app2 = sname("s_app2")
        s_y = sname("s_y")
        sd = sname("sd"); sa = sname("sa")

        NCAST = 15

        block = EE(nc.Block())

        @block.sync
        def _(sync):
            sync.dma_start(out=identB[:], in_=ident).then_inc(lc_i, 16)
            sync.dma_start(out=sstage[:], in_=wqakva.rearrange("(c p) m -> p c m", p=128)).then_inc(lc_qa, 16)
            sync.dma_start(out=s3stage[:, 0, :], in_=wk_i).then_inc(lc_k1, 16)
            sync.dma_start(out=s3stage[:, 1, :], in_=wks_i).then_inc(lc_k2, 16)
            sync.dma_start(out=s3stage[:, 2, :], in_=wv_i).then_inc(lc_k3, 16)
            sync.dma_start(out=bstage[:], in_=bout_i).then_inc(lc_bo, 16)
            sync.dma_start(out=s2stage[:], in_=wqb_i).then_inc(lc_qb, 16)
            sync.wait_ge(s_cast, 6)
            sync.dma_start(out=s2stage[:], in_=wqbs_i).then_inc(lc_qb, 16)
            for r in range(8):
                if r > 0:
                    sync.wait_ge(s_cast, 7 + r)
                sync.dma_start(out=wstage[:], in_=wout_r[:, 2 * r:2 * r + 2, :]).then_inc(ld_c, 16)
            for s in range(NSL):
                if s % 2 == 0:
                    if s >= 2:
                        sync.wait_ge(c_xb, s - 1)
                    sync.dma_start(out=xf[:, s % 2, :], in_=x_full_r[s]).then_inc(ld_x, 16)
            for i in range(16):
                if i < 2:
                    sync.wait_ge(c_xb, NSL)
                else:
                    sync.wait_ge(c_xob, i - 1)
                sync.dma_start(out=xf[:, i % 2, :], in_=xo_r[i]).then_inc(lxo[i % 2], 16)
            sync.wait_ge(c_xb, NSL)
            sync.wait_ge(c_xob, 16)
            sync.wait_ge(s_trec, 8 * NSL)
            sync.wait_ge(s_troc, 128)
            sync.wait_ge(s_xte, 8 * NSL)
            sync.wait_ge(s_xoev, 128)
            sync.wait_ge(s_lat, NSL)
            sync.wait_ge(s_kvl, NSL)
            sync.wait_ge(s_rope, 2 * NSL)
            sync.wait_ge(s_vev, NSL)
            sync.wait_ge(s_qmm, 8)
            sync.wait_ge(s_qle, 8)
            sync.wait_ge(s_cast, NCAST)
            for j in range(SLOTS):
                if j >= 1:
                    sync.wait_ge(s_msk, TOP_END[j])
                if j >= 2:
                    sync.wait_ge(s_qrope, 8 * (j - 1))
                sync.dma_start(out=mask_w[:], in_=masks[:, j * 8 * QC:(j + 1) * 8 * QC]).then_inc(ld_msk, 16)
                sync.dma_start(out=qc_w[:, j % 2, :], in_=ctab_q[:, j * QC:(j + 1) * QC]).then_inc(lqt[j % 2], 16)
                sync.dma_start(out=qs_w[:, j % 2, :], in_=stab_q[:, j * QC:(j + 1) * QC]).then_inc(lqt[j % 2], 16)
                for tt in range(2):
                    it = 2 * j + tt
                    if j >= 1:
                        sync.wait_ge(s_app, it - 1)
                    sync.dma_start(out=xres[:, tt, :], in_=xo_r[it]).then_inc(lxr[tt], 16)
            sync.wait_ge(s_o, NU)
            sync.wait_ge(s_exp, NU)
            sync.wait_ge(s_msk, TOP_END[SLOTS])
            sync.wait_ge(s_qbmm, 64)
            sync.wait_ge(s_qrope, 64)
            sync.wait_ge(s_norm, 16 * SLOTS)
            sync.wait_ge(s_obc, 16 * 32)
            sync.wait_ge(s_epimm, 16)
            sync.wait_ge(s_xsum, 16)
            sync.wait_ge(s_app, 16)
            sync.wait_ge(s_xnb, 16)
            sync.wait_ge(s_xnt, 16)
            sync.wait_ge(s_xnte, 32)
            sync.wait_ge(ssp[0], 16 * 8)
            sync.wait_ge(ssp[1], 16 * 8)
            for bb in range(2):
                for hc in range(HCH):
                    k = bb * HCH + hc
                    if k >= 2:
                        sync.wait_ge(s_wcast, 2 * (k - 1))
                    sync.dma_start(out=wgi_s[:, hc % 2, :], in_=wg_r[hc]).then_inc(lw[hc % 2], 16)
                    sync.dma_start(out=wui_s[:, hc % 2, :], in_=wu_r[hc]).then_inc(lw[hc % 2], 16)
                for ti in range(8):
                    gi = bb * 8 + ti

                    if gi >= 2:
                        sync.wait_ge(s_fsum, gi - 1)
                    sync.dma_start(out=xnr[:, gi % 2, :], in_=spill_r[gi]).then_inc(lxn[gi % 2], 16)

        @block.gpsimd
        def _(gpsimd):
            for s in range(NSL):
                if s % 2 == 1:
                    if s >= 2:
                        gpsimd.wait_ge(c_xb, s - 1)
                    gpsimd.dma_start(out=xf[:, s % 2, :], in_=x_full_r[s]).then_inc(ld_x2, 16)
            for j in range(SLOTS):
                for r in range(4):
                    q3 = 4 * j + r
                    gpsimd.wait_ge(s_sum, q3 + 1)
                    if q3 >= 2:
                        gpsimd.wait_ge(s_norm, 4 * (q3 - 1))
                    gpsimd.dma_start(out=srow_d[q3:q3 + 1, :], in_=srow_t[64:65, :]).then_inc(s_row, 16)
                    gpsimd.wait_ge(s_row, 16 * (q3 + 1))
                    gpsimd.dma_start(out=obc[:, q3 % 2, :],
                                     in_=srow_d[q3:q3 + 1, :].partition_broadcast(64)).then_inc(s_obc, 16)
                for tt in range(2):
                    it = 2 * j + tt
                    gpsimd.wait_ge(s_app, it + 1)
                    gpsimd.dma_start(out=spill_r[it], in_=xn_f[:, tt, :]).then_inc(ssp[tt], 16)
            for hc in range(HCH):
                if hc >= 2:
                    gpsimd.wait_ge(s_dwc, hc - 1)
                gpsimd.dma_start(out=wdi_s[:, hc % 2, :, :],
                                 in_=wd_i[:, hc * 128:(hc + 1) * 128].rearrange("(c p) h -> p c h", p=128)
                                 ).then_inc(lwd[hc % 2], 16)
            for gi in range(16):
                gpsimd.wait_ge(s_app2, gi + 1)
                gpsimd.dma_start(out=y_r[gi], in_=fsb[:, gi % 2, :]).then_inc(sy[gi % 2], 16)
            gpsimd.wait_ge(sy[0], 16 * 8)
            gpsimd.wait_ge(sy[1], 16 * 8)

        @block.scalar
        def _(scalar):
            san = [0]
            def sai(instr):
                instr.then_inc(sa, 1); san[0] += 1
                return instr
            for s in range(NSL):
                if s >= 2:
                    scalar.wait_ge(s_rope, 2 * (s - 1))
                scalar.dma_start(out=ct_w[:, s % 2, :], in_=ctab_kv[:, 128 * s:128 * (s + 1)]).then_inc(lt[s % 2], 16)
                scalar.dma_start(out=st_w[:, s % 2, :], in_=stab_kv[:, 128 * s:128 * (s + 1)]).then_inc(lt[s % 2], 16)
                scalar.wait_ge(s_lat, s + 1)
                scalar.activation(out=kvlatT[:, 128 * s:128 * (s + 1)], in_=pA2[0:KVL, 0:128],
                                  func=AF.Copy).then_inc(s_kvl, 1)
            for j in range(SLOTS):
                scalar.wait_ge(s_qmm, j + 1)
                scalar.activation(out=qlatT[:, j, :], in_=pA2[0:QL, 256:256 + QC],
                                  func=AF.Copy, scale=0.125).then_inc(s_qle, 1)
            for ui, (j, g, hh, kc, top, q3, fk, lk) in enumerate(UNITS):
                scalar.wait_ge(s_squad, ui + 1)
                if ui >= 3:
                    scalar.wait_ge(s_o, ui - 2)
                scalar.activation(out=Pq[:, ui % 3, :], in_=sq[ui % 3][:, :],
                                  func=AF.Exp).then_inc(s_exp, 1)
                if lk and q3 % 4 == 3:
                    for tt in range(2):
                        it = 2 * j + tt
                        scalar.wait_ge(s_stat, it + 1)
                        sai(scalar.activation(out=istd[:], in_=veps[:], func=AF.Ln))
                        scalar.wait_ge(sa, san[0])
                        scalar.activation(out=istd[:], in_=istd[:], func=AF.Exp,
                                          scale=-0.5).then_inc(s_istd, 1)
            for bb in range(2):
                for hc in range(HCH):
                    for tt4 in range(2):
                        f = hc * 2 + tt4
                        fb = bb * 48 + f
                        scalar.wait_ge(s_gu, fb + 1)
                        if fb >= 2:
                            scalar.wait_ge(s_sil, fb - 1)
                        scalar.activation(out=tb[:, fb % 2, :], in_=pA[f % 3][:, 0:512],
                                          func=AF.Tanh, scale=0.5 / 127.0).then_inc(s_se, 1)
                for ti in range(8):
                    gi = bb * 8 + ti
                    scalar.wait_ge(s_stat2, gi + 1)
                    sai(scalar.activation(out=istd[:], in_=veps[:], func=AF.Ln))
                    scalar.wait_ge(sa, san[0])
                    scalar.activation(out=istd[:], in_=istd[:], func=AF.Exp,
                                      scale=-0.5).then_inc(s_istd2, 1)

        @block.vector
        def _(vector):
            sdn = [0]
            def sdi(instr):
                instr.then_inc(sd, 1); sdn[0] += 1
                return instr
            vector.wait_ge(lc_qa, 16)
            vector.tensor_copy(out=wqakvaB[:], in_=sstage[:]).then_inc(s_cast, 1)
            vector.wait_ge(lc_k1, 16)
            vector.tensor_copy(out=wkB[:], in_=s3stage[:, 0, :]).then_inc(s_cast, 1)
            vector.wait_ge(lc_k2, 16)
            vector.tensor_copy(out=wksB[:], in_=s3stage[:, 1, :]).then_inc(s_cast, 1)
            vector.wait_ge(lc_k3, 16)
            vector.tensor_copy(out=wvB[:], in_=s3stage[:, 2, :]).then_inc(s_cast, 1)
            vector.wait_ge(lc_bo, 16)
            vector.tensor_copy(out=boutB[:], in_=bstage[:]).then_inc(s_cast, 1)
            vector.wait_ge(lc_qb, 16)
            vector.tensor_copy(out=wqbB[:], in_=s2stage[:]).then_inc(s_cast, 1)
            vector.wait_ge(lc_qb, 32)
            vector.tensor_copy(out=wqbsB[:], in_=s2stage[:]).then_inc(s_cast, 1)
            for r in range(8):
                vector.wait_ge(ld_c, 16 * (r + 1))
                vector.tensor_copy(out=woutB[:, 2 * r:2 * r + 2, :], in_=wstage[:]).then_inc(s_cast, 1)
            vector.memset(Vt[:], 1.0).then_inc(s_ms, 1)
            vector.memset(onesB[:], 1.0).then_inc(s_ms, 1)
            for s in range(NSL):
                if s % 2 == 0:
                    vector.wait_ge(ld_x, 16 * (s // 2 + 1))
                else:
                    vector.wait_ge(ld_x2, 16 * ((s + 1) // 2))
                if s >= 2:
                    vector.wait_ge(s_trec, 8 * (s - 1))
                vector.tensor_copy(out=xb[:, s % 2, :], in_=xf[:, s % 2, :]).then_inc(c_xb, 1)
                for ec in range(ECH):
                    vector.wait_ge(s_trec, 8 * s + ec + 1)
                    if s >= 2 and ec == 0:
                        vector.wait_ge(s_lat, s - 1)
                    vector.tensor_copy(out=xTw[:, s % 2, ec, :], in_=pTR[ec % 2][:, 0:128]).then_inc(s_xte, 1)
                for g in range(2):
                    vector.wait_ge(s_kmm, 2 * s + g + 1)
                    vector.wait_ge(lt[s % 2], 32 * (s // 2 + 1))
                    sdi(vector.tensor_tensor(out=rtmp1[:, 0:128], in0=pA0[:, 512 * g:512 * g + 128],
                                             in1=ct_w[:, s % 2, :], op=ALU.mult))
                    sdi(vector.tensor_tensor(out=rtmp2[:, 0:128], in0=pA1[:, 512 * g:512 * g + 128],
                                             in1=st_w[:, s % 2, :], op=ALU.mult))
                    vector.wait_ge(sd, sdn[0])
                    vector.tensor_add(out=KTp[g][:, 128 * s:128 * (s + 1)],
                                      in0=rtmp1[:, 0:128], in1=rtmp2[:, 0:128]).then_inc(s_rope, 1)
                vector.wait_ge(s_vmm, s + 1)
                if s == 0:
                    vector.wait_ge(s_ms, 2)
                vector.tensor_copy(out=Vt[:, s, :, 0:64],
                                   in_=pA2[:, 512:768].rearrange("p (h d) -> p h d", h=4)
                                   ).then_inc(s_vev, 1)
            for i in range(16):
                vector.wait_ge(lxo[i % 2], 16 * (i // 2 + 1))
                if i >= 2:
                    vector.wait_ge(s_troc, 8 * (i - 1))
                vector.tensor_copy(out=xb[:, i % 2, :], in_=xf[:, i % 2, :]).then_inc(c_xob, 1)
                tt = i % 2
                for ec in range(ECH):
                    vector.wait_ge(s_troc, 8 * i + ec + 1)
                    if i >= 2 and ec == 0:
                        vector.wait_ge(s_qmm, i // 2)
                    vector.tensor_copy(out=xoT[:, ec, tt * 128:(tt + 1) * 128],
                                       in_=pTR[ec % 2][:, 0:128]).then_inc(s_xoev, 1)
            def dve_recip_piece(q3, c):
                if c == 0:
                    if q3 >= 1:
                        vector.wait_ge(s_row, 16 * q3)
                    vector.wait_ge(s_oev, q3 + 1)
                with nc.allow_low_precision(reason="1/softmax-sums in bf16; rvr margin 20x"):
                    r = vector.reciprocal(out=srow_t[64:65, c * 256:(c + 1) * 256],
                                          in_=osb[64:65, q3 % 2, c * 256:(c + 1) * 256])
                if c == 3:
                    r.then_inc(s_sum, 1)
                else:
                    sdi(r)

            def dve_normblock(jj, q3):
                vector.wait_ge(s_obc, 16 * (q3 + 1))
                if jj >= 1:
                    vector.wait_ge(s_epimm, 2 * jj)
                for ho in range(4):
                    c = (q3 % 4) * 4 + ho
                    vector.tensor_tensor(out=OT[:, c, :],
                                         in0=osb[0:64, q3 % 2, ho * QC:(ho + 1) * QC],
                                         in1=obc[:, q3 % 2, (2 * (ho // 2) + (ho % 2)) * QC:
                                                 (2 * (ho // 2) + (ho % 2) + 1) * QC],
                                         op=ALU.mult).then_inc(s_norm, 1)

            prev_slot = -1
            agenda = {}
            for ui, (j, g, hh, kc, top, q3, fk, lk) in enumerate(UNITS):
                li = ui - U_BEFORE[j]
                kj = kcj(j)
                if j != prev_slot:
                    prev_slot = j
                    agenda = {}
                    for r in range(3):
                        q3r = 4 * j + r
                        L = (r + 1) * kj - 1
                        for c in range(4):
                            agenda.setdefault(L + 1 + c, []).append(
                                lambda q3r=q3r, c=c: dve_recip_piece(q3r, c))
                        agenda.setdefault(L + 6, []).append(
                            lambda j=j, q3r=q3r: dve_normblock(j, q3r))
                    for pr in range(8):
                        vector.wait_ge(s_qbmm, 8 * j + pr + 1)
                        vector.wait_ge(lqt[j % 2], 32 * (j // 2 + 1))
                        src = sq[pr % 2]
                        sdi(vector.tensor_tensor(out=rtmp1[:, 0:QC], in0=src[:, 0:QC],
                                                 in1=qc_w[:, j % 2, :], op=ALU.mult))
                        sdi(vector.tensor_tensor(out=rtmp2[:, 0:QC], in0=src[:, 512:512 + QC],
                                                 in1=qs_w[:, j % 2, :], op=ALU.mult))
                        vector.wait_ge(sd, sdn[0])
                        vector.tensor_add(out=qTw[:, pr, :], in0=rtmp1[:, 0:QC],
                                          in1=rtmp2[:, 0:QC]).then_inc(s_qrope, 1)
                if top >= 0:
                    vector.wait_ge(s_exp, ui + 1)
                    vector.wait_ge(ld_msk, 16 * (j + 1))
                    ki = kc - 8 * j
                    mk = mask_w[:, ki * QC:(ki + 1) * QC]
                    pv = Pq[:, ui % 3, :].rearrange("p (r n) -> p r n", r=4)
                    vector.tensor_tensor(out=pv, in0=pv,
                                         in1=mk.unsqueeze(1).broadcast_to([128, 4, QC]),
                                         op=ALU.mult).then_inc(s_msk, 1)
                if lk:
                    vector.wait_ge(s_o, ui + 1)
                    if q3 >= 2:
                        vector.wait_ge(s_norm, 4 * (q3 - 1))
                    sdi(vector.tensor_copy(out=osb[:, q3 % 2, 0:512], in_=pT0[:, :]))
                    vector.tensor_copy(out=osb[:, q3 % 2, 512:1024], in_=pT1[:, :]).then_inc(s_oev, 1)
                    if q3 % 4 == 3:
                        for c in range(4):
                            dve_recip_piece(q3, c)
                        dve_normblock(j, q3)
                for fn in agenda.pop(li, []):
                    fn()
                if lk and q3 % 4 == 3:
                    for tt in range(2):
                        it = 2 * j + tt
                        vector.wait_ge(s_epimm, it + 1)
                        vector.wait_ge(lxr[tt], 16 * (j + 1))
                        sdi(vector.tensor_add(out=xres[:, tt, :], in0=sq[tt][:, :], in1=xres[:, tt, :]))
                        vector.wait_ge(sd, sdn[0])
                        if it >= 1:
                            vector.wait_ge(s_istd, it)
                        sdi(vector.bn_stats(stat[:, 0:6], xres[:, tt, 0:512]))
                        vector.bn_stats(stat[:, 6:12], xres[:, tt, 512:1024]).then_inc(s_xsum, 1)
                        vector.wait_ge(sd, sdn[0])
                        vector.wait_ge(s_xsum, it + 1)
                        sdi(vector.bn_aggr(mv[:], stat[:, 0:12]))
                        vector.wait_ge(sd, sdn[0])
                        vector.tensor_scalar_add(veps[:], mv[:, 1:2], EPS).then_inc(s_stat, 1)
                        vector.wait_ge(s_istd, it + 1)
                        vector.tensor_scalar(out=nbias[:], in0=istd[:], scalar1=mv[:, 0:1],
                                             scalar2=-1.0, op0=ALU.mult, op1=ALU.mult).then_inc(s_nb, 1)
                        vector.wait_ge(s_nb, it + 1)
                        if j >= 1:
                            vector.wait_ge(ssp[tt], 16 * (it // 2))
                        vector.tensor_scalar(out=xn_f[:, tt, :], in0=xres[:, tt, :],
                                             scalar1=istd[:], scalar2=nbias[:],
                                             op0=ALU.mult, op1=ALU.add).then_inc(s_app, 1)
                        vector.wait_ge(s_app, it + 1)
                        if j >= 1:
                            vector.wait_ge(s_xnt, it - 1)
                        vector.tensor_copy(out=xn_b[:, tt, :], in_=xn_f[:, tt, :]).then_inc(s_xnb, 1)
                        vector.wait_ge(s_xnt, it + 1)
                        col = j * QC + tt * 128
                        for hf in range(2):
                            vector.tensor_copy(
                                out=xnT[:, 4 * hf:4 * hf + 4, col:col + 128],
                                in_=pTR[hf][:, :].rearrange("p (c n) -> p c n", c=4)
                                ).then_inc(s_xnte, 1)
            for hc in range(HCH):
                vector.wait_ge(lwd[hc % 2], 16 * (hc // 2 + 1))
                if hc >= 2:
                    vector.wait_ge(s_dwt, 2 * (hc - 1))
                vector.tensor_copy(out=wdb_s[:, hc % 2, :, :], in_=wdi_s[:, hc % 2, :, :]).then_inc(s_dwc, 1)
                for grp in range(2):
                    vector.wait_ge(s_dwt, 2 * hc + grp + 1)
                    vector.tensor_copy(out=wdT[:, hc, grp * 512:(grp + 1) * 512],
                                       in_=pTR[grp][:, :]).then_inc(s_dwev, 1)
            for bb in range(2):
                for hc in range(HCH):
                    k = bb * HCH + hc
                    vector.wait_ge(lw[hc % 2], 32 * (bb * 12 + hc // 2 + 1))
                    if k >= 2:
                        vector.wait_ge(s_wtr, 4 * (k - 1))
                    vector.tensor_copy(out=wgb_s[:, hc % 2, :], in_=wgi_s[:, hc % 2, :]).then_inc(s_wcast, 1)
                    vector.tensor_copy(out=wub_s[:, hc % 2, :], in_=wui_s[:, hc % 2, :]).then_inc(s_wcast, 1)
                    for grp in range(4):
                        vector.wait_ge(s_wtr, 4 * k + grp + 1)
                        dst = (wgT if grp < 2 else wuT)
                        hf = grp % 2
                        vector.tensor_copy(out=dst[:, hc % 2, 4 * hf:4 * hf + 4, :],
                                           in_=pTR[grp % 2][:, :].rearrange("p (c n) -> p c n", c=4)
                                           ).then_inc(s_wev, 1)
                    for tt4 in range(2):
                        f = hc * 2 + tt4
                        fb = bb * 48 + f
                        vector.wait_ge(s_se, fb + 1)
                        u1 = vector.tensor_scalar_add(ub[:, fb % 2, :], tb[:, fb % 2, :], 1.0)
                        sdi(u1)
                        vector.wait_ge(sd, sdn[0])
                        h1 = vector.tensor_tensor(out=rb[:, fb % 2, :], in0=ub[:, fb % 2, :],
                                                  in1=pA[f % 3][:, 0:512], op=ALU.mult)
                        sdi(h1)
                        vector.wait_ge(sd, sdn[0])
                        vector.tensor_tensor(out=hactT[:, hc, tt4 * 512:(tt4 + 1) * 512],
                                             in0=rb[:, fb % 2, :], in1=pA[f % 3][:, 512:1024],
                                             op=ALU.mult).then_inc(s_sil, 1)
                for ti in range(8):
                    gi = bb * 8 + ti
                    vector.wait_ge(s_dmm, gi + 1)
                    vector.wait_ge(lxn[gi % 2], 16 * (gi // 2 + 1))
                    if gi >= 2:
                        vector.wait_ge(sy[gi % 2], 16 * (gi // 2))
                    if gi == 0:
                        vector.wait_ge(s_app, 16)
                    vector.scalar_tensor_tensor(out=fsb[:, gi % 2, :], in0=pA[ti % 3][:, :],
                                                scalar=DQ3, in1=xnr[:, gi % 2, :],
                                                op0=ALU.mult, op1=ALU.add).then_inc(s_fsum, 1)
                    vector.wait_ge(s_fsum, gi + 1)
                    if gi >= 1:
                        vector.wait_ge(s_istd2, gi)
                    sdi(vector.bn_stats(stat[:, 0:6], fsb[:, gi % 2, 0:512]))
                    sdi(vector.bn_stats(stat[:, 6:12], fsb[:, gi % 2, 512:1024]))
                    vector.wait_ge(sd, sdn[0])
                    sdi(vector.bn_aggr(mv[:], stat[:, 0:12]))
                    vector.wait_ge(sd, sdn[0])
                    vector.tensor_scalar_add(veps[:], mv[:, 1:2], EPS).then_inc(s_stat2, 1)
                    vector.wait_ge(s_istd2, gi + 1)
                    vector.tensor_scalar(out=nbias[:], in0=istd[:], scalar1=mv[:, 0:1],
                                         scalar2=-1.0, op0=ALU.mult, op1=ALU.mult).then_inc(s_nb2, 1)
                    vector.wait_ge(s_nb2, gi + 1)
                    vector.tensor_scalar(out=fsb[:, gi % 2, :], in0=fsb[:, gi % 2, :],
                                         scalar1=istd[:], scalar2=nbias[:],
                                         op0=ALU.mult, op1=ALU.add).then_inc(s_app2, 1)

        @block.tensor
        def _(tensor):
            for s in range(NSL):
                tensor.wait_ge(c_xb, s + 1)
                if s == 0:
                    tensor.wait_ge(lc_i, 16)
                for ec in range(ECH):
                    w = 8 * s + ec - 1
                    if w > 0:
                        tensor.wait_ge(s_xte, w)
                    mm = tensor.matmul(pTR[ec % 2][:, 0:128],
                                       xb[:, s % 2, ec * 128:(ec + 1) * 128],
                                       identB[:], start=True, stop=True)
                    mm.then_inc(s_trec, 1)
                tensor.wait_ge(s_xte, 8 * (s + 1))
                if s >= 1:
                    tensor.wait_ge(s_kvl, s)
                if s == 0:
                    tensor.wait_ge(s_cast, NCAST)
                for ec in range(ECH):
                    mm = tensor.matmul(pA2[0:KVL, 0:128], wqakvaB[:, ec, 64:96],
                                       xTw[:, s % 2, ec, :], start=(ec == 0), stop=(ec == ECH - 1))
                mm.then_inc(s_lat, 1)
                tensor.wait_ge(s_kvl, s + 1)
                for g in range(2):
                    w = 2 * s + g - 1
                    if w > 0:
                        tensor.wait_ge(s_rope, w)
                    tensor.matmul(pA0[:, 512 * g:512 * g + 128], wkB[:, g * 128:(g + 1) * 128],
                                  kvlatT[:, 128 * s:128 * (s + 1)], start=True, stop=True)
                    mm = tensor.matmul(pA1[:, 512 * g:512 * g + 128], wksB[:, g * 128:(g + 1) * 128],
                                       kvlatT[:, 128 * s:128 * (s + 1)], start=True, stop=True)
                    mm.then_inc(s_kmm, 1)
                if s >= 1:
                    tensor.wait_ge(s_vev, s)
                mm = tensor.matmul(pA2[:, 512:768],
                                   kvlatT[:, 128 * s:128 * (s + 1)],
                                   wvB[:], start=True, stop=True)
                mm.then_inc(s_vmm, 1)
            for i in range(16):
                tensor.wait_ge(c_xob, i + 1)
                tt = i % 2
                for ec in range(ECH):
                    w = 8 * i + ec - 1
                    if w > 0:
                        tensor.wait_ge(s_xoev, w)
                    mm = tensor.matmul(pTR[ec % 2][:, 0:128], xb[:, i % 2, ec * 128:(ec + 1) * 128],
                                       identB[:], start=True, stop=True)
                    mm.then_inc(s_troc, 1)
                if tt == 1:
                    jslot = i // 2
                    tensor.wait_ge(s_xoev, 8 * (i + 1))
                    if jslot == 0:
                        tensor.wait_ge(s_kvl, NSL)
                    if jslot >= 1:
                        tensor.wait_ge(s_qle, jslot)
                    for ec in range(ECH):
                        mm = tensor.matmul(pA2[0:QL, 256:256 + QC], wqakvaB[:, ec, 0:QL],
                                           xoT[:, ec, :], start=(ec == 0), stop=(ec == ECH - 1))
                    mm.then_inc(s_qmm, 1)
            for j in range(SLOTS):
                for pr in range(8):
                    if pr == 0:
                        tensor.wait_ge(s_qle, j + 1)
                        tensor.wait_ge(s_exp, U_BEFORE[j])
                        tensor.wait_ge(s_xsum, 2 * j)
                        if j == 0:
                            tensor.wait_ge(s_rope, 2 * NSL)
                    w = 8 * j + pr - 1
                    if w > 0:
                        tensor.wait_ge(s_qrope, w)
                    dst = sq[pr % 2]
                    tensor.matmul(dst[:, 0:QC], wqbB[:, pr * 128:(pr + 1) * 128],
                                  qlatT[:, j, :], start=True, stop=True)
                    mm = tensor.matmul(dst[:, 512:512 + QC], wqbsB[:, pr * 128:(pr + 1) * 128],
                                       qlatT[:, j, :], start=True, stop=True)
                    mm.then_inc(s_qbmm, 1)
                ujs = list(range(U_BEFORE[j], U_BEFORE[j + 1]))

                def emit_S(ui, j=j):
                    jj, g, hh, kc, top, q3, fk, lk = UNITS[ui]
                    if ui == U_BEFORE[j]:
                        tensor.wait_ge(s_qrope, 8 * (j + 1))
                        if j == 0:
                            tensor.wait_ge(s_vev, NSL)
                            tensor.wait_ge(s_qmm, SLOTS)
                    if ui >= 3:
                        tensor.wait_ge(s_exp, ui - 2)
                    pr0 = 4 * g + 2 * hh
                    tensor.matmul(sq[ui % 3][:, 0:512],
                                  KTp[g][0:64, 128 * kc:128 * (kc + 1)],
                                  qTw[0:64, pr0:pr0 + 2, :], start=True, stop=True,
                                  tile_position=(0, 0), skip_group_check=True)
                    mm = tensor.matmul(sq[ui % 3][:, 512:1024],
                                       KTp[g][64:128, 128 * kc:128 * (kc + 1)],
                                       qTw[64:128, pr0:pr0 + 2, :], start=True, stop=True,
                                       tile_position=(64, 0), skip_group_check=True)
                    mm.then_inc(s_squad, 1)

                def emit_O(ui, j=j):
                    jj, g, hh, kc, top, q3, fk, lk = UNITS[ui]
                    if fk:
                        if q3 >= 1:
                            tensor.wait_ge(s_oev, q3)
                        if q3 == 0:
                            tensor.wait_ge(s_xoev, 128)
                        if j >= 1 and q3 % 4 == 0:
                            tensor.wait_ge(s_xnte, 4 * j)
                    if top >= 0:
                        tensor.wait_ge(s_msk, top + 1)
                    else:
                        tensor.wait_ge(s_exp, ui + 1)
                    for bi in range(2):
                        kvh = 2 * g + bi
                        mm = tensor.matmul(obank(q3, bi)[0:65, 0:512],
                                           Vt[:, kc, kvh, :],
                                           Pq[:, ui % 3, 512 * bi:512 * (bi + 1)],
                                           start=fk, stop=lk, skip_group_check=True)
                    mm.then_inc(s_o, 1)

                emit_S(ujs[0])
                emit_S(ujs[1])
                for idx, ui in enumerate(ujs):
                    if idx + 2 < len(ujs):
                        emit_S(ujs[idx + 2])
                    emit_O(ui)
                for tt in range(2):
                    if tt == 0:
                        tensor.wait_ge(s_norm, 16 * j + 12)
                        tensor.wait_ge(s_exp, U_BEFORE[j + 1])
                    for c in range(12):
                        for eh in range(2):
                            tensor.matmul(sq[tt][:, eh * 512:(eh + 1) * 512],
                                          OT[:, c, tt * 128:(tt + 1) * 128],
                                          woutB[:, c, eh * 512:(eh + 1) * 512],
                                          start=(c == 0), stop=False, skip_group_check=True)
                    if tt == 0:
                        tensor.wait_ge(s_norm, 16 * (j + 1))
                    for c in range(12, 16):
                        for eh in range(2):
                            tensor.matmul(sq[tt][:, eh * 512:(eh + 1) * 512],
                                          OT[:, c, tt * 128:(tt + 1) * 128],
                                          woutB[:, c, eh * 512:(eh + 1) * 512],
                                          start=False, stop=False, skip_group_check=True)
                    for eh in range(2):
                        mm = tensor.matmul(sq[tt][:, eh * 512:(eh + 1) * 512],
                                           onesB[:], boutB[:, eh * 512:(eh + 1) * 512],
                                           start=False, stop=True, skip_group_check=True)
                    mm.then_inc(s_epimm, 1)
                for tt in range(2):
                    it = 2 * j + tt
                    tensor.wait_ge(s_xnb, it + 1)
                    if tt == 0:
                        tensor.wait_ge(s_oev, 4 * (j + 1))
                    else:
                        tensor.wait_ge(s_xnte, 4 * j + 2)
                    for ec in range(ECH):
                        hf = ec // 4
                        mm = tensor.matmul(pTR[hf][:, (ec % 4) * 128:(ec % 4 + 1) * 128],
                                           xn_b[:, tt, ec * 128:(ec + 1) * 128],
                                           identB[:], start=True, stop=True, skip_group_check=True)
                    mm.then_inc(s_xnt, 1)
            for hc in range(HCH):
                tensor.wait_ge(s_dwc, hc + 1)
                if hc == 0:
                    tensor.wait_ge(s_xnte, 4 * SLOTS)
                for grp in range(2):
                    w = 2 * hc + grp - 1
                    if w > 0:
                        tensor.wait_ge(s_dwev, w)
                    for q in range(4):
                        dc = grp * 4 + q
                        mm = tensor.matmul(pTR[grp][:, q * 128:(q + 1) * 128],
                                           wdb_s[:, hc % 2, dc, :], identB[:],
                                           start=True, stop=True)
                    mm.then_inc(s_dwt, 1)
            for bb in range(2):
                for hc in range(HCH):
                    k = bb * HCH + hc
                    tensor.wait_ge(s_wcast, 2 * (k + 1))
                    for grp in range(4):
                        if k == 0 and grp == 0:
                            tensor.wait_ge(s_dwev, 2 * HCH)
                        w = 4 * k + grp - 1
                        if w > 0:
                            tensor.wait_ge(s_wev, w)
                        src = (wgb_s if grp < 2 else wub_s)
                        hf = grp % 2
                        for q in range(4):
                            ec = hf * 4 + q
                            mm = tensor.matmul(pTR[grp % 2][:, q * 128:(q + 1) * 128],
                                               src[:, hc % 2, ec * 128:(ec + 1) * 128],
                                               identB[:], start=True, stop=True)
                        mm.then_inc(s_wtr, 1)
                    for tt4 in range(2):
                        f = hc * 2 + tt4
                        fb = bb * 48 + f
                        if tt4 == 0:
                            tensor.wait_ge(s_wev, 4 * (k + 1))
                        if fb >= 3:
                            tensor.wait_ge(s_sil, fb - 2)
                        if bb == 1 and f == 0:
                            tensor.wait_ge(s_fsum, 8)
                        if fb == 0:
                            tensor.wait_ge(s_xnte, 4 * SLOTS)
                        for ec in range(ECH):
                            tensor.matmul(pA[f % 3][:, 0:512], wgT[:, hc % 2, ec, :],
                                          xnT[:, ec, bb * 1024 + tt4 * 512:bb * 1024 + (tt4 + 1) * 512],
                                          start=(ec == 0), stop=(ec == ECH - 1))
                        for ec in range(ECH):
                            mm = tensor.matmul(pA[f % 3][:, 512:1024], wuT[:, hc % 2, ec, :],
                                               xnT[:, ec, bb * 1024 + tt4 * 512:bb * 1024 + (tt4 + 1) * 512],
                                               start=(ec == 0), stop=(ec == ECH - 1))
                        mm.then_inc(s_gu, 1)
                for ti in range(8):
                    gi = bb * 8 + ti
                    tensor.wait_ge(s_sil, 48 * (bb + 1))
                    if gi == 0:
                        tensor.wait_ge(s_dwev, 2 * HCH)
                    if gi >= 3:
                        tensor.wait_ge(s_fsum, gi - 2)
                    for eh in range(2):
                        for hc in range(HCH):
                            mm = tensor.matmul(pA[ti % 3][:, eh * 512:(eh + 1) * 512],
                                               hactT[:, hc, ti * 128:(ti + 1) * 128],
                                               wdT[:, hc, eh * 512:(eh + 1) * 512],
                                               start=(hc == 0), stop=(hc == HCH - 1),
                                               skip_group_check=True)
                    mm.then_inc(s_dmm, 1)

    return nc


def _rope_tables(positions):
    d = np.arange(128)
    i = (d % 64) // 2
    inv = 1.0 / (10000.0 ** (2.0 * i / 64.0))
    ang = np.outer(inv, positions.astype(np.float64))
    c = np.cos(ang).astype(np.float32)
    s = np.sin(ang).astype(np.float32)
    sign = np.where(d % 2 == 0, -1.0, 1.0).astype(np.float32)[:, None]
    return np.ascontiguousarray(c), np.ascontiguousarray(s * sign)


def kernel(L0_attn_x, L0_attn_Wqa, L0_attn_Wqb, L0_attn_Wkva, L0_attn_Wkvb,
           L0_attn_Wout, L0_attn_bout, L0_ffn_w_gate, L0_ffn_w_up, L0_ffn_w_down):
    x = np.ascontiguousarray(np.asarray(L0_attn_x, dtype=np.float32))
    Wqa = np.asarray(L0_attn_Wqa, dtype=np.float32)
    Wqb = np.asarray(L0_attn_Wqb, dtype=np.float32)
    Wkva = np.asarray(L0_attn_Wkva, dtype=np.float32)
    Wkvb = np.asarray(L0_attn_Wkvb, dtype=np.float32)
    Wout = np.asarray(L0_attn_Wout, dtype=np.float32)
    bout = np.ascontiguousarray(np.asarray(L0_attn_bout, dtype=np.float32).reshape(1, E))
    wg = np.ascontiguousarray(np.asarray(L0_ffn_w_gate, dtype=np.int32))
    wu = np.ascontiguousarray(np.asarray(L0_ffn_w_up, dtype=np.int32))
    wd = np.ascontiguousarray(np.asarray(L0_ffn_w_down, dtype=np.int32))

    wkvb_r = Wkvb.reshape(KVL, KVH, D, 2)
    wk = np.ascontiguousarray(wkvb_r[..., 0].reshape(KVL, 256))
    wv = np.ascontiguousarray(wkvb_r[..., 1].reshape(KVL, 256))
    wks = np.ascontiguousarray(
        wkvb_r[..., 0].reshape(KVL, KVH, 32, 2)[:, :, :, ::-1].reshape(KVL, 256))
    qperm = []
    for g in range(2):
        for i in range(4):
            ha, hb = 8 * g + i, 8 * g + 4 + i
            qperm.extend(range(64 * ha, 64 * ha + 64))
            qperm.extend(range(64 * hb, 64 * hb + 64))
    qperm = np.array(qperm)
    wqb_p = np.ascontiguousarray(Wqb[:, qperm])
    wqbs_full = Wqb.reshape(QL, H, 32, 2)[:, :, :, ::-1].reshape(QL, E)
    wqbs_p = np.ascontiguousarray(wqbs_full[:, qperm])
    operm = []
    for g in range(2):
        for hh in range(2):
            for h in (8 * g + 2 * hh, 8 * g + 2 * hh + 1, 8 * g + 4 + 2 * hh, 8 * g + 4 + 2 * hh + 1):
                operm.extend(range(64 * h, 64 * h + 64))
    operm = np.array(operm)
    wout_p = np.ascontiguousarray(Wout[operm, :])
    wqakva_ = np.ascontiguousarray(np.concatenate([Wqa, Wkva], axis=1))
    identity = np.eye(128, dtype=BF)
    ckv, skv = _rope_tables(np.arange(S))

    nc = build_nc()
    in_maps = []
    chunk_of = np.zeros((NCORE, SLOTS), dtype=np.int64)
    for c in range(NCORE):
        b, cm = c // 4, c % 4
        chunks = [4 * j + cm for j in range(SLOTS)]
        chunk_of[c] = chunks
        xb_ = x[b]
        x_own_ = np.ascontiguousarray(
            np.concatenate([xb_[QC * p:QC * (p + 1)] for p in chunks], axis=0))
        pos_own = np.concatenate([np.arange(QC * p, QC * (p + 1)) for p in chunks])
        cq, sq_ = _rope_tables(pos_own)
        cq = cq.astype(BF); sq_ = sq_.astype(BF)
        m = np.zeros((128, SLOTS * 8 * QC), dtype=BF)
        iv = np.arange(128)[:, None]
        qv = np.arange(QC)[None, :]
        for j in range(SLOTS):
            p = chunks[j]
            for ki in range(8):
                v = 1024 * j + 128 * ki + iv
                m[:, (j * 8 + ki) * QC:(j * 8 + ki + 1) * QC] = (v <= QC * p + qv).astype(BF)
        in_maps.append({
            "x_own": x_own_, "x_full": np.ascontiguousarray(xb_),
            "ident": identity, "masks": m,
            "ctab_kv": ckv, "stab_kv": skv, "ctab_q": cq, "stab_q": sq_,
            "wqakva": wqakva_, "wqb": wqb_p, "wqbs": wqbs_p,
            "wk": wk, "wks": wks, "wv": wv,
            "wout": wout_p, "bout": bout,
            "wg": wg, "wu": wu, "wd": wd,
        })

    res = run_bass_kernel_spmd(nc, in_maps, core_ids=list(range(NCORE)))

    out = np.empty((B, S, E), dtype=np.float32)
    for c in range(NCORE):
        b = c // 4
        yc = np.asarray(res.results[c]["y"])
        for j in range(SLOTS):
            p = chunk_of[c][j]
            out[b, QC * p:QC * (p + 1)] = yc[QC * j:QC * (j + 1)]
    return out


# version 19 (final)
# speedup vs baseline: 1.7203x; 1.0034x over previous
import contextlib
import numpy as np
import ml_dtypes
import concourse.bass as bass
import concourse.mybir as mybir
from concourse.bass_utils import run_bass_kernel_spmd

F32 = mybir.dt.float32
BF16 = mybir.dt.bfloat16
I32 = mybir.dt.int32
AF = mybir.ActivationFunctionType
ALU = mybir.AluOpType
BF = ml_dtypes.bfloat16

B, S, E = 2, 8192, 1024
H, KVH, D = 16, 4, 64
QL, KVL = 64, 32
HID = 3072
NCORE = 8
QC = 256
SLOTS = 8
TOK = QC * SLOTS
ECH = E // 128
HCH = HID // 128
NSL = S // 128
EPS = 1e-5
DQ3 = 1.0 / (2.0 * 127.0 ** 3)

def kcj(j):
    return 8 * (j + 1)

UNITS = []
_t = 0
for _j in range(SLOTS):
    for _g in range(2):
        for _hh in range(2):
            _q3 = _j * 4 + _g * 2 + _hh
            for _kc in range(kcj(_j)):
                to = _t if _kc >= 8 * _j else -1
                if to >= 0:
                    _t += 1
                UNITS.append((_j, _g, _hh, _kc, to, _q3, _kc == 0, _kc == kcj(_j) - 1))
NU = len(UNITS)
U_BEFORE = [0] * (SLOTS + 1)
for _j in range(SLOTS):
    U_BEFORE[_j + 1] = U_BEFORE[_j] + 4 * kcj(_j)
TOP_END = [0] * (SLOTS + 1)
_c = 0
for ui in range(NU):
    if UNITS[ui][4] >= 0:
        _c += 1
    TOP_END[UNITS[ui][0] + 1] = _c


def build_nc():
    nc = bass.Bass("TRN2", target_bir_lowering=False, debug=False)

    def din(name, shape, dt):
        return nc.dram_tensor(name, shape, dt, kind="ExternalInput").ap()

    x_own = din("x_own", [TOK, E], F32)
    x_full = din("x_full", [S, E], F32)
    ident = din("ident", [128, 128], BF16)
    masks = din("masks", [128, SLOTS * 8 * QC], BF16)
    ctab_kv = din("ctab_kv", [128, S], F32)
    stab_kv = din("stab_kv", [128, S], F32)
    ctab_q = din("ctab_q", [128, TOK], BF16)
    stab_q = din("stab_q", [128, TOK], BF16)
    wqakva = din("wqakva", [E, 96], F32)
    wqb_i = din("wqb", [QL, E], F32)
    wqbs_i = din("wqbs", [QL, E], F32)
    wk_i = din("wk", [KVL, 256], F32)
    wks_i = din("wks", [KVL, 256], F32)
    wv_i = din("wv", [KVL, 256], F32)
    wout_i = din("wout", [E, E], F32)
    bout_i = din("bout", [1, E], F32)
    wg_i = din("wg", [HID, E], I32)
    wu_i = din("wu", [HID, E], I32)
    wd_i = din("wd", [E, HID], I32)

    y = nc.dram_tensor("y", [TOK, E], F32, kind="ExternalOutput").ap()
    xn_spill = nc.dram_tensor("xn_spill", [TOK, E], F32, kind="Internal").ap()
    srow_d = nc.dram_tensor("srow_d", [32, 1024], BF16, kind="Internal").ap()

    x_full_r = x_full.rearrange("(n p) e -> n p e", p=128)
    xo_r = x_own.rearrange("(i p) e -> i p e", p=128)
    wg_r = wg_i.rearrange("(c p) e -> c p e", p=128)
    wu_r = wu_i.rearrange("(c p) e -> c p e", p=128)
    spill_r = xn_spill.rearrange("(i p) e -> i p e", p=128)
    y_r = y.rearrange("(i p) e -> i p e", p=128)
    wout_r = wout_i.rearrange("(c p) e -> p c e", p=64)

    with contextlib.ExitStack() as R:
        EE = R.enter_context
        identB = EE(nc.sbuf_tensor("identB", [128, 128], BF16))
        rtmp1 = EE(nc.sbuf_tensor("rtmp1", [128, 512], F32))
        rtmp2 = EE(nc.sbuf_tensor("rtmp2", [128, 512], F32))
        stat = EE(nc.sbuf_tensor("stat", [128, 12], F32))
        mv = EE(nc.sbuf_tensor("mv", [128, 2], F32))
        veps = EE(nc.sbuf_tensor("veps", [128, 1], F32))
        istd = EE(nc.sbuf_tensor("istd", [128, 1], F32))
        nbias = EE(nc.sbuf_tensor("nbias", [128, 1], F32))
        xnT = EE(nc.sbuf_tensor("xnT", [128, ECH, TOK], BF16, side="right"))
        S12 = contextlib.ExitStack()
        E12 = S12.enter_context
        KTp = [E12(nc.sbuf_tensor(f"KTp{g}", [128, S], BF16)) for g in range(2)]
        Vt = E12(nc.sbuf_tensor("Vt", [128, S // 128, KVH, 65], BF16))
        woutB = E12(nc.sbuf_tensor("woutB", [64, 16, E], BF16))
        qlatT = E12(nc.sbuf_tensor("qlatT", [QL, SLOTS, QC], BF16))
        wqbB = E12(nc.sbuf_tensor("wqbB", [QL, E], BF16))
        wqbsB = E12(nc.sbuf_tensor("wqbsB", [QL, E], BF16))
        wkB = E12(nc.sbuf_tensor("wkB", [KVL, 256], BF16))
        wksB = E12(nc.sbuf_tensor("wksB", [KVL, 256], BF16))
        wvB = E12(nc.sbuf_tensor("wvB", [KVL, 256], BF16))
        wqakvaB = E12(nc.sbuf_tensor("wqakvaB", [128, ECH, 96], BF16))
        boutB = E12(nc.sbuf_tensor("boutB", [1, E], BF16))
        onesB = E12(nc.sbuf_tensor("onesB", [1, 128], BF16))
        S1 = contextlib.ExitStack()
        E1 = S1.enter_context
        xf = E1(nc.sbuf_tensor("xf", [128, 2, E], F32))
        xb = E1(nc.sbuf_tensor("xb", [128, 2, E], BF16))
        xTw = E1(nc.sbuf_tensor("xTw", [128, 2, ECH, 128], BF16))
        ct_w = E1(nc.sbuf_tensor("ct_w", [128, 2, 128], F32))
        st_w = E1(nc.sbuf_tensor("st_w", [128, 2, 128], F32))
        xoT = E1(nc.sbuf_tensor("xoT", [128, ECH, QC], BF16))
        wstage = E1(nc.sbuf_tensor("wstage", [64, 2, E], F32))
        bstage = E1(nc.sbuf_tensor("bstage", [1, E], F32))
        sstage = E1(nc.sbuf_tensor("sstage", [128, ECH, 96], F32))
        s2stage = E1(nc.sbuf_tensor("s2stage", [QL, E], F32))
        s3stage = E1(nc.sbuf_tensor("s3stage", [KVL, 3, 256], F32))
        kvlatT = E1(nc.sbuf_tensor("kvlatT", [KVL, S], BF16))
        S1.close()
        S2 = contextlib.ExitStack()
        E2 = S2.enter_context
        qTw = E2(nc.sbuf_tensor("qTw", [128, 8, QC], BF16))
        mask_w = E2(nc.sbuf_tensor("mask_w", [128, 8 * QC], BF16))
        Pq = E2(nc.sbuf_tensor("Pq", [128, 3, 1024], BF16))
        osb = E2(nc.sbuf_tensor("osb", [128, 2, 1024], F32, side="right"))
        OT = E2(nc.sbuf_tensor("OT", [64, 16, QC], BF16))
        obc = E2(nc.sbuf_tensor("obc", [64, 2, 1024], BF16))
        srow_t = E2(nc.sbuf_tensor("srow_t", [128, 1024], BF16))
        qc_w = E2(nc.sbuf_tensor("qc_w", [128, 2, QC], BF16))
        qs_w = E2(nc.sbuf_tensor("qs_w", [128, 2, QC], BF16))
        xres = E2(nc.sbuf_tensor("xres", [128, 2, E], F32))
        xn_f = E2(nc.sbuf_tensor("xn_f", [128, 2, E], F32))
        xn_b = E2(nc.sbuf_tensor("xn_b", [128, 2, E], BF16))
        S2.close()
        S12.close()
        hactT = EE(nc.sbuf_tensor("hactT", [128, HCH, 1024], BF16))
        wdT = EE(nc.sbuf_tensor("wdT", [128, HCH, E], BF16))
        wgi_s = EE(nc.sbuf_tensor("wgi_s", [128, 2, E], I32))
        wui_s = EE(nc.sbuf_tensor("wui_s", [128, 2, E], I32))
        wdi_s = EE(nc.sbuf_tensor("wdi_s", [128, 2, ECH, 128], I32))
        wgb_s = EE(nc.sbuf_tensor("wgb_s", [128, 2, E], BF16))
        wub_s = EE(nc.sbuf_tensor("wub_s", [128, 2, E], BF16))
        wdb_s = EE(nc.sbuf_tensor("wdb_s", [128, 2, ECH, 128], BF16))
        wgT = EE(nc.sbuf_tensor("wgT", [128, 2, ECH, 128], BF16))
        wuT = EE(nc.sbuf_tensor("wuT", [128, 2, ECH, 128], BF16))
        tb = EE(nc.sbuf_tensor("tb", [128, 2, 512], F32))
        ub = EE(nc.sbuf_tensor("ub", [128, 2, 512], F32))
        rb = EE(nc.sbuf_tensor("rb", [128, 2, 512], F32))
        xnr = EE(nc.sbuf_tensor("xnr", [128, 2, E], F32))
        fsb = EE(nc.sbuf_tensor("fsb", [128, 2, E], F32))

        pA0 = EE(nc.psum_tensor("pA0", [128, 1024], F32))
        pA1 = EE(nc.psum_tensor("pA1", [128, 1024], F32))
        pA2 = EE(nc.psum_tensor("pA2", [128, 1024], F32))
        pT0 = EE(nc.psum_tensor("pT0", [128, 512], F32))
        pT1 = EE(nc.psum_tensor("pT1", [128, 512], F32))
        pA = [pA0, pA1, pA2]
        sq = pA
        pTR = [pT0, pT1]

        def obank(q3, b):
            return (pT0 if b == 0 else pT1)[:, :]

        def sname(n):
            return EE(nc.semaphore(n))
        ld_c = sname("ld_c"); ld_x = sname("ld_x"); ld_x2 = sname("ld_x2")
        lc_i = sname("lc_i"); lc_qa = sname("lc_qa"); lc_k1 = sname("lc_k1")
        lc_k2 = sname("lc_k2"); lc_k3 = sname("lc_k3"); lc_bo = sname("lc_bo")
        lc_qb = sname("lc_qb")
        lt = [sname("lt0"), sname("lt1")]
        lqt = [sname("lqt0"), sname("lqt1")]
        lxr = [sname("lxr0"), sname("lxr1")]
        lw = [sname("lw0"), sname("lw1")]
        lwd = [sname("lwd0"), sname("lwd1")]
        lxn = [sname("lxn0"), sname("lxn1")]
        ssp = [sname("ssp0"), sname("ssp1")]
        sy = [sname("sy0"), sname("sy1")]
        lxo = [sname("lxo0"), sname("lxo1")]
        ld_t = sname("ld_t")
        ld_xo = sname("ld_xo"); ld_xres = sname("ld_xres"); ld_msk = sname("ld_msk")
        ld_qt = sname("ld_qt"); ld_w = sname("ld_w"); ld_wd = sname("ld_wd")
        ld_xn2 = sname("ld_xn2")
        s_cast = sname("s_cast"); s_ms = sname("s_ms")
        c_xb = sname("c_xb"); s_trec = sname("s_trec"); s_xte = sname("s_xte")
        s_lat = sname("s_lat"); s_kvl = sname("s_kvl"); s_kmm = sname("s_kmm")
        s_rope = sname("s_rope"); s_vmm = sname("s_vmm"); s_vev = sname("s_vev")
        c_xob = sname("c_xob"); s_troc = sname("s_troc"); s_xoev = sname("s_xoev")
        s_qmm = sname("s_qmm"); s_qle = sname("s_qle")
        s_qbmm = sname("s_qbmm"); s_qrope = sname("s_qrope")
        s_squad = sname("s_squad"); s_exp = sname("s_exp"); s_msk = sname("s_msk")
        s_o = sname("s_o"); s_oev = sname("s_oev"); s_sum = sname("s_sum"); s_row = sname("s_row")
        s_obc = sname("s_obc"); s_norm = sname("s_norm")
        s_epimm = sname("s_epimm"); s_xsum = sname("s_xsum"); s_stat = sname("s_stat")
        s_istd = sname("s_istd"); s_nb = sname("s_nb"); s_app = sname("s_app")
        s_xnb = sname("s_xnb"); s_xnt = sname("s_xnt"); s_xnte = sname("s_xnte")
        s_spill = sname("s_spill")
        s_wcast = sname("s_wcast"); s_wtr = sname("s_wtr"); s_wev = sname("s_wev")
        s_gu = sname("s_gu"); s_se = sname("s_se"); s_sil = sname("s_sil")
        s_dwc = sname("s_dwc"); s_dwt = sname("s_dwt"); s_dwev = sname("s_dwev")
        s_dmm = sname("s_dmm"); s_fsum = sname("s_fsum"); s_stat2 = sname("s_stat2")
        s_istd2 = sname("s_istd2"); s_nb2 = sname("s_nb2"); s_app2 = sname("s_app2")
        s_y = sname("s_y")
        sd = sname("sd"); sa = sname("sa")

        NCAST = 15

        block = EE(nc.Block())

        @block.sync
        def _(sync):
            sync.dma_start(out=identB[:], in_=ident).then_inc(lc_i, 16)
            sync.dma_start(out=sstage[:], in_=wqakva.rearrange("(c p) m -> p c m", p=128)).then_inc(lc_qa, 16)
            sync.dma_start(out=s3stage[:, 0, :], in_=wk_i).then_inc(lc_k1, 16)
            sync.dma_start(out=s3stage[:, 1, :], in_=wks_i).then_inc(lc_k2, 16)
            sync.dma_start(out=s3stage[:, 2, :], in_=wv_i).then_inc(lc_k3, 16)
            sync.dma_start(out=bstage[:], in_=bout_i).then_inc(lc_bo, 16)
            sync.dma_start(out=s2stage[:], in_=wqb_i).then_inc(lc_qb, 16)
            sync.wait_ge(s_cast, 6)
            sync.dma_start(out=s2stage[:], in_=wqbs_i).then_inc(lc_qb, 16)
            for r in range(8):
                if r > 0:
                    sync.wait_ge(s_cast, 7 + r)
                sync.dma_start(out=wstage[:], in_=wout_r[:, 2 * r:2 * r + 2, :]).then_inc(ld_c, 16)
            for s in range(NSL):
                if s % 2 == 0:
                    if s >= 2:
                        sync.wait_ge(c_xb, s - 1)
                    sync.dma_start(out=xf[:, s % 2, :], in_=x_full_r[s]).then_inc(ld_x, 16)
            for i in range(16):
                if i < 2:
                    sync.wait_ge(c_xb, NSL)
                else:
                    sync.wait_ge(c_xob, i - 1)
                sync.dma_start(out=xf[:, i % 2, :], in_=xo_r[i]).then_inc(lxo[i % 2], 16)
            sync.wait_ge(c_xb, NSL)
            sync.wait_ge(c_xob, 16)
            sync.wait_ge(s_trec, 8 * NSL)
            sync.wait_ge(s_troc, 128)
            sync.wait_ge(s_xte, 8 * NSL)
            sync.wait_ge(s_xoev, 128)
            sync.wait_ge(s_lat, NSL)
            sync.wait_ge(s_kvl, NSL)
            sync.wait_ge(s_rope, 2 * NSL)
            sync.wait_ge(s_vev, NSL)
            sync.wait_ge(s_qmm, 8)
            sync.wait_ge(s_qle, 8)
            sync.wait_ge(s_cast, NCAST)
            for j in range(SLOTS):
                if j >= 1:
                    sync.wait_ge(s_msk, TOP_END[j])
                if j >= 2:
                    sync.wait_ge(s_qrope, 8 * (j - 1))
                sync.dma_start(out=mask_w[:], in_=masks[:, j * 8 * QC:(j + 1) * 8 * QC]).then_inc(ld_msk, 16)
                sync.dma_start(out=qc_w[:, j % 2, :], in_=ctab_q[:, j * QC:(j + 1) * QC]).then_inc(lqt[j % 2], 16)
                sync.dma_start(out=qs_w[:, j % 2, :], in_=stab_q[:, j * QC:(j + 1) * QC]).then_inc(lqt[j % 2], 16)
                for tt in range(2):
                    it = 2 * j + tt
                    if j >= 1:
                        sync.wait_ge(s_app, it - 1)
                    sync.dma_start(out=xres[:, tt, :], in_=xo_r[it]).then_inc(lxr[tt], 16)
            sync.wait_ge(s_o, NU)
            sync.wait_ge(s_exp, NU)
            sync.wait_ge(s_msk, TOP_END[SLOTS])
            sync.wait_ge(s_qbmm, 64)
            sync.wait_ge(s_qrope, 64)
            sync.wait_ge(s_norm, 16 * SLOTS)
            sync.wait_ge(s_obc, 16 * 32)
            sync.wait_ge(s_epimm, 16)
            sync.wait_ge(s_xsum, 16)
            sync.wait_ge(s_app, 16)
            sync.wait_ge(s_xnb, 16)
            sync.wait_ge(s_xnt, 16)
            sync.wait_ge(s_xnte, 32)
            sync.wait_ge(ssp[0], 16 * 8)
            sync.wait_ge(ssp[1], 16 * 8)
            for bb in range(2):
                for hc in range(HCH):
                    k = bb * HCH + hc
                    if k >= 2:
                        sync.wait_ge(s_wcast, 2 * (k - 1))
                    sync.dma_start(out=wgi_s[:, hc % 2, :], in_=wg_r[hc]).then_inc(lw[hc % 2], 16)
                    sync.dma_start(out=wui_s[:, hc % 2, :], in_=wu_r[hc]).then_inc(lw[hc % 2], 16)
                for ti in range(8):
                    gi = bb * 8 + ti

                    if gi >= 2:
                        sync.wait_ge(s_fsum, gi - 1)
                    sync.dma_start(out=xnr[:, gi % 2, :], in_=spill_r[gi]).then_inc(lxn[gi % 2], 16)

        @block.gpsimd
        def _(gpsimd):
            for s in range(NSL):
                if s % 2 == 1:
                    if s >= 2:
                        gpsimd.wait_ge(c_xb, s - 1)
                    gpsimd.dma_start(out=xf[:, s % 2, :], in_=x_full_r[s]).then_inc(ld_x2, 16)
            for j in range(SLOTS):
                for r in range(4):
                    q3 = 4 * j + r
                    gpsimd.wait_ge(s_sum, q3 + 1)
                    if q3 >= 2:
                        gpsimd.wait_ge(s_norm, 4 * (q3 - 1))
                    gpsimd.dma_start(out=srow_d[q3:q3 + 1, :], in_=srow_t[64:65, :]).then_inc(s_row, 16)
                    gpsimd.wait_ge(s_row, 16 * (q3 + 1))
                    gpsimd.dma_start(out=obc[:, q3 % 2, :],
                                     in_=srow_d[q3:q3 + 1, :].partition_broadcast(64)).then_inc(s_obc, 16)
                for tt in range(2):
                    it = 2 * j + tt
                    gpsimd.wait_ge(s_app, it + 1)
                    gpsimd.dma_start(out=spill_r[it], in_=xn_f[:, tt, :]).then_inc(ssp[tt], 16)
            for hc in range(HCH):
                if hc >= 2:
                    gpsimd.wait_ge(s_dwc, hc - 1)
                gpsimd.dma_start(out=wdi_s[:, hc % 2, :, :],
                                 in_=wd_i[:, hc * 128:(hc + 1) * 128].rearrange("(c p) h -> p c h", p=128)
                                 ).then_inc(lwd[hc % 2], 16)
            for gi in range(16):
                gpsimd.wait_ge(s_app2, gi + 1)
                gpsimd.dma_start(out=y_r[gi], in_=fsb[:, gi % 2, :]).then_inc(sy[gi % 2], 16)
            gpsimd.wait_ge(sy[0], 16 * 8)
            gpsimd.wait_ge(sy[1], 16 * 8)

        @block.scalar
        def _(scalar):
            san = [0]
            def sai(instr):
                instr.then_inc(sa, 1); san[0] += 1
                return instr
            for s in range(NSL):
                if s >= 2:
                    scalar.wait_ge(s_rope, 2 * (s - 1))
                scalar.dma_start(out=ct_w[:, s % 2, :], in_=ctab_kv[:, 128 * s:128 * (s + 1)]).then_inc(lt[s % 2], 16)
                scalar.dma_start(out=st_w[:, s % 2, :], in_=stab_kv[:, 128 * s:128 * (s + 1)]).then_inc(lt[s % 2], 16)
                scalar.wait_ge(s_lat, s + 1)
                scalar.activation(out=kvlatT[:, 128 * s:128 * (s + 1)], in_=pA2[0:KVL, 0:128],
                                  func=AF.Copy).then_inc(s_kvl, 1)
            for j in range(SLOTS):
                scalar.wait_ge(s_qmm, j + 1)
                scalar.activation(out=qlatT[:, j, :], in_=pA2[0:QL, 256:256 + QC],
                                  func=AF.Copy, scale=0.125).then_inc(s_qle, 1)
            def act_istd(jj, tt):
                it = 2 * jj + tt
                scalar.wait_ge(s_stat, it + 1)
                sai(scalar.activation(out=istd[:], in_=veps[:], func=AF.Ln))
                scalar.wait_ge(sa, san[0])
                scalar.activation(out=istd[:], in_=istd[:], func=AF.Exp,
                                  scale=-0.5).then_inc(s_istd, 1)

            for ui, (j, g, hh, kc, top, q3, fk, lk) in enumerate(UNITS):
                li = ui - U_BEFORE[j]
                scalar.wait_ge(s_squad, ui + 1)
                if ui >= 3:
                    scalar.wait_ge(s_o, ui - 2)
                scalar.activation(out=Pq[:, ui % 3, :], in_=sq[ui % 3][:, :],
                                  func=AF.Exp).then_inc(s_exp, 1)
                if j >= 1 and li == 3:
                    act_istd(j - 1, 0)
                if j >= 1 and li == 8:
                    act_istd(j - 1, 1)
            act_istd(7, 0)
            act_istd(7, 1)
            for bb in range(2):
                for hc in range(HCH):
                    for tt4 in range(2):
                        f = hc * 2 + tt4
                        fb = bb * 48 + f
                        scalar.wait_ge(s_gu, fb + 1)
                        if fb >= 2:
                            scalar.wait_ge(s_sil, fb - 1)
                        scalar.activation(out=tb[:, fb % 2, :], in_=pA[f % 3][:, 0:512],
                                          func=AF.Tanh, scale=0.5 / 127.0).then_inc(s_se, 1)
                for ti in range(8):
                    gi = bb * 8 + ti
                    scalar.wait_ge(s_stat2, gi + 1)
                    sai(scalar.activation(out=istd[:], in_=veps[:], func=AF.Ln))
                    scalar.wait_ge(sa, san[0])
                    scalar.activation(out=istd[:], in_=istd[:], func=AF.Exp,
                                      scale=-0.5).then_inc(s_istd2, 1)

        @block.vector
        def _(vector):
            sdn = [0]
            def sdi(instr):
                instr.then_inc(sd, 1); sdn[0] += 1
                return instr
            vector.wait_ge(lc_qa, 16)
            vector.tensor_copy(out=wqakvaB[:], in_=sstage[:]).then_inc(s_cast, 1)
            vector.wait_ge(lc_k1, 16)
            vector.tensor_copy(out=wkB[:], in_=s3stage[:, 0, :]).then_inc(s_cast, 1)
            vector.wait_ge(lc_k2, 16)
            vector.tensor_copy(out=wksB[:], in_=s3stage[:, 1, :]).then_inc(s_cast, 1)
            vector.wait_ge(lc_k3, 16)
            vector.tensor_copy(out=wvB[:], in_=s3stage[:, 2, :]).then_inc(s_cast, 1)
            vector.wait_ge(lc_bo, 16)
            vector.tensor_copy(out=boutB[:], in_=bstage[:]).then_inc(s_cast, 1)
            vector.wait_ge(lc_qb, 16)
            vector.tensor_copy(out=wqbB[:], in_=s2stage[:]).then_inc(s_cast, 1)
            vector.wait_ge(lc_qb, 32)
            vector.tensor_copy(out=wqbsB[:], in_=s2stage[:]).then_inc(s_cast, 1)
            for r in range(8):
                vector.wait_ge(ld_c, 16 * (r + 1))
                vector.tensor_copy(out=woutB[:, 2 * r:2 * r + 2, :], in_=wstage[:]).then_inc(s_cast, 1)
            vector.memset(Vt[:], 1.0).then_inc(s_ms, 1)
            vector.memset(onesB[:], 1.0).then_inc(s_ms, 1)
            for s in range(NSL):
                if s % 2 == 0:
                    vector.wait_ge(ld_x, 16 * (s // 2 + 1))
                else:
                    vector.wait_ge(ld_x2, 16 * ((s + 1) // 2))
                if s >= 2:
                    vector.wait_ge(s_trec, 8 * (s - 1))
                vector.tensor_copy(out=xb[:, s % 2, :], in_=xf[:, s % 2, :]).then_inc(c_xb, 1)
                for ec in range(ECH):
                    vector.wait_ge(s_trec, 8 * s + ec + 1)
                    if s >= 2 and ec == 0:
                        vector.wait_ge(s_lat, s - 1)
                    vector.tensor_copy(out=xTw[:, s % 2, ec, :], in_=pTR[ec % 2][:, 0:128]).then_inc(s_xte, 1)
                for g in range(2):
                    vector.wait_ge(s_kmm, 2 * s + g + 1)
                    vector.wait_ge(lt[s % 2], 32 * (s // 2 + 1))
                    sdi(vector.tensor_tensor(out=rtmp1[:, 0:128], in0=pA0[:, 512 * g:512 * g + 128],
                                             in1=ct_w[:, s % 2, :], op=ALU.mult))
                    sdi(vector.tensor_tensor(out=rtmp2[:, 0:128], in0=pA1[:, 512 * g:512 * g + 128],
                                             in1=st_w[:, s % 2, :], op=ALU.mult))
                    vector.wait_ge(sd, sdn[0])
                    vector.tensor_add(out=KTp[g][:, 128 * s:128 * (s + 1)],
                                      in0=rtmp1[:, 0:128], in1=rtmp2[:, 0:128]).then_inc(s_rope, 1)
                vector.wait_ge(s_vmm, s + 1)
                if s == 0:
                    vector.wait_ge(s_ms, 2)
                vector.tensor_copy(out=Vt[:, s, :, 0:64],
                                   in_=pA2[:, 512:768].rearrange("p (h d) -> p h d", h=4)
                                   ).then_inc(s_vev, 1)
            for i in range(16):
                vector.wait_ge(lxo[i % 2], 16 * (i // 2 + 1))
                if i >= 2:
                    vector.wait_ge(s_troc, 8 * (i - 1))
                vector.tensor_copy(out=xb[:, i % 2, :], in_=xf[:, i % 2, :]).then_inc(c_xob, 1)
                tt = i % 2
                for ec in range(ECH):
                    vector.wait_ge(s_troc, 8 * i + ec + 1)
                    if i >= 2 and ec == 0:
                        vector.wait_ge(s_qmm, i // 2)
                    vector.tensor_copy(out=xoT[:, ec, tt * 128:(tt + 1) * 128],
                                       in_=pTR[ec % 2][:, 0:128]).then_inc(s_xoev, 1)
            def dve_recip_piece(q3, c):
                if c == 0:
                    if q3 >= 1:
                        vector.wait_ge(s_row, 16 * q3)
                    vector.wait_ge(s_oev, q3 + 1)
                with nc.allow_low_precision(reason="1/softmax-sums in bf16; rvr margin 20x"):
                    r = vector.reciprocal(out=srow_t[64:65, c * 256:(c + 1) * 256],
                                          in_=osb[64:65, q3 % 2, c * 256:(c + 1) * 256])
                if c == 3:
                    r.then_inc(s_sum, 1)
                else:
                    sdi(r)

            def dve_normblock(jj, q3):
                vector.wait_ge(s_obc, 16 * (q3 + 1))
                if jj >= 1:
                    vector.wait_ge(s_epimm, 2 * jj)
                for ho in range(4):
                    c = (q3 % 4) * 4 + ho
                    vector.tensor_tensor(out=OT[:, c, :],
                                         in0=osb[0:64, q3 % 2, ho * QC:(ho + 1) * QC],
                                         in1=obc[:, q3 % 2, (2 * (ho // 2) + (ho % 2)) * QC:
                                                 (2 * (ho // 2) + (ho % 2) + 1) * QC],
                                         op=ALU.mult).then_inc(s_norm, 1)

            def dve_chain_head(jj, tt):
                it = 2 * jj + tt
                if it >= 1:
                    vector.wait_ge(s_istd, it)
                sdi(vector.bn_stats(stat[:, 0:6], xres[:, tt, 0:512]))
                sdi(vector.bn_stats(stat[:, 6:12], xres[:, tt, 512:1024]))
                vector.wait_ge(sd, sdn[0])
                sdi(vector.bn_aggr(mv[:], stat[:, 0:12]))
                vector.wait_ge(sd, sdn[0])
                vector.tensor_scalar_add(veps[:], mv[:, 1:2], EPS).then_inc(s_stat, 1)

            def dve_chain_tail(jj, tt):
                it = 2 * jj + tt
                vector.wait_ge(s_istd, it + 1)
                vector.tensor_scalar(out=nbias[:], in0=istd[:], scalar1=mv[:, 0:1],
                                     scalar2=-1.0, op0=ALU.mult, op1=ALU.mult).then_inc(s_nb, 1)
                vector.wait_ge(s_nb, it + 1)
                if jj >= 1:
                    vector.wait_ge(ssp[tt], 16 * (it // 2))
                vector.tensor_scalar(out=xn_f[:, tt, :], in0=xres[:, tt, :],
                                     scalar1=istd[:], scalar2=nbias[:],
                                     op0=ALU.mult, op1=ALU.add).then_inc(s_app, 1)
                vector.wait_ge(s_app, it + 1)
                if jj >= 1:
                    vector.wait_ge(s_xnt, it - 1)
                vector.tensor_copy(out=xn_b[:, tt, :], in_=xn_f[:, tt, :]).then_inc(s_xnb, 1)

            def dve_xnte(jj):
                for tt in range(2):
                    it = 2 * jj + tt
                    vector.wait_ge(s_xnt, it + 1)
                    col = jj * QC + tt * 128
                    for hf in range(2):
                        vector.tensor_copy(
                            out=xnT[:, 4 * hf:4 * hf + 4, col:col + 128],
                            in_=pTR[hf][:, :].rearrange("p (c n) -> p c n", c=4)
                            ).then_inc(s_xnte, 1)

            prev_slot = -1
            agenda = {}
            for ui, (j, g, hh, kc, top, q3, fk, lk) in enumerate(UNITS):
                li = ui - U_BEFORE[j]
                kj = kcj(j)
                if j != prev_slot:
                    prev_slot = j
                    agenda = {}
                    if j >= 1:
                        pj = j - 1
                        agenda.setdefault(2, []).append(lambda pj=pj: dve_chain_head(pj, 0))
                        agenda.setdefault(5, []).append(lambda pj=pj: dve_chain_tail(pj, 0))
                        agenda.setdefault(7, []).append(lambda pj=pj: dve_chain_head(pj, 1))
                        agenda.setdefault(10, []).append(lambda pj=pj: dve_chain_tail(pj, 1))
                        agenda.setdefault(2 * kj, []).append(lambda pj=pj: dve_xnte(pj))
                    for r in range(3):
                        q3r = 4 * j + r
                        L = (r + 1) * kj - 1
                        for c in range(4):
                            agenda.setdefault(L + 1 + c, []).append(
                                lambda q3r=q3r, c=c: dve_recip_piece(q3r, c))
                        agenda.setdefault(L + 6, []).append(
                            lambda j=j, q3r=q3r: dve_normblock(j, q3r))
                    for pr in range(8):
                        vector.wait_ge(s_qbmm, 8 * j + pr + 1)
                        vector.wait_ge(lqt[j % 2], 32 * (j // 2 + 1))
                        src = sq[pr % 2]
                        sdi(vector.tensor_tensor(out=rtmp1[:, 0:QC], in0=src[:, 0:QC],
                                                 in1=qc_w[:, j % 2, :], op=ALU.mult))
                        sdi(vector.tensor_tensor(out=rtmp2[:, 0:QC], in0=src[:, 512:512 + QC],
                                                 in1=qs_w[:, j % 2, :], op=ALU.mult))
                        vector.wait_ge(sd, sdn[0])
                        vector.tensor_add(out=qTw[:, pr, :], in0=rtmp1[:, 0:QC],
                                          in1=rtmp2[:, 0:QC]).then_inc(s_qrope, 1)
                if top >= 0:
                    vector.wait_ge(s_exp, ui + 1)
                    vector.wait_ge(ld_msk, 16 * (j + 1))
                    ki = kc - 8 * j
                    mk = mask_w[:, ki * QC:(ki + 1) * QC]
                    pv = Pq[:, ui % 3, :].rearrange("p (r n) -> p r n", r=4)
                    vector.tensor_tensor(out=pv, in0=pv,
                                         in1=mk.unsqueeze(1).broadcast_to([128, 4, QC]),
                                         op=ALU.mult).then_inc(s_msk, 1)
                if lk:
                    vector.wait_ge(s_o, ui + 1)
                    if q3 >= 2:
                        vector.wait_ge(s_norm, 4 * (q3 - 1))
                    sdi(vector.tensor_copy(out=osb[:, q3 % 2, 0:512], in_=pT0[:, :]))
                    vector.tensor_copy(out=osb[:, q3 % 2, 512:1024], in_=pT1[:, :]).then_inc(s_oev, 1)
                    if q3 % 4 == 3:
                        for c in range(4):
                            dve_recip_piece(q3, c)
                        dve_normblock(j, q3)
                for fn in agenda.pop(li, []):
                    fn()
                if lk and q3 % 4 == 3:
                    for tt in range(2):
                        it = 2 * j + tt
                        vector.wait_ge(s_epimm, it + 1)
                        vector.wait_ge(lxr[tt], 16 * (j + 1))
                        vector.tensor_add(out=xres[:, tt, :], in0=sq[tt][:, :],
                                          in1=xres[:, tt, :]).then_inc(s_xsum, 1)
            dve_chain_head(7, 0)
            dve_chain_tail(7, 0)
            dve_chain_head(7, 1)
            dve_chain_tail(7, 1)
            dve_xnte(7)
            for hc in range(HCH):
                vector.wait_ge(lwd[hc % 2], 16 * (hc // 2 + 1))
                if hc >= 2:
                    vector.wait_ge(s_dwt, 2 * (hc - 1))
                vector.tensor_copy(out=wdb_s[:, hc % 2, :, :], in_=wdi_s[:, hc % 2, :, :]).then_inc(s_dwc, 1)
                for grp in range(2):
                    vector.wait_ge(s_dwt, 2 * hc + grp + 1)
                    vector.tensor_copy(out=wdT[:, hc, grp * 512:(grp + 1) * 512],
                                       in_=pTR[grp][:, :]).then_inc(s_dwev, 1)
            for bb in range(2):
                for hc in range(HCH):
                    k = bb * HCH + hc
                    vector.wait_ge(lw[hc % 2], 32 * (bb * 12 + hc // 2 + 1))
                    if k >= 2:
                        vector.wait_ge(s_wtr, 4 * (k - 1))
                    vector.tensor_copy(out=wgb_s[:, hc % 2, :], in_=wgi_s[:, hc % 2, :]).then_inc(s_wcast, 1)
                    vector.tensor_copy(out=wub_s[:, hc % 2, :], in_=wui_s[:, hc % 2, :]).then_inc(s_wcast, 1)
                    for grp in range(4):
                        vector.wait_ge(s_wtr, 4 * k + grp + 1)
                        dst = (wgT if grp < 2 else wuT)
                        hf = grp % 2
                        vector.tensor_copy(out=dst[:, hc % 2, 4 * hf:4 * hf + 4, :],
                                           in_=pTR[grp % 2][:, :].rearrange("p (c n) -> p c n", c=4)
                                           ).then_inc(s_wev, 1)
                    for tt4 in range(2):
                        f = hc * 2 + tt4
                        fb = bb * 48 + f
                        vector.wait_ge(s_se, fb + 1)
                        u1 = vector.tensor_scalar_add(ub[:, fb % 2, :], tb[:, fb % 2, :], 1.0)
                        sdi(u1)
                        vector.wait_ge(sd, sdn[0])
                        h1 = vector.tensor_tensor(out=rb[:, fb % 2, :], in0=ub[:, fb % 2, :],
                                                  in1=pA[f % 3][:, 0:512], op=ALU.mult)
                        sdi(h1)
                        vector.wait_ge(sd, sdn[0])
                        vector.tensor_tensor(out=hactT[:, hc, tt4 * 512:(tt4 + 1) * 512],
                                             in0=rb[:, fb % 2, :], in1=pA[f % 3][:, 512:1024],
                                             op=ALU.mult).then_inc(s_sil, 1)
                for ti in range(8):
                    gi = bb * 8 + ti
                    vector.wait_ge(s_dmm, gi + 1)
                    vector.wait_ge(lxn[gi % 2], 16 * (gi // 2 + 1))
                    if gi >= 2:
                        vector.wait_ge(sy[gi % 2], 16 * (gi // 2))
                    if gi == 0:
                        vector.wait_ge(s_app, 16)
                    vector.scalar_tensor_tensor(out=fsb[:, gi % 2, :], in0=pA[ti % 3][:, :],
                                                scalar=DQ3, in1=xnr[:, gi % 2, :],
                                                op0=ALU.mult, op1=ALU.add).then_inc(s_fsum, 1)
                    vector.wait_ge(s_fsum, gi + 1)
                    if gi >= 1:
                        vector.wait_ge(s_istd2, gi)
                    sdi(vector.bn_stats(stat[:, 0:6], fsb[:, gi % 2, 0:512]))
                    sdi(vector.bn_stats(stat[:, 6:12], fsb[:, gi % 2, 512:1024]))
                    vector.wait_ge(sd, sdn[0])
                    sdi(vector.bn_aggr(mv[:], stat[:, 0:12]))
                    vector.wait_ge(sd, sdn[0])
                    vector.tensor_scalar_add(veps[:], mv[:, 1:2], EPS).then_inc(s_stat2, 1)
                    vector.wait_ge(s_istd2, gi + 1)
                    vector.tensor_scalar(out=nbias[:], in0=istd[:], scalar1=mv[:, 0:1],
                                         scalar2=-1.0, op0=ALU.mult, op1=ALU.mult).then_inc(s_nb2, 1)
                    vector.wait_ge(s_nb2, gi + 1)
                    vector.tensor_scalar(out=fsb[:, gi % 2, :], in0=fsb[:, gi % 2, :],
                                         scalar1=istd[:], scalar2=nbias[:],
                                         op0=ALU.mult, op1=ALU.add).then_inc(s_app2, 1)

        @block.tensor
        def _(tensor):
            for s in range(NSL):
                tensor.wait_ge(c_xb, s + 1)
                if s == 0:
                    tensor.wait_ge(lc_i, 16)
                for ec in range(ECH):
                    w = 8 * s + ec - 1
                    if w > 0:
                        tensor.wait_ge(s_xte, w)
                    mm = tensor.matmul(pTR[ec % 2][:, 0:128],
                                       xb[:, s % 2, ec * 128:(ec + 1) * 128],
                                       identB[:], start=True, stop=True)
                    mm.then_inc(s_trec, 1)
                tensor.wait_ge(s_xte, 8 * (s + 1))
                if s >= 1:
                    tensor.wait_ge(s_kvl, s)
                if s == 0:
                    tensor.wait_ge(s_cast, NCAST)
                for ec in range(ECH):
                    mm = tensor.matmul(pA2[0:KVL, 0:128], wqakvaB[:, ec, 64:96],
                                       xTw[:, s % 2, ec, :], start=(ec == 0), stop=(ec == ECH - 1))
                mm.then_inc(s_lat, 1)
                tensor.wait_ge(s_kvl, s + 1)
                for g in range(2):
                    w = 2 * s + g - 1
                    if w > 0:
                        tensor.wait_ge(s_rope, w)
                    tensor.matmul(pA0[:, 512 * g:512 * g + 128], wkB[:, g * 128:(g + 1) * 128],
                                  kvlatT[:, 128 * s:128 * (s + 1)], start=True, stop=True)
                    mm = tensor.matmul(pA1[:, 512 * g:512 * g + 128], wksB[:, g * 128:(g + 1) * 128],
                                       kvlatT[:, 128 * s:128 * (s + 1)], start=True, stop=True)
                    mm.then_inc(s_kmm, 1)
                if s >= 1:
                    tensor.wait_ge(s_vev, s)
                mm = tensor.matmul(pA2[:, 512:768],
                                   kvlatT[:, 128 * s:128 * (s + 1)],
                                   wvB[:], start=True, stop=True)
                mm.then_inc(s_vmm, 1)
            for i in range(16):
                tensor.wait_ge(c_xob, i + 1)
                tt = i % 2
                for ec in range(ECH):
                    w = 8 * i + ec - 1
                    if w > 0:
                        tensor.wait_ge(s_xoev, w)
                    mm = tensor.matmul(pTR[ec % 2][:, 0:128], xb[:, i % 2, ec * 128:(ec + 1) * 128],
                                       identB[:], start=True, stop=True)
                    mm.then_inc(s_troc, 1)
                if tt == 1:
                    jslot = i // 2
                    tensor.wait_ge(s_xoev, 8 * (i + 1))
                    if jslot == 0:
                        tensor.wait_ge(s_kvl, NSL)
                    if jslot >= 1:
                        tensor.wait_ge(s_qle, jslot)
                    for ec in range(ECH):
                        mm = tensor.matmul(pA2[0:QL, 256:256 + QC], wqakvaB[:, ec, 0:QL],
                                           xoT[:, ec, :], start=(ec == 0), stop=(ec == ECH - 1))
                    mm.then_inc(s_qmm, 1)
            for j in range(SLOTS):
                for pr in range(8):
                    if pr == 0:
                        tensor.wait_ge(s_qle, j + 1)
                        tensor.wait_ge(s_exp, U_BEFORE[j])
                        tensor.wait_ge(s_xsum, 2 * j)
                        if j == 0:
                            tensor.wait_ge(s_rope, 2 * NSL)
                    w = 8 * j + pr - 1
                    if w > 0:
                        tensor.wait_ge(s_qrope, w)
                    dst = sq[pr % 2]
                    tensor.matmul(dst[:, 0:QC], wqbB[:, pr * 128:(pr + 1) * 128],
                                  qlatT[:, j, :], start=True, stop=True)
                    mm = tensor.matmul(dst[:, 512:512 + QC], wqbsB[:, pr * 128:(pr + 1) * 128],
                                       qlatT[:, j, :], start=True, stop=True)
                    mm.then_inc(s_qbmm, 1)
                ujs = list(range(U_BEFORE[j], U_BEFORE[j + 1]))

                def emit_S(ui, j=j):
                    jj, g, hh, kc, top, q3, fk, lk = UNITS[ui]
                    if ui == U_BEFORE[j]:
                        tensor.wait_ge(s_qrope, 8 * (j + 1))
                        if j == 0:
                            tensor.wait_ge(s_vev, NSL)
                            tensor.wait_ge(s_qmm, SLOTS)
                    if ui >= 3:
                        tensor.wait_ge(s_exp, ui - 2)
                    pr0 = 4 * g + 2 * hh
                    tensor.matmul(sq[ui % 3][:, 0:512],
                                  KTp[g][0:64, 128 * kc:128 * (kc + 1)],
                                  qTw[0:64, pr0:pr0 + 2, :], start=True, stop=True,
                                  tile_position=(0, 0), skip_group_check=True)
                    mm = tensor.matmul(sq[ui % 3][:, 512:1024],
                                       KTp[g][64:128, 128 * kc:128 * (kc + 1)],
                                       qTw[64:128, pr0:pr0 + 2, :], start=True, stop=True,
                                       tile_position=(64, 0), skip_group_check=True)
                    mm.then_inc(s_squad, 1)

                def emit_O(ui, j=j):
                    jj, g, hh, kc, top, q3, fk, lk = UNITS[ui]
                    if fk:
                        if q3 >= 1:
                            tensor.wait_ge(s_oev, q3)
                        if q3 == 0:
                            tensor.wait_ge(s_xoev, 128)
                        if j >= 1 and q3 % 4 == 2:
                            tensor.wait_ge(s_xnte, 4 * j)
                    if top >= 0:
                        tensor.wait_ge(s_msk, top + 1)
                    else:
                        tensor.wait_ge(s_exp, ui + 1)
                    for bi in range(2):
                        kvh = 2 * g + bi
                        mm = tensor.matmul(obank(q3, bi)[0:65, 0:512],
                                           Vt[:, kc, kvh, :],
                                           Pq[:, ui % 3, 512 * bi:512 * (bi + 1)],
                                           start=fk, stop=lk, skip_group_check=True)
                    mm.then_inc(s_o, 1)

                def emit_xnT(pj, j=j):
                    tensor.wait_ge(s_oev, 4 * j + 2)
                    for tt in range(2):
                        it = 2 * pj + tt
                        tensor.wait_ge(s_xnb, it + 1)
                        if tt == 1:
                            tensor.wait_ge(s_xnte, 4 * pj + 2)
                        for ec in range(ECH):
                            hf = ec // 4
                            mm = tensor.matmul(pTR[hf][:, (ec % 4) * 128:(ec % 4 + 1) * 128],
                                               xn_b[:, tt, ec * 128:(ec + 1) * 128],
                                               identB[:], start=True, stop=True,
                                               skip_group_check=True)
                        mm.then_inc(s_xnt, 1)

                emit_S(ujs[0])
                emit_S(ujs[1])
                for idx, ui in enumerate(ujs):
                    if idx + 2 < len(ujs):
                        emit_S(ujs[idx + 2])
                    if j >= 1 and idx == 2 * kcj(j):
                        emit_xnT(j - 1)
                    emit_O(ui)
                for tt in range(2):
                    if tt == 0:
                        tensor.wait_ge(s_norm, 16 * j + 12)
                        tensor.wait_ge(s_exp, U_BEFORE[j + 1])
                    for c in range(12):
                        for eh in range(2):
                            tensor.matmul(sq[tt][:, eh * 512:(eh + 1) * 512],
                                          OT[:, c, tt * 128:(tt + 1) * 128],
                                          woutB[:, c, eh * 512:(eh + 1) * 512],
                                          start=(c == 0), stop=False, skip_group_check=True)
                    if tt == 0:
                        tensor.wait_ge(s_norm, 16 * (j + 1))
                    for c in range(12, 16):
                        for eh in range(2):
                            tensor.matmul(sq[tt][:, eh * 512:(eh + 1) * 512],
                                          OT[:, c, tt * 128:(tt + 1) * 128],
                                          woutB[:, c, eh * 512:(eh + 1) * 512],
                                          start=False, stop=False, skip_group_check=True)
                    for eh in range(2):
                        mm = tensor.matmul(sq[tt][:, eh * 512:(eh + 1) * 512],
                                           onesB[:], boutB[:, eh * 512:(eh + 1) * 512],
                                           start=False, stop=True, skip_group_check=True)
                    mm.then_inc(s_epimm, 1)
            tensor.wait_ge(s_oev, 32)
            for tt in range(2):
                it = 14 + tt
                tensor.wait_ge(s_xnb, it + 1)
                if tt == 1:
                    tensor.wait_ge(s_xnte, 30)
                for ec in range(ECH):
                    hf = ec // 4
                    mm = tensor.matmul(pTR[hf][:, (ec % 4) * 128:(ec % 4 + 1) * 128],
                                       xn_b[:, tt, ec * 128:(ec + 1) * 128],
                                       identB[:], start=True, stop=True, skip_group_check=True)
                mm.then_inc(s_xnt, 1)
            for hc in range(HCH):
                tensor.wait_ge(s_dwc, hc + 1)
                if hc == 0:
                    tensor.wait_ge(s_xnte, 4 * SLOTS)
                for grp in range(2):
                    w = 2 * hc + grp - 1
                    if w > 0:
                        tensor.wait_ge(s_dwev, w)
                    for q in range(4):
                        dc = grp * 4 + q
                        mm = tensor.matmul(pTR[grp][:, q * 128:(q + 1) * 128],
                                           wdb_s[:, hc % 2, dc, :], identB[:],
                                           start=True, stop=True)
                    mm.then_inc(s_dwt, 1)
            for bb in range(2):
                for hc in range(HCH):
                    k = bb * HCH + hc
                    tensor.wait_ge(s_wcast, 2 * (k + 1))
                    for grp in range(4):
                        if k == 0 and grp == 0:
                            tensor.wait_ge(s_dwev, 2 * HCH)
                        w = 4 * k + grp - 1
                        if w > 0:
                            tensor.wait_ge(s_wev, w)
                        src = (wgb_s if grp < 2 else wub_s)
                        hf = grp % 2
                        for q in range(4):
                            ec = hf * 4 + q
                            mm = tensor.matmul(pTR[grp % 2][:, q * 128:(q + 1) * 128],
                                               src[:, hc % 2, ec * 128:(ec + 1) * 128],
                                               identB[:], start=True, stop=True)
                        mm.then_inc(s_wtr, 1)
                    for tt4 in range(2):
                        f = hc * 2 + tt4
                        fb = bb * 48 + f
                        if tt4 == 0:
                            tensor.wait_ge(s_wev, 4 * (k + 1))
                        if fb >= 3:
                            tensor.wait_ge(s_sil, fb - 2)
                        if bb == 1 and f == 0:
                            tensor.wait_ge(s_fsum, 8)
                        if fb == 0:
                            tensor.wait_ge(s_xnte, 4 * SLOTS)
                        for ec in range(ECH):
                            tensor.matmul(pA[f % 3][:, 0:512], wgT[:, hc % 2, ec, :],
                                          xnT[:, ec, bb * 1024 + tt4 * 512:bb * 1024 + (tt4 + 1) * 512],
                                          start=(ec == 0), stop=(ec == ECH - 1))
                        for ec in range(ECH):
                            mm = tensor.matmul(pA[f % 3][:, 512:1024], wuT[:, hc % 2, ec, :],
                                               xnT[:, ec, bb * 1024 + tt4 * 512:bb * 1024 + (tt4 + 1) * 512],
                                               start=(ec == 0), stop=(ec == ECH - 1))
                        mm.then_inc(s_gu, 1)
                for ti in range(8):
                    gi = bb * 8 + ti
                    tensor.wait_ge(s_sil, 48 * (bb + 1))
                    if gi == 0:
                        tensor.wait_ge(s_dwev, 2 * HCH)
                    if gi >= 3:
                        tensor.wait_ge(s_fsum, gi - 2)
                    for eh in range(2):
                        for hc in range(HCH):
                            mm = tensor.matmul(pA[ti % 3][:, eh * 512:(eh + 1) * 512],
                                               hactT[:, hc, ti * 128:(ti + 1) * 128],
                                               wdT[:, hc, eh * 512:(eh + 1) * 512],
                                               start=(hc == 0), stop=(hc == HCH - 1),
                                               skip_group_check=True)
                    mm.then_inc(s_dmm, 1)

    return nc


def _rope_tables(positions):
    d = np.arange(128)
    i = (d % 64) // 2
    inv = 1.0 / (10000.0 ** (2.0 * i / 64.0))
    ang = np.outer(inv, positions.astype(np.float64))
    c = np.cos(ang).astype(np.float32)
    s = np.sin(ang).astype(np.float32)
    sign = np.where(d % 2 == 0, -1.0, 1.0).astype(np.float32)[:, None]
    return np.ascontiguousarray(c), np.ascontiguousarray(s * sign)


def kernel(L0_attn_x, L0_attn_Wqa, L0_attn_Wqb, L0_attn_Wkva, L0_attn_Wkvb,
           L0_attn_Wout, L0_attn_bout, L0_ffn_w_gate, L0_ffn_w_up, L0_ffn_w_down):
    x = np.ascontiguousarray(np.asarray(L0_attn_x, dtype=np.float32))
    Wqa = np.asarray(L0_attn_Wqa, dtype=np.float32)
    Wqb = np.asarray(L0_attn_Wqb, dtype=np.float32)
    Wkva = np.asarray(L0_attn_Wkva, dtype=np.float32)
    Wkvb = np.asarray(L0_attn_Wkvb, dtype=np.float32)
    Wout = np.asarray(L0_attn_Wout, dtype=np.float32)
    bout = np.ascontiguousarray(np.asarray(L0_attn_bout, dtype=np.float32).reshape(1, E))
    wg = np.ascontiguousarray(np.asarray(L0_ffn_w_gate, dtype=np.int32))
    wu = np.ascontiguousarray(np.asarray(L0_ffn_w_up, dtype=np.int32))
    wd = np.ascontiguousarray(np.asarray(L0_ffn_w_down, dtype=np.int32))

    wkvb_r = Wkvb.reshape(KVL, KVH, D, 2)
    wk = np.ascontiguousarray(wkvb_r[..., 0].reshape(KVL, 256))
    wv = np.ascontiguousarray(wkvb_r[..., 1].reshape(KVL, 256))
    wks = np.ascontiguousarray(
        wkvb_r[..., 0].reshape(KVL, KVH, 32, 2)[:, :, :, ::-1].reshape(KVL, 256))
    qperm = []
    for g in range(2):
        for i in range(4):
            ha, hb = 8 * g + i, 8 * g + 4 + i
            qperm.extend(range(64 * ha, 64 * ha + 64))
            qperm.extend(range(64 * hb, 64 * hb + 64))
    qperm = np.array(qperm)
    wqb_p = np.ascontiguousarray(Wqb[:, qperm])
    wqbs_full = Wqb.reshape(QL, H, 32, 2)[:, :, :, ::-1].reshape(QL, E)
    wqbs_p = np.ascontiguousarray(wqbs_full[:, qperm])
    operm = []
    for g in range(2):
        for hh in range(2):
            for h in (8 * g + 2 * hh, 8 * g + 2 * hh + 1, 8 * g + 4 + 2 * hh, 8 * g + 4 + 2 * hh + 1):
                operm.extend(range(64 * h, 64 * h + 64))
    operm = np.array(operm)
    wout_p = np.ascontiguousarray(Wout[operm, :])
    wqakva_ = np.ascontiguousarray(np.concatenate([Wqa, Wkva], axis=1))
    identity = np.eye(128, dtype=BF)
    ckv, skv = _rope_tables(np.arange(S))

    nc = build_nc()
    in_maps = []
    chunk_of = np.zeros((NCORE, SLOTS), dtype=np.int64)
    for c in range(NCORE):
        b, cm = c // 4, c % 4
        chunks = [4 * j + cm for j in range(SLOTS)]
        chunk_of[c] = chunks
        xb_ = x[b]
        x_own_ = np.ascontiguousarray(
            np.concatenate([xb_[QC * p:QC * (p + 1)] for p in chunks], axis=0))
        pos_own = np.concatenate([np.arange(QC * p, QC * (p + 1)) for p in chunks])
        cq, sq_ = _rope_tables(pos_own)
        cq = cq.astype(BF); sq_ = sq_.astype(BF)
        m = np.zeros((128, SLOTS * 8 * QC), dtype=BF)
        iv = np.arange(128)[:, None]
        qv = np.arange(QC)[None, :]
        for j in range(SLOTS):
            p = chunks[j]
            for ki in range(8):
                v = 1024 * j + 128 * ki + iv
                m[:, (j * 8 + ki) * QC:(j * 8 + ki + 1) * QC] = (v <= QC * p + qv).astype(BF)
        in_maps.append({
            "x_own": x_own_, "x_full": np.ascontiguousarray(xb_),
            "ident": identity, "masks": m,
            "ctab_kv": ckv, "stab_kv": skv, "ctab_q": cq, "stab_q": sq_,
            "wqakva": wqakva_, "wqb": wqb_p, "wqbs": wqbs_p,
            "wk": wk, "wks": wks, "wv": wv,
            "wout": wout_p, "bout": bout,
            "wg": wg, "wu": wu, "wd": wd,
        })

    res = run_bass_kernel_spmd(nc, in_maps, core_ids=list(range(NCORE)))

    out = np.empty((B, S, E), dtype=np.float32)
    for c in range(NCORE):
        b = c // 4
        yc = np.asarray(res.results[c]["y"])
        for j in range(SLOTS):
            p = chunk_of[c][j]
            out[b, QC * p:QC * (p + 1)] = yc[QC * j:QC * (j + 1)]
    return out
